# version 04
# baseline (speedup 1.0000x reference)
import numpy as np
from contextlib import ExitStack
import concourse.bass as bass
import concourse.mybir as mybir
from concourse.bass_utils import run_bass_kernel_spmd
import ml_dtypes

F32 = mybir.dt.float32
BF16 = mybir.dt.bfloat16
AF = mybir.ActivationFunctionType
ALU = mybir.AluOpType
AX = mybir.AxisListType
BF = ml_dtypes.bfloat16

C = 1024
B = 4
T = 4096
NT = 2048
NCH = 8
NEG = -1e33
EXPB = -20.0
SCALE = 0.125


class Prog:

    ENGINES = ("sync", "gpsimd", "vector", "scalar", "tensor")

    def __init__(self):
        self.nc = bass.Bass("TRN2", target_bir_lowering=False, debug=False)
        self.es = ExitStack()
        self.alloc_scope = self.es
        self.q = {e: [] for e in self.ENGINES}
        self.cnt = {}
        self.sems = {}

    def sem(self, name):
        if name not in self.sems:
            self.sems[name] = self.es.enter_context(self.nc.semaphore(name))
            self.cnt[name] = 0
        return self.sems[name]

    def inc(self, name, k=1):
        self.sem(name)
        self.cnt[name] += k
        return self.sems[name], self.cnt[name]

    def emit(self, eng, fn):
        self.q[eng].append(fn)

    def wait(self, eng, sem_name, v):
        s = self.sem(sem_name)
        self.emit(eng, lambda e, s=s, v=v: e.wait_ge(s, v))

    CHAINED = {"vector": "vch", "scalar": "sch", "gpsimd": "gch"}

    def op(self, eng, meth, *args, sem=None, inc=1, **kw):
        ch = self.CHAINED.get(eng)
        if ch is not None and meth not in ("dma_start", "collective_compute"):
            assert sem is None or sem == ch, (eng, meth, sem)
            s = self.sem(ch)
            prev = self.cnt[ch]
            self.cnt[ch] += 1

            def fn(e, a=args, k=kw, m=meth, s=s, prev=prev):
                if prev > 0:
                    e.wait_ge(s, prev)
                getattr(e, m)(*a, **k).then_inc(s, 1)
            self.emit(eng, fn)
            return self.cnt[ch]
        if sem is not None:
            s, v = self.inc(sem, inc)
            self.emit(eng, lambda e, a=args, k=kw, s=s, m=meth, i=inc:
                      getattr(e, m)(*a, **k).then_inc(s, i))
            return v
        self.emit(eng, lambda e, a=args, k=kw, m=meth: getattr(e, m)(*a, **k))
        return None

    def dma(self, eng, out, in_, sem):
        return self.op(eng, "dma_start", sem=sem, inc=16, out=out, in_=in_)

    def sb(self, name, shape, dt):
        return self.alloc_scope.enter_context(self.nc.sbuf_tensor(name, shape, dt))

    def sb_perm(self, name, shape, dt):
        return self.es.enter_context(self.nc.sbuf_tensor(name, shape, dt))

    def ps(self, name, shape, dt=F32):
        return self.es.enter_context(self.nc.psum_tensor(name, shape, dt))

    def inp(self, name, shape, dt):
        return self.nc.dram_tensor(name, shape, dt, kind="ExternalInput").ap()

    def outp(self, name, shape, dt):
        return self.nc.dram_tensor(name, shape, dt, kind="ExternalOutput").ap()

    def finalize(self):
        qq = self.q
        with self.nc.Block() as block:
            @block.sync
            def _(e):
                for f in qq["sync"]:
                    f(e)

            @block.gpsimd
            def _(e):
                for f in qq["gpsimd"]:
                    f(e)

            @block.vector
            def _(e):
                for f in qq["vector"]:
                    f(e)

            @block.scalar
            def _(e):
                for f in qq["scalar"]:
                    f(e)

            @block.tensor
            def _(e):
                for f in qq["tensor"]:
                    f(e)
        self.es.close()
        return self.nc


def emit_prelude(p):
    p.x_s = p.sb("x_s", [128, NCH * NT], F32)
    p.u_s = p.sb("u_s", [128, NCH * NT], BF16)
    p.cst_s = p.sb("cst_s", [128, 8], F32)
    p.ones_s = p.sb("ones_s", [128, 128], BF16)
    p.onesf_s = p.sb("onesf_s", [1, 128], F32)
    p.identb_s = p.sb("identb_s", [128, 128], BF16)
    p.identf_s = p.sb("identf_s", [128, 128], F32)
    p.mask_s = p.sb("mask_s", [128, 8 * 128], BF16)
    cst = p.inp("cst", [128, 8], F32)
    ones = p.inp("onesb", [128, 128], BF16)
    onesf = p.inp("onesf", [1, 128], F32)
    identb = p.inp("identb", [128, 128], BF16)
    identf = p.inp("identf", [128, 128], F32)
    p.dma("sync", p.cst_s[:, :], cst, "ldc")
    p.dma("sync", p.ones_s[:, :], ones, "ldc")
    p.dma("sync", p.identb_s[:, :], identb, "ldc")
    p.dma("sync", p.identf_s[:, :], identf, "ldc")
    p.ldc_v = p.dma("sync", p.onesf_s[:, :], onesf, "ldc")


def host_consts():
    cst = np.zeros((128, 8), np.float32)
    cst[:, 0] = EXPB
    cst[:, 1] = 1e-5
    cst[:, 2] = 0.5
    cst[:, 3] = 1.0
    return {"cst": cst,
            "onesb": np.ones((128, 128), BF),
            "onesf": np.ones((1, 128), np.float32),
            "identb": np.eye(128, dtype=BF),
            "identf": np.eye(128, dtype=np.float32)}


def emit_load_x(p):
    x_in = p.inp("x", [128, NCH * NT], F32)
    v = p.dma("sync", p.x_s[:, :], x_in, "ldx")
    return ("ldx", v)


def emit_store_x(p, deps):
    x_out = p.outp("y", [128, NCH * NT], F32)
    for s, v in deps:
        p.wait("sync", s, v)
    v = p.dma("sync", x_out, p.x_s[:, :], "stx")
    p.wait("sync", "stx", v)


def emit_ln(p, tag, deps):
    nc = p.nc
    sv, ss, sm = "vch", "sch", "mm"
    with ExitStack() as st:
        xbf = st.enter_context(nc.sbuf_tensor(tag + "xbf", [128, NCH * NT], BF16))
        xsq = st.enter_context(nc.sbuf_tensor(tag + "xsq", [128, NCH * NT], BF16))
        rows = st.enter_context(nc.sbuf_tensor(tag + "rw", [1, 1024], F32))
        pstA = st.enter_context(nc.psum_tensor(tag + "pA", [1, 512], F32))
        pstB = st.enter_context(nc.psum_tensor(tag + "pB", [1, 512], F32))
        pbcM = st.enter_context(nc.psum_tensor(tag + "pM", [128, 512], F32))
        pbcR = st.enter_context(nc.psum_tensor(tag + "pR", [128, 512], F32))

        for s, v in deps:
            p.wait("vector", s, v)
            p.wait("scalar", s, v)
            p.wait("tensor", s, v)
        p.wait("vector", "ldc", p.ldc_v)
        p.wait("scalar", "ldc", p.ldc_v)
        p.wait("tensor", "ldc", p.ldc_v)

        for cc in range(NCH):
            xc = p.x_s[:, cc * NT:(cc + 1) * NT]
            p.op("vector", "tensor_copy", xbf[:, cc * NT:(cc + 1) * NT], xc,
                 sem=sv)
            p.op("scalar", "activation", xsq[:, cc * NT:(cc + 1) * NT], xc,
                 AF.Square, sem=ss)
        vcop, scop = p.cnt[sv], p.cnt[ss]
        p.wait("tensor", sv, vcop)
        p.wait("tensor", ss, scop)

        for tc in range(4):
            for cc in range(NCH):
                off = cc * NT + tc * 512
                p.op("tensor", "matmul", pstA[:, :], p.ones_s[:, 0:1],
                     xbf[:, off:off + 512], start=(cc == 0),
                     stop=(cc == NCH - 1), sem=sm)
                p.op("tensor", "matmul", pstB[:, :], p.ones_s[:, 0:1],
                     xsq[:, off:off + 512], start=(cc == 0),
                     stop=(cc == NCH - 1), sem=sm)
            mmv = p.cnt[sm]
            p.wait("vector", sm, mmv)
            p.op("vector", "tensor_scalar", rows[0:1, 0:512], pstA[0:1, :],
                 1.0 / C, None, op0=ALU.mult, sem=sv)
            p.op("vector", "tensor_tensor", rows[0:1, 512:1024],
                 rows[0:1, 0:512], rows[0:1, 0:512], op=ALU.mult, sem=sv)
            p.op("vector", "scalar_tensor_tensor", rows[0:1, 512:1024],
                 pstB[0:1, :], 1.0 / C, rows[0:1, 512:1024],
                 op0=ALU.mult, op1=ALU.subtract, sem=sv)
            vv = p.cnt[sv]
            p.wait("scalar", sv, vv)
            p.op("scalar", "activation", rows[0:1, 512:1024],
                 rows[0:1, 512:1024], AF.Ln, bias=p.cst_s[0:1, 1:2], sem=ss)
            p.op("scalar", "activation", rows[0:1, 512:1024],
                 rows[0:1, 512:1024], AF.Exp, scale=-0.5, sem=ss)
            sv2 = p.cnt[ss]
            p.wait("tensor", ss, sv2)
            p.op("tensor", "matmul", pbcM[:, :], p.onesf_s[0:1, :],
                 rows[0:1, 0:512], start=True, stop=True, sem=sm)
            p.op("tensor", "matmul", pbcR[:, :], p.onesf_s[0:1, :],
                 rows[0:1, 512:1024], start=True, stop=True, sem=sm)
            mmv2 = p.cnt[sm]
            p.wait("vector", sm, mmv2)
            for cc in range(NCH):
                off = cc * NT + tc * 512
                xc = p.x_s[:, off:off + 512]
                p.op("vector", "tensor_tensor", xbf[:, off:off + 512], xc,
                     pbcM[:, :], op=ALU.subtract)
                p.op("vector", "tensor_tensor", p.u_s[:, off:off + 512],
                     xbf[:, off:off + 512], pbcR[:, :], op=ALU.mult, sem=sv)
            uvv = p.cnt[sv]
            if tc < 3:
                p.wait("tensor", sv, uvv)
    return (sv, p.cnt[sv])




def emit_ffn0_weff(p):
    nc = p.nc
    tag = "wf"
    sv, ss, sm = "vch", "sch", "mm"
    p.weff_s = p.sb("weff_s", [128, NCH * 1024], BF16)
    p.bffn_s = p.sb("bffn_s", [128, 8], F32)
    with ExitStack() as st:
        wt = st.enter_context(nc.sbuf_tensor(tag + "wt", [128, 1024], F32))
        ldt = st.enter_context(nc.sbuf_tensor(tag + "ldt", [100, 4], F32))
        dax = st.enter_context(nc.sbuf_tensor(tag + "dax", [100, 1024], F32))
        lut = st.enter_context(nc.sbuf_tensor(tag + "lut", [4, 50], F32))
        uat = st.enter_context(nc.sbuf_tensor(tag + "uat", [50, 1024], F32))
        aA = st.enter_context(nc.sbuf_tensor(tag + "A", [4, 1024], F32))
        bB = st.enter_context(nc.sbuf_tensor(tag + "B", [50, 1024], F32))
        bcol = st.enter_context(nc.sbuf_tensor(tag + "bc", [128, 8], F32))
        psA = st.enter_context(nc.psum_tensor(tag + "psA", [4, 1024], F32))
        psB = st.enter_context(nc.psum_tensor(tag + "psB", [50, 1024], F32))
        psC = st.enter_context(nc.psum_tensor(tag + "psC", [128, 1024], F32))

        wt_in = p.inp("ffnWT", [1024, 1024], F32)
        b_in = p.inp("ffnb", [128, 8], F32)
        ldt_in = p.inp("ffnldT", [100, 4], F32)
        dax_in = p.inp("ffndax", [100, 1024], F32)
        lut_in = p.inp("ffnluT", [4, 50], F32)
        uat_in = p.inp("ffnuaT", [50, 1024], F32)

        p.dma("sync", ldt[:, :], ldt_in, "ldw")
        p.dma("sync", dax[:, :], dax_in, "ldw")
        p.dma("sync", lut[:, :], lut_in, "ldw")
        p.dma("sync", bcol[:, :], b_in, "ldw")
        v0 = p.dma("sync", uat[:, :], uat_in, "ldw")
        p.wait("tensor", "ldw", v0)
        for half in range(2):
            p.op("tensor", "matmul", psA[:, half * 512:(half + 1) * 512],
                 ldt[:, :], dax[:, half * 512:(half + 1) * 512],
                 start=True, stop=True, sem=sm)
        p.wait("vector", sm, p.cnt[sm])
        p.op("vector", "tensor_copy", aA[:, :], psA[:, :], sem=sv)
        p.op("vector", "tensor_scalar", p.bffn_s[:, :], bcol[:, :], 0.5, None,
             op0=ALU.mult, sem=sv)
        p.wait("tensor", sv, p.cnt[sv])
        for half in range(2):
            p.op("tensor", "matmul", psB[:, half * 512:(half + 1) * 512],
                 lut[:, :], aA[:, half * 512:(half + 1) * 512],
                 start=True, stop=True, sem=sm)
        p.wait("vector", sm, p.cnt[sm])
        p.op("vector", "tensor_copy", bB[:, :], psB[:, :], sem=sv)
        bb_v = p.cnt[sv]
        p.wait("tensor", sv, bb_v)
        for cc in range(NCH):
            wv = p.dma("sync", wt[:, :],
                       wt_in[cc * 128:(cc + 1) * 128, :], "ldw")
            for half in range(2):
                p.op("tensor", "matmul", psC[:, half * 512:(half + 1) * 512],
                     bB[:, cc * 128:(cc + 1) * 128],
                     uat[:, half * 512:(half + 1) * 512],
                     start=True, stop=True, sem=sm)
            mmv = p.cnt[sm]
            p.wait("vector", sm, mmv)
            p.wait("vector", "ldw", wv)
            p.op("vector", "tensor_tensor", p.weff_s[:, cc * 1024:(cc + 1) * 1024],
                 psC[:, :], wt[:, :], op=ALU.add, sem=sv)
            if cc < NCH - 1:
                p.wait("tensor", sv, p.cnt[sv])
                p.wait("sync", sv, p.cnt[sv])
    return (sv, p.cnt[sv])


def emit_ffn0_apply(p, tag, u_dep, weff_dep):
    sv, ss, sm = "vch", "sch", "mm"
    nc = p.nc
    with ExitStack() as st:
        tmp = st.enter_context(nc.sbuf_tensor(tag + "t", [128, 4 * 512], F32))
        pso = [st.enter_context(nc.psum_tensor(tag + "p%d" % i, [128, 512], F32))
               for i in range(4)]
        p.wait("tensor", u_dep[0], u_dep[1])
        p.wait("tensor", weff_dep[0], weff_dep[1])
        p.wait("scalar", weff_dep[0], weff_dep[1])
        slot_act = [None] * 4
        slot_dve = [None] * 4
        nslot = 0
        for oc in range(NCH):
            for tc in range(4):
                slot = nslot % 4
                nslot += 1
                psu = pso[slot]
                if slot_act[slot] is not None:
                    p.wait("tensor", ss, slot_act[slot])
                for cc in range(NCH):
                    p.op("tensor", "matmul", psu[:, :],
                         p.weff_s[:, cc * 1024 + oc * 128: cc * 1024 + (oc + 1) * 128],
                         p.u_s[:, cc * NT + tc * 512: cc * NT + (tc + 1) * 512],
                         start=(cc == 0), stop=(cc == NCH - 1),
                         sem=(sm if cc == NCH - 1 else None))
                mmv = p.cnt[sm]
                p.wait("scalar", sm, mmv)
                if slot_dve[slot] is not None:
                    p.wait("scalar", sv, slot_dve[slot])
                tsl = tmp[:, slot * 512:(slot + 1) * 512]
                p.op("scalar", "activation", tsl, psu[:, :], AF.Identity,
                     bias=p.bffn_s[:, oc:oc + 1], scale=0.5, sem=ss)
                slot_act[slot] = p.cnt[ss]
                p.wait("vector", ss, slot_act[slot])
                xsl = p.x_s[:, oc * NT + tc * 512: oc * NT + (tc + 1) * 512]
                p.op("vector", "tensor_tensor", xsl, xsl, tsl, op=ALU.add,
                     sem=sv)
                slot_dve[slot] = p.cnt[sv]
    return (sv, p.cnt[sv])


def shard_x(x):
    xs = []
    xr = np.ascontiguousarray(x).reshape(B, 32, 128, C)
    for c in range(8):
        v = xr[:, c::8]
        v = v.transpose(3, 0, 1, 2).reshape(C, NT)
        v = v.reshape(NCH, 128, NT).transpose(1, 0, 2).reshape(128, NCH * NT)
        xs.append(np.ascontiguousarray(v))
    return xs


def unshard_x(ys):
    out = np.empty((B, 32, 128, C), np.float32)
    for c in range(8):
        v = ys[c].reshape(128, NCH, NT).transpose(1, 0, 2).reshape(C, NT)
        v = v.reshape(C, B, 4, 128).transpose(1, 2, 3, 0)
        out[:, c::8] = v
    return out.reshape(B, T, C)


def ffn0_host_inputs(W, b, lora_down, lora_up, down_aux, up_aux):
    return {
        "ffnWT": np.ascontiguousarray(W.T),
        "ffnb": np.ascontiguousarray(np.asarray(b, np.float32)
                                     .reshape(NCH, 128).T),
        "ffnldT": np.ascontiguousarray(lora_down.T),
        "ffndax": np.ascontiguousarray(down_aux),
        "ffnluT": np.ascontiguousarray(lora_up.T),
        "ffnuaT": np.ascontiguousarray(up_aux.T),
    }


def moe_host_inputs(inputs):
    return {
        "centT": np.ascontiguousarray(inputs["L1_ffn_centroids"].T),
        "ebias": np.ascontiguousarray(
            np.asarray(inputs["L1_ffn_expert_biases"], np.float32)[None, :]),
        "brT": np.ascontiguousarray(inputs["L1_ffn_br"]),
        "bs2": np.ascontiguousarray(inputs["L1_ffn_bs"]),
        "half2": np.full((2, 1), 0.5, np.float32),
        "wsT": np.ascontiguousarray(inputs["L1_ffn_Ws"].transpose(0, 2, 1)),
        "wrT": np.ascontiguousarray(inputs["L1_ffn_Wr"].transpose(0, 2, 1)),
    }


def build_ffn0_launch():
    p = Prog()
    emit_prelude(p)
    xd = emit_load_x(p)
    wd = emit_ffn0_weff(p)
    ud = emit_ln(p, "ln", [xd, wd])
    done = emit_ffn0_apply(p, "fa", ud, wd)
    emit_store_x(p, [done])
    return p.finalize()


def build_moe_launch():
    p = Prog()
    emit_prelude(p)
    xd = emit_load_x(p)
    wrt_in = p.inp("wrT", [16, 1024, 1024], F32)
    pd = emit_moe_prep(p)
    ud = emit_ln(p, "ln", [xd, pd])
    done = emit_moe_apply(p, "ma", ud, pd, wrt_in)
    emit_store_x(p, [done])
    return p.finalize()


def run_launch(nc, in_maps):
    res = run_bass_kernel_spmd(nc, in_maps, core_ids=list(range(8)))
    return [r["y"] for r in res.results]


_NC_CACHE = {}


def _launch(kind, xs, inputs):
    consts = host_consts()
    if kind == "ffn0":
        if kind not in _NC_CACHE:
            _NC_CACHE[kind] = build_ffn0_launch()
        w = ffn0_host_inputs(
            inputs["L0_ffn_W"], inputs["L0_ffn_b"],
            inputs["L0_ffn_lora_down"], inputs["L0_ffn_lora_up"],
            inputs["L0_ffn_down_aux"], inputs["L0_ffn_up_aux"])
    elif kind == "moe":
        if kind not in _NC_CACHE:
            _NC_CACHE[kind] = build_moe_launch()
        w = moe_host_inputs(inputs)
    elif kind in ("attn0", "attn1"):
        L = int(kind[-1])
        if kind not in _NC_CACHE:
            _NC_CACHE[kind] = build_attn_launch(L)
        in_maps = [dict(consts, x=xs[c], **attn_host_inputs(L, c, inputs))
                   for c in range(8)]
        ys = run_launch(_NC_CACHE[kind], in_maps)
        for c in range(8):
            xs[c] = ys[c]
        return
    else:
        raise KeyError(kind)
    in_maps = [dict(consts, x=xs[c], **w) for c in range(8)]
    ys = run_launch(_NC_CACHE[kind], in_maps)
    for c in range(8):
        xs[c] = ys[c]


def build_fused(n_subs=6):
    p = Prog()
    emit_prelude(p)
    xd = emit_load_x(p)
    wrt_in = p.inp("wrT", [16, 1024, 1024], F32)
    d = xd
    with ExitStack() as s_l0:
        if n_subs >= 1:
            p.alloc_scope = s_l0
            wd = emit_ffn0_weff(p)
            a0p = emit_attn_prep(p, 0, deps=[wd]) if n_subs >= 2 else None
            p.alloc_scope = p.es
            u1 = emit_ln(p, "ln1", [xd, wd] + ([a0p] if a0p else []))
            d = emit_ffn0_apply(p, "f1", u1, wd)
        if n_subs >= 2:
            u2 = emit_ln(p, "ln2", [d])
            d = emit_attn(p, "a0", 0, u2, a0p)
        if n_subs >= 3:
            u3 = emit_ln(p, "ln3", [d])
            d = emit_ffn0_apply(p, "f3", u3, wd)
    with ExitStack() as s_l1:
        if n_subs >= 4:
            p.alloc_scope = s_l1
            mp = emit_moe_prep(p, deps=[d])
            a1p = emit_attn_prep(p, 1, deps=[d, mp]) if n_subs >= 5 else None
            p.alloc_scope = p.es
            u4 = emit_ln(p, "ln4", [d, mp] + ([a1p] if a1p else []))
            d = emit_moe_apply(p, "m4", u4, mp, wrt_in)
        if n_subs >= 5:
            u5 = emit_ln(p, "ln5", [d])
            d = emit_attn(p, "a1", 1, u5, a1p)
        if n_subs >= 6:
            u6 = emit_ln(p, "ln6", [d])
            d = emit_moe_apply(p, "m6", u6, mp, wrt_in)
    emit_store_x(p, [d])
    return p.finalize()


def fused_in_maps(inputs):
    consts = host_consts()
    xs = shard_x(inputs["L0_attn_x"])
    ffn_w = ffn0_host_inputs(
        inputs["L0_ffn_W"], inputs["L0_ffn_b"],
        inputs["L0_ffn_lora_down"], inputs["L0_ffn_lora_up"],
        inputs["L0_ffn_down_aux"], inputs["L0_ffn_up_aux"])
    moe_w = moe_host_inputs(inputs)
    maps = []
    for c in range(8):
        m = dict(consts, x=xs[c], **ffn_w)
        m.update(moe_w)
        m.update(attn_host_inputs(0, c, inputs))
        m.update(attn_host_inputs(1, c, inputs))
        maps.append(m)
    return maps


FUSED = True
CHAIN = ["ffn0", "attn0", "ffn0", "moe", "attn1", "moe"]


def kernel(**inputs):
    inputs = {k: np.asarray(v, np.float32) for k, v in inputs.items()}
    import os
    chain = os.environ.get("MK_CHAIN", "")
    if FUSED and not chain:
        if "fused" not in _NC_CACHE:
            _NC_CACHE["fused"] = build_fused()
        ys = run_launch(_NC_CACHE["fused"], fused_in_maps(inputs))
        return unshard_x(ys)
    xs = shard_x(inputs["L0_attn_x"])
    for kind in (chain.split(",") if chain else CHAIN):
        _launch(kind, xs, inputs)
    return unshard_x(xs)


def emit_moe_prep(p, deps=()):
    nc = p.nc
    tag = "mp"
    sv, ss, sm = "vch", "sch", "mm"
    for s, v in deps:
        for eng in ("sync", "gpsimd", "tensor", "vector", "scalar"):
            p.wait(eng, s, v)
    p.centT_s = p.sb("centT_s", [128, NCH * 16], BF16)
    p.eb_s = p.sb("eb_s", [128, 16], F32)
    p.br_s = p.sb("br_s", [16, 1024], BF16)
    p.bsrow_s = p.sb("bsrow_s", [1, 1024], BF16)
    p.wseff_s = p.sb("wseff_s", [128, NCH * 1024], BF16)

    centT = p.inp("centT", [1024, 16], F32)
    ebias = p.inp("ebias", [1, 16], F32)
    br_in = p.inp("brT", [16, 1024], F32)
    bs_in = p.inp("bs2", [2, 1024], F32)
    half2 = p.inp("half2", [2, 1], F32)
    wst_in = p.inp("wsT", [2, 1024, 1024], F32)
    p.dma("gpsimd", p.centT_s[:, :].rearrange("p (c n) -> p c n", n=16),
          centT.rearrange("(c p) n -> p c n", p=128), "mpw")
    p.dma("gpsimd", p.br_s[:, :], br_in, "mpw")
    with ExitStack() as st:
        ebrow = st.enter_context(nc.sbuf_tensor(tag + "eb", [1, 16], F32))
        bss = st.enter_context(nc.sbuf_tensor(tag + "bs", [2, 1024], F32))
        h2 = st.enter_context(nc.sbuf_tensor(tag + "h2", [2, 1], F32))
        wsa = st.enter_context(nc.sbuf_tensor(tag + "wsa", [128, 1024], F32))
        wsb = st.enter_context(nc.sbuf_tensor(tag + "wsb", [128, 1024], F32))
        pse = st.enter_context(nc.psum_tensor(tag + "pe", [128, 16], F32))
        psb = st.enter_context(nc.psum_tensor(tag + "pb", [1, 1024], F32))
        v0 = p.dma("sync", ebrow[:, :], ebias, "ldw")
        p.dma("sync", bss[:, :], bs_in, "ldw")
        v1 = p.dma("sync", h2[:, :], half2, "ldw")
        p.wait("tensor", "ldw", v1)
        p.wait("tensor", "ldc", p.ldc_v)
        p.op("tensor", "matmul", pse[:, :], p.onesf_s[0:1, :], ebrow[0:1, :],
             start=True, stop=True, sem=sm)
        for half in range(2):
            p.op("tensor", "matmul", psb[:, half * 512:(half + 1) * 512],
                 h2[:, :], bss[:, half * 512:(half + 1) * 512],
                 start=True, stop=True, sem=sm)
        mmv = p.cnt[sm]
        p.wait("scalar", sm, mmv)
        p.op("scalar", "copy", p.eb_s[:, :], pse[:, :], sem=ss)
        p.op("scalar", "copy", p.bsrow_s[:, :], psb[0:1, :], sem=ss)
        for cc in range(NCH):
            va = p.dma("sync", wsa[:, :], wst_in[0, cc * 128:(cc + 1) * 128, :],
                       "mpwa")
            vb = p.dma("sync", wsb[:, :], wst_in[1, cc * 128:(cc + 1) * 128, :],
                       "mpwa")
            p.wait("vector", "mpwa", vb)
            p.op("vector", "tensor_tensor", wsa[:, :], wsa[:, :], wsb[:, :],
                 op=ALU.add)
            p.op("vector", "tensor_scalar",
                 p.wseff_s[:, cc * 1024:(cc + 1) * 1024], wsa[:, :], 0.5, None,
                 op0=ALU.mult, sem=sv)
            if cc < NCH - 1:
                p.wait("sync", sv, p.cnt[sv])
        p.wait("vector", ss, p.cnt[ss])
        p.wait("vector", "mpw", 32)
    return (sv, p.cnt[sv])


def emit_moe_apply(p, tag, u_dep, prep_dep, wrt_in, dbg=()):
    nc = p.nc
    sv, ss, sm, sg = "vch", "sch", "mm", tag + "g"
    with ExitStack() as st:
        g05 = st.enter_context(nc.sbuf_tensor(tag + "g05", [128, 16 * 16], F32))
        gfm = st.enter_context(nc.sbuf_tensor(tag + "gfm", [16, NT], BF16))
        acc = st.enter_context(nc.sbuf_tensor(tag + "acc", [128, 16 * 1024], F32))
        r16 = st.enter_context(nc.sbuf_tensor(tag + "r16", [128, 16], F32))
        r16b = st.enter_context(nc.sbuf_tensor(tag + "r16b", [128, 16], F32))
        rcol = st.enter_context(nc.sbuf_tensor(tag + "rc", [128, 4], F32))
        gbf = st.enter_context(nc.sbuf_tensor(tag + "gbf", [128, 16], BF16))

        for eng in ("tensor", "vector", "scalar", "gpsimd", "sync"):
            p.wait(eng, u_dep[0], u_dep[1])
            p.wait(eng, prep_dep[0], prep_dep[1])

        with ExitStack() as st2:
            psr = st2.enter_context(nc.psum_tensor(tag + "psr", [128, 16], F32))
            psg = st2.enter_context(nc.psum_tensor(tag + "psg", [16, 128], BF16))
            for m in range(16):
                for cc in range(NCH):
                    p.op("tensor", "matmul", psr[:, :],
                         p.u_s[:, cc * NT + m * 128: cc * NT + (m + 1) * 128],
                         p.centT_s[:, cc * 16:(cc + 1) * 16],
                         start=(cc == 0), stop=(cc == NCH - 1),
                         sem=(sm if cc == NCH - 1 else None))
                mmv = p.cnt[sm]
                p.wait("vector", sm, mmv)
                p.op("vector", "tensor_tensor", r16[:, :], psr[:, :],
                     p.eb_s[:, :], op=ALU.add, sem=sv)
                rr_v = p.cnt[sv]
                p.op("vector", "reduce_max", rcol[:, 0:1], r16[:, :], axis=AX.X,
                     negate=True, sem=sv)
                vv = p.cnt[sv]
                p.wait("scalar", sv, vv)
                p.op("scalar", "activation", r16[:, :], r16[:, :], AF.Exp,
                     bias=rcol[:, 0:1], sem=ss)
                ssv = p.cnt[ss]
                p.wait("vector", ss, ssv)
                p.op("vector", "reduce_sum", rcol[:, 1:2], r16[:, :], axis=AX.X)
                p.op("vector", "reciprocal", rcol[:, 2:3], rcol[:, 1:2])
                p.op("vector", "tensor_scalar", r16[:, :], r16[:, :],
                     rcol[:, 2:3], None, op0=ALU.mult)
                p.op("vector", "reduce_max", rcol[:, 0:1], r16[:, :], axis=AX.X,
                     negate=True)
                p.op("vector", "tensor_scalar", r16b[:, :], r16[:, :],
                     rcol[:, 0:1], None, op0=ALU.add)
                p.op("vector", "tensor_scalar", r16b[:, :], r16b[:, :], 0.0,
                     None, op0=ALU.is_equal)
                p.op("vector", "scalar_tensor_tensor", r16b[:, :], r16b[:, :],
                     NEG, r16[:, :], op0=ALU.mult, op1=ALU.add)
                p.op("vector", "reduce_max", rcol[:, 3:4], r16b[:, :], axis=AX.X,
                     negate=True)
                p.op("vector", "tensor_scalar", r16b[:, :], r16[:, :],
                     rcol[:, 3:4], None, op0=ALU.add)
                p.op("vector", "tensor_scalar", r16b[:, :], r16b[:, :], 0.0,
                     None, op0=ALU.is_ge)
                p.op("vector", "tensor_tensor", r16b[:, :], r16b[:, :],
                     r16[:, :], op=ALU.mult)
                p.op("vector", "tensor_scalar", g05[:, m * 16:(m + 1) * 16],
                     r16b[:, :], 0.5, None, op0=ALU.mult)
                p.op("vector", "tensor_copy", gbf[:, :], r16b[:, :], sem=sv)
                gv = p.cnt[sv]
                p.wait("tensor", sv, gv)
                p.op("tensor", "transpose", psg[:, :], gbf[:, :],
                     p.identb_s[:, :], sem=sm)
                tv = p.cnt[sm]
                p.wait("scalar", sm, tv)
                p.op("scalar", "copy", gfm[0:16, m * 128:(m + 1) * 128],
                     psg[:, :], sem=ss)
                p.wait("tensor", ss, p.cnt[ss])
                p.wait("tensor", sv, rr_v)
        router_v = p.cnt[ss]
        if dbg:
            o1 = p.outp("dbg_g05", [128, 16 * 16], F32)
            o2 = p.outp("dbg_gfm", [16, NT], BF16)
            p.wait("sync", ss, router_v)
            p.wait("sync", sv, p.cnt[sv])
            p.dma("sync", o1, g05[:, :], "dbgst")
            v = p.dma("sync", o2, gfm[:, :], "dbgst")
            if dbg == {"router"}:
                p.wait("sync", "dbgst", v)
                return (sv, p.cnt[sv])

        with ExitStack() as st2:
            pssh = [st2.enter_context(
                nc.psum_tensor(tag + "pss%d" % i, [128, 512], F32))
                for i in range(2)]
            slot_ready = [None, None]
            p.wait("tensor", ss, router_v)
            for m in range(16):
                for oh in range(2):
                    slot = (m * 2 + oh) % 2
                    psu = pssh[slot]
                    if slot_ready[slot] is not None:
                        p.wait("tensor", sv, slot_ready[slot])
                    for cc in range(NCH):
                        p.op("tensor", "matmul", psu[:, :],
                             p.u_s[:, cc * NT + m * 128: cc * NT + (m + 1) * 128],
                             p.wseff_s[:, cc * 1024 + oh * 512: cc * 1024 + (oh + 1) * 512],
                             start=(cc == 0), stop=False)
                    p.op("tensor", "matmul", psu[:, :],
                         gfm[0:16, m * 128:(m + 1) * 128],
                         p.br_s[0:16, oh * 512:(oh + 1) * 512],
                         start=False, stop=False)
                    p.op("tensor", "matmul", psu[:, :], p.ones_s[0:1, :],
                         p.bsrow_s[0:1, oh * 512:(oh + 1) * 512],
                         start=False, stop=True, sem=sm)
                    mmv = p.cnt[sm]
                    p.wait("vector", sm, mmv)
                    p.op("vector", "tensor_scalar",
                         acc[:, m * 1024 + oh * 512: m * 1024 + (oh + 1) * 512],
                         psu[:, :], 0.5, None, op0=ALU.mult, sem=sv)
                    slot_ready[slot] = p.cnt[sv]
        shared_v = p.cnt[sv]
        if "shared" in dbg:
            o3 = p.outp("dbg_accS", [128, 16 * 1024], F32)
            p.wait("sync", sv, shared_v)
            v = p.dma("sync", o3, acc[:, :], "dbgst")
            if "experts" not in dbg:
                p.wait("sync", "dbgst", v)
                return (sv, p.cnt[sv])

        with ExitStack() as st2:
            wrbuf = [st2.enter_context(
                nc.sbuf_tensor(tag + "wr%d" % i, [128, NCH * 512], BF16))
                for i in range(2)]
            pse = [st2.enter_context(
                nc.psum_tensor(tag + "pe%d" % i, [128, 512], F32))
                for i in range(8)]
            pairs = [(e, oh) for e in range(16) for oh in range(2)]
            sgs = [sg + "A", sg + "B"]
            buf_ready = {}
            for i in (0, 1):
                e, oh = pairs[i]
                src = wrt_in[e, :, oh * 512:(oh + 1) * 512].rearrange(
                    "(c p) n -> p c n", p=128)
                dst = wrbuf[i % 2][:, :].rearrange("p (c n) -> p c n", n=512)
                buf_ready[i] = p.dma("gpsimd", dst, src, sgs[i % 2])
            bank_free = [None] * 8
            p.wait("tensor", sv, shared_v)
            for i, (e, oh) in enumerate(pairs):
                p.wait("tensor", sgs[i % 2], buf_ready[i])
                wb = wrbuf[i % 2]
                for mg in range(4):
                    half = mg % 2
                    for cc in range(NCH):
                        for mi in range(4):
                            m = mg * 4 + mi
                            bank = half * 4 + mi
                            if cc == 0 and bank_free[bank] is not None:
                                p.wait("tensor", sv, bank_free[bank])
                            p.op("tensor", "matmul", pse[bank][:, :],
                                 p.u_s[:, cc * NT + m * 128: cc * NT + (m + 1) * 128],
                                 wb[:, cc * 512:(cc + 1) * 512],
                                 start=(cc == 0), stop=(cc == NCH - 1),
                                 sem=(sm if (cc == NCH - 1 and mi == 3) else None))
                    mmv = p.cnt[sm]
                    p.wait("vector", sm, mmv)
                    for mi in range(4):
                        m = mg * 4 + mi
                        bank = half * 4 + mi
                        asl = acc[:, m * 1024 + oh * 512: m * 1024 + (oh + 1) * 512]
                        p.op("vector", "scalar_tensor_tensor", asl,
                             pse[bank][:, :], g05[:, m * 16 + e: m * 16 + e + 1],
                             asl, op0=ALU.mult, op1=ALU.add, sem=sv)
                        bank_free[bank] = p.cnt[sv]
                if i + 2 < len(pairs):
                    e2, oh2 = pairs[i + 2]
                    p.wait("gpsimd", sm, p.cnt[sm])
                    src = wrt_in[e2, :, oh2 * 512:(oh2 + 1) * 512].rearrange(
                        "(c p) n -> p c n", p=128)
                    dst = wrbuf[i % 2][:, :].rearrange("p (c n) -> p c n", n=512)
                    buf_ready[i + 2] = p.dma("gpsimd", dst, src, sgs[i % 2])
        expert_v = p.cnt[sv]
        if "experts" in dbg:
            o4 = p.outp("dbg_accE", [128, 16 * 1024], F32)
            p.wait("sync", sv, expert_v)
            v = p.dma("sync", o4, acc[:, :], "dbgst")
            p.wait("sync", "dbgst", v)
            return (sv, p.cnt[sv])

        for cc in range(NCH):
            xc = p.x_s[:, cc * NT:(cc + 1) * NT]
            p.op("vector", "scalar_tensor_tensor", xc,
                 p.u_s[:, cc * NT:(cc + 1) * NT], 0.5, xc,
                 op0=ALU.mult, op1=ALU.add, sem=sv)

        with ExitStack() as st2:
            pst = [st2.enter_context(
                nc.psum_tensor(tag + "pt%d" % i, [128, 512], F32))
                for i in range(2)]
            grp_free = [None, None]
            p.wait("tensor", sv, expert_v)
            for m in range(16):
                for og in range(2):
                    slot = (m * 2 + og) % 2
                    if grp_free[slot] is not None:
                        p.wait("tensor", sv, grp_free[slot])
                    for oi in range(4):
                        oc = og * 4 + oi
                        p.op("tensor", "transpose",
                             pst[slot][:, oi * 128:(oi + 1) * 128],
                             acc[:, m * 1024 + oc * 128: m * 1024 + (oc + 1) * 128],
                             p.identf_s[:, :],
                             sem=(sm if oi == 3 else None))
                    mmv = p.cnt[sm]
                    p.wait("vector", sm, mmv)
                    for oi in range(4):
                        oc = og * 4 + oi
                        xsl = p.x_s[:, oc * NT + m * 128: oc * NT + (m + 1) * 128]
                        p.op("vector", "tensor_tensor", xsl, xsl,
                             pst[slot][:, oi * 128:(oi + 1) * 128], op=ALU.add,
                             sem=sv)
                    grp_free[slot] = p.cnt[sv]
    return (sv, p.cnt[sv])


def emit_attn_prep(p, L, deps=()):
    nc = p.nc
    tag = "ap%d" % L
    sv, ss, sm = "vch", "sch", "mm"
    for s, v in deps:
        for eng in ("sync", "gpsimd", "tensor", "vector", "scalar"):
            p.wait(eng, s, v)
    if not hasattr(p, "kc_d"):
        p.kc_d = [p.nc.dram_tensor("kc%d" % g, [256, NT], BF16) for g in range(4)]
        p.vc_d = [p.nc.dram_tensor("vc%d" % g, [NT, 256], BF16) for g in range(4)]
        p.gk_d = [p.nc.dram_tensor("gk%d" % g, [8 * 256, NT], BF16) for g in range(4)]
        p.gv_d = [p.nc.dram_tensor("gv%d" % g, [8 * NT, 256], BF16) for g in range(4)]
        mk = p.inp("maskt", [128, 8 * 128], BF16)
        p.mask_v = p.dma("sync", p.mask_s[:, :], mk, "ldc")
    if L == 0:
        p.cos_s = p.sb("cos_s", [128, NT], BF16)
        p.sin_s = p.sb("sin_s", [128, NT], BF16)
        p.perm_s = p.sb("perm_s", [128, 128], BF16)
        p.dma("sync", p.cos_s[:, :], p.inp("cos_t", [128, NT], BF16), "ldc")
        p.dma("sync", p.sin_s[:, :], p.inp("sin_t", [128, NT], BF16), "ldc")
        v = p.dma("sync", p.perm_s[:, :], p.inp("perm", [128, 128], BF16), "ldc")
        return ("ldc", v)
    p.m2k_s = p.sb("m2k_s", [128, 128], BF16)
    p.m2v_s = p.sb("m2v_s", [128, 128], BF16)
    with ExitStack() as st:
        wkc = st.enter_context(nc.sbuf_tensor(tag + "wkc", [32, 64], F32))
        wke = st.enter_context(nc.sbuf_tensor(tag + "wke", [32, 64], F32))
        wvc = st.enter_context(nc.sbuf_tensor(tag + "wvc", [32, 64], F32))
        wve = st.enter_context(nc.sbuf_tensor(tag + "wve", [32, 64], F32))
        psk = st.enter_context(nc.psum_tensor(tag + "psk", [64, 64], F32))
        psv = st.enter_context(nc.psum_tensor(tag + "psv", [64, 64], F32))
        p.dma("sync", wkc[:, :], p.inp("wkcT", [32, 64], F32), "ldw")
        p.dma("sync", wke[:, :], p.inp("wke", [32, 64], F32), "ldw")
        p.dma("sync", wvc[:, :], p.inp("wvcT", [32, 64], F32), "ldw")
        v0 = p.dma("sync", wve[:, :], p.inp("wve", [32, 64], F32), "ldw")
        mz = p.op("gpsimd", "memset", p.m2k_s[:, :], 0.0)
        mz = p.op("gpsimd", "memset", p.m2v_s[:, :], 0.0)
        p.wait("tensor", "ldw", v0)
        p.op("tensor", "matmul", psk[:, :], wkc[:, :], wke[:, :],
             start=True, stop=True, sem=sm)
        p.op("tensor", "matmul", psv[:, :], wvc[:, :], wve[:, :],
             start=True, stop=True, sem=sm)
        mmv = p.cnt[sm]
        p.wait("scalar", sm, mmv)
        p.wait("scalar", "gch", mz)
        p.op("scalar", "copy", p.m2k_s[0:64, 0:64], psk[:, :], sem=ss)
        p.op("scalar", "copy", p.m2k_s[64:128, 64:128], psk[:, :], sem=ss)
        p.op("scalar", "copy", p.m2v_s[0:64, 0:64], psv[:, :], sem=ss)
        p.op("scalar", "copy", p.m2v_s[64:128, 64:128], psv[:, :], sem=ss)
    return (ss, p.cnt[ss])


def emit_attn(p, tag, L, u_dep, prep_dep):
    nc = p.nc
    sv, ss, sm = "vch", "sch", "mm"
    scc = tag + "c"
    W = {}
    for nm in ("Wq", "Wk", "Wv", "Wp"):
        W[nm] = p.inp(tag + nm, [1024, 1024], F32)
    bq = p.inp(tag + "bq", [128, 8], F32)
    bk = p.inp(tag + "bk", [128, 8], F32)
    bp = p.inp(tag + "bp", [128, 8], F32)
    bvr = p.inp(tag + "bvr", [1, 1024], F32)

    for eng in ("tensor", "vector", "scalar", "gpsimd", "sync"):
        p.wait(eng, u_dep[0], u_dep[1])
        p.wait(eng, prep_dep[0], prep_dep[1])
        p.wait(eng, "ldc", p.mask_v)

    with ExitStack() as st:
        sb = lambda n, sh, dt: st.enter_context(nc.sbuf_tensor(tag + "_" + n, sh, dt))
        q_s = sb("q", [128, NCH * NT], BF16)
        bq_s = sb("bq", [128, 8], F32)
        bk_s = sb("bk", [128, 8], F32)
        bp_s = sb("bp", [128, 8], F32)
        bvr_s = sb("bvr", [1, 1024], BF16)
        vstg = sb("vstg", [128, 512], BF16)

        p.dma("sync", bq_s[:, :], bq, "ldw")
        p.dma("sync", bk_s[:, :], bk, "ldw")
        p.dma("sync", bp_s[:, :], bp, "ldw")
        bv_v = p.dma("gpsimd", bvr_s[:, :], bvr, tag + "bv")

        with ExitStack() as st2:
            psqk = [st2.enter_context(nc.psum_tensor(tag + "pq%d" % i,
                                                     [128, 512], F32))
                    for i in range(2)]
            psw = st2.enter_context(nc.psum_tensor(tag + "psw", [128, 512], F32))
            wsl = [st2.enter_context(nc.sbuf_tensor(tag + "_wsl%d" % i,
                                                    [128, 8 * 128], BF16))
                   for i in range(2)]
            kraw = st2.enter_context(nc.sbuf_tensor(tag + "_kraw", [128, 512], BF16))
            ksw = st2.enter_context(nc.sbuf_tensor(tag + "_ksw", [128, 512], BF16))
            t1 = st2.enter_context(nc.sbuf_tensor(tag + "_t1", [128, 512], BF16))
            wq_jobs = [("Wq", oc) for oc in range(NCH)] + \
                      [("Wk", oc) for oc in range(NCH)]
            wsem = [tag + "wA", tag + "wB"]
            wrdy = {}
            for j in (0, 1):
                nm, oc = wq_jobs[j]
                src = W[nm][:, oc * 128:(oc + 1) * 128].rearrange(
                    "(c p) n -> p c n", p=128)
                wrdy[j] = p.dma("gpsimd", wsl[j % 2][:, :].rearrange(
                    "p (c n) -> p c n", n=128), src, wsem[j % 2])
            slot_rd = [None, None]
            for j, (nm, oc) in enumerate(wq_jobs):
                p.wait("tensor", wsem[j % 2], wrdy[j])
                wb = wsl[j % 2]
                for tc in range(4):
                    slot = tc % 2
                    if slot_rd[slot] is not None:
                        p.wait("tensor", ss, slot_rd[slot])
                    for cc in range(NCH):
                        p.op("tensor", "matmul", psqk[slot][:, :],
                             wb[:, cc * 128:(cc + 1) * 128],
                             p.u_s[:, cc * NT + tc * 512: cc * NT + (tc + 1) * 512],
                             start=(cc == 0), stop=(cc == NCH - 1),
                             sem=(sm if cc == NCH - 1 else None))
                    mmv = p.cnt[sm]
                    p.wait("scalar", sm, mmv)
                    isq = (nm == "Wq")
                    bias = (bq_s if isq else bk_s)[:, oc:oc + 1]
                    if L == 0:
                        p.op("scalar", "activation", kraw[:, :],
                             psqk[slot][:, :], AF.Identity, bias=bias, sem=ss)
                        kv = p.cnt[ss]
                        slot_rd[slot] = kv
                        p.wait("tensor", ss, kv)
                        p.op("tensor", "matmul", psw[:, :], p.perm_s[:, :],
                             kraw[:, :], start=True, stop=True, sem=sm)
                        swv = p.cnt[sm]
                        p.wait("scalar", sm, swv)
                        p.op("scalar", "copy", ksw[:, :], psw[:, :], sem=ss)
                        swc = p.cnt[ss]
                        cs = slice(tc * 512, (tc + 1) * 512)
                        p.wait("vector", ss, kv)
                        p.op("vector", "tensor_tensor", t1[:, :], kraw[:, :],
                             p.cos_s[:, cs], op=ALU.mult)
                        p.wait("vector", ss, swc)
                        p.op("vector", "tensor_tensor", ksw[:, :], ksw[:, :],
                             p.sin_s[:, cs], op=ALU.mult)
                        dst = (q_s[:, oc * NT + tc * 512: oc * NT + (tc + 1) * 512]
                               if isq else kraw[:, :])
                        rv = p.op("vector", "tensor_tensor", dst, t1[:, :],
                                  ksw[:, :], op=ALU.add)
                        p.wait("scalar", sv, rv)
                        if not isq:
                            hg, half = oc // 2, oc % 2
                            p.wait("sync", sv, rv)
                            p.dma("sync",
                                  p.kc_d[hg].ap()[half * 128:(half + 1) * 128,
                                                  tc * 512:(tc + 1) * 512],
                                  kraw[:, :], tag + "kv%d" % hg)
                            p.wait("scalar", tag + "kv%d" % hg,
                                   p.cnt[tag + "kv%d" % hg])
                    else:
                        if isq:
                            p.op("scalar", "activation",
                                 q_s[:, oc * NT + tc * 512: oc * NT + (tc + 1) * 512],
                                 psqk[slot][:, :], AF.Identity, bias=bias,
                                 sem=ss)
                            slot_rd[slot] = p.cnt[ss]
                        else:
                            p.op("scalar", "activation", kraw[:, :],
                                 psqk[slot][:, :], AF.Identity, bias=bias,
                                 sem=ss)
                            kv = p.cnt[ss]
                            slot_rd[slot] = kv
                            p.wait("tensor", ss, kv)
                            p.op("tensor", "matmul", psw[:, :], p.m2k_s[:, :],
                                 kraw[:, :], start=True, stop=True, sem=sm)
                            swv = p.cnt[sm]
                            p.wait("scalar", sm, swv)
                            p.op("scalar", "copy", ksw[:, :], psw[:, :],
                                 sem=ss)
                            kxv = p.cnt[ss]
                            hg, half = oc // 2, oc % 2
                            p.wait("sync", ss, kxv)
                            p.dma("sync",
                                  p.kc_d[hg].ap()[half * 128:(half + 1) * 128,
                                                  tc * 512:(tc + 1) * 512],
                                  ksw[:, :], tag + "kv%d" % hg)
                            p.wait("scalar", tag + "kv%d" % hg,
                                   p.cnt[tag + "kv%d" % hg])
                if j + 2 < len(wq_jobs):
                    nm2, oc2 = wq_jobs[j + 2]
                    p.wait("gpsimd", sm, p.cnt[sm])
                    src = W[nm2][:, oc2 * 128:(oc2 + 1) * 128].rearrange(
                        "(c p) n -> p c n", p=128)
                    wrdy[j + 2] = p.dma("gpsimd", wsl[j % 2][:, :].rearrange(
                        "p (c n) -> p c n", n=128), src, wsem[j % 2])

        with ExitStack() as st2:
            psv = [st2.enter_context(nc.psum_tensor(tag + "pv%d" % i,
                                                    [128, 512], F32))
                   for i in range(2)]
            wvh = [st2.enter_context(nc.sbuf_tensor(tag + "_wvh%d" % i,
                                                    [128, 8 * 512], BF16))
                   for i in range(2)]
            vsem = [tag + "vA", tag + "vB"]
            vrdy = {}
            p.wait("gpsimd", ss, p.cnt[ss])
            p.wait("gpsimd", sv, p.cnt[sv])
            p.wait("gpsimd", sm, p.cnt[sm])
            for g in range(4):
                if (tag + "kv%d" % g) in p.cnt:
                    p.wait("gpsimd", tag + "kv%d" % g, p.cnt[tag + "kv%d" % g])
            for j in (0, 1):
                src = W["Wv"][:, j * 512:(j + 1) * 512].rearrange(
                    "(c p) n -> p c n", p=128)
                vrdy[j] = p.dma("gpsimd", wvh[j][:, :].rearrange(
                    "p (c n) -> p c n", n=512), src, vsem[j])
            p.wait("tensor", tag + "bv", bv_v)
            p.wait("tensor", ss, p.cnt[ss])
            p.wait("tensor", sv, p.cnt[sv])
            for g in range(4):
                if (tag + "kv%d" % g) in p.cnt:
                    p.wait("tensor", tag + "kv%d" % g, p.cnt[tag + "kv%d" % g])
            slot_rd = [None, None]
            for oh in range(2):
                p.wait("tensor", vsem[oh], vrdy[oh])
                for m in range(16):
                    slot = m % 2
                    if slot_rd[slot] is not None:
                        p.wait("tensor", ss, slot_rd[slot])
                    for cc in range(NCH):
                        p.op("tensor", "matmul", psv[slot][:, :],
                             p.u_s[:, cc * NT + m * 128: cc * NT + (m + 1) * 128],
                             wvh[oh][:, cc * 512:(cc + 1) * 512],
                             start=(cc == 0), stop=False)
                    p.op("tensor", "matmul", psv[slot][:, :],
                         p.ones_s[0:1, :],
                         bvr_s[0:1, oh * 512:(oh + 1) * 512],
                         start=False, stop=True, sem=sm)
                    mmv = p.cnt[sm]
                    p.wait("scalar", sm, mmv)
                    p.op("scalar", "copy", vstg[:, :], psv[slot][:, :], sem=ss)
                    vv = p.cnt[ss]
                    slot_rd[slot] = vv
                    p.wait("sync", ss, vv)
                    for hh in range(2):
                        hg = oh * 2 + hh
                        p.dma("sync",
                              p.vc_d[hg].ap()[m * 128:(m + 1) * 128, :],
                              vstg[:, hh * 256:(hh + 1) * 256],
                              tag + "kv%d" % hg)
                    for hh in range(2):
                        hgx = oh * 2 + hh
                        p.wait("scalar", tag + "kv%d" % hgx,
                               p.cnt[tag + "kv%d" % hgx])

        for hg in range(4):
            p.wait("gpsimd", tag + "kv%d" % hg, p.cnt[tag + "kv%d" % hg])
            p.op("gpsimd", "collective_compute",
                 "AllGather", ALU.bypass,
                 replica_groups=[list(range(8))],
                 ins=[p.kc_d[hg].ap().opt()], outs=[p.gk_d[hg].ap().opt()],
                 sem=scc)
            p.op("gpsimd", "collective_compute",
                 "AllGather", ALU.bypass,
                 replica_groups=[list(range(8))],
                 ins=[p.vc_d[hg].ap().opt()], outs=[p.gv_d[hg].ap().opt()],
                 sem=scc)
        cc_base = p.cnt[scc] - 8

        khg = [sb("khg0", [128, 2 * T], BF16)]
        vhg = [sb("vhg0", [128, 4 * 32 * 65], BF16)]
        ohgb = sb("ohgb", [128, 2 * 512], BF16)
        ohgb_raw = ohgb
        sig_s = sb("sig", [1, 2048], BF16)
        rec_row = sb("rcr", [1, 512], F32)
        pbuf = [sb("pb%d" % i, [128, 1024], BF16) for i in range(2)]
        wp_s = sb("wp", [128, 2 * 1024], BF16)
        p.wait("gpsimd", ss, p.cnt[ss])
        p.wait("gpsimd", sm, p.cnt[sm])
        mz1 = p.op("gpsimd", "memset", vhg[0][:, :], 1.0)
        p.wait("sync", "gch", mz1)

        with ExitStack() as st2:
            psS = [st2.enter_context(nc.psum_tensor(tag + "pS%d" % i,
                                                    [128, 1024], F32))
                   for i in range(2)]
            psO = [st2.enter_context(nc.psum_tensor(tag + "pO%d" % i,
                                                    [65, 512], F32))
                   for i in range(2)]
            psP = [st2.enter_context(nc.psum_tensor(tag + "pP0",
                                                    [128, 512], F32))]
            stg = [tag + "sA", tag + "sB"]
            wpsem = [tag + "pA2", tag + "pB2"]

            def stage(i, hg, b):
                sem = stg[0]
                gk = p.gk_d[hg].ap()
                gv = p.gv_d[hg].ap()
                n0 = p.cnt.get(sem, 0)
                for r in range(8):
                    for hp in range(2):
                        src = gk[256 * r + 128 * hp: 256 * r + 128 * (hp + 1),
                                 b * 512:(b + 1) * 512]
                        src = src.rearrange("p (t n) -> p t n", n=128)
                        dst = khg[0][:, hp * T: (hp + 1) * T].rearrange(
                            "p (t n) -> p t n", n=128)[:, r::8, :]
                        p.dma("sync", dst, src, sem)
                    srcv = gv[2048 * r + 512 * b: 2048 * r + 512 * (b + 1), :]
                    srcv = srcv.rearrange("(t p) (h d) -> p h t d", p=128, d=64)
                    dstv = vhg[0][:, :].rearrange(
                        "p (h k e) -> p h k e", h=4, e=65)[:, :, r::8, 0:64]
                    p.dma("sync", dstv, srcv, sem)
                return p.cnt[sem]

            def wp_load(hg):
                src = W["Wp"][2 * hg * 128:(2 * hg + 2) * 128, :].rearrange(
                    "(c p) n -> p c n", p=128)
                return p.dma("gpsimd", wp_s[:, :].rearrange(
                    "p (c n) -> p c n", n=1024), src, wpsem[0])

            hgb = [(hg, b) for hg in range(4) for b in range(B)]
            stg_rdy = {}
            wp_rdy = {}
            reg_rd = [0, 0]
            reg_av = [0, 0]
            po_free = [0, 0]
            psS_guard = 0
            wp_rdy[0] = wp_load(0)
            last_proj_dve = None
            pe_prev = 0
            for i, (hg, b) in enumerate(hgb):
                if i == 0:
                    p.wait("sync", scc, cc_base + 2)
                    stg_rdy[0] = stage(0, hg, b)
                p.wait("tensor", stg[0], stg_rdy[i])
                if b == 0:
                    p.wait("tensor", wpsem[0], wp_rdy[hg])
                kb, vb = khg[0], vhg[0]
                wpb = wp_s
                def tile_regions():
                    regs, cur, off = [], [], 0
                    for kt in range(32):
                        N = 512 - 128 * (kt // 8)
                        if off % 512 + N > 512 and off % 512 != 0:
                            off = (off // 512 + 1) * 512
                        if off + N > 1024:
                            regs.append(cur)
                            cur, off = [], 0
                        cur.append((kt, off, N))
                        off += N
                    regs.append(cur)
                    return regs
                TREG = tile_regions()
                R = [(hq, ri) for hq in range(4) for ri in range(len(TREG))]
                score_mm, exp_rd = {}, {}

                def emit_scores(j):
                    hq, ri = R[j]
                    reg = j % 2
                    hb = (hq % 2) * 64
                    hp = hq // 2
                    qbase = (hg * 2 + hp) * NT + b * 512
                    if reg_rd[reg]:
                        p.wait("tensor", ss, reg_rd[reg])
                    if reg == 0 and psS_guard:
                        p.wait("tensor", sv, psS_guard)
                    for (kt, off, N) in TREG[ri]:
                        q0 = 128 * (kt // 8)
                        p.op("tensor", "matmul",
                             psS[reg][:, off:off + N],
                             kb[hb:hb + 64,
                                hp * T + kt * 128: hp * T + (kt + 1) * 128],
                             q_s[hb:hb + 64, qbase + q0: qbase + 512],
                             start=(off % 512 == 0), stop=False)
                        p.op("tensor", "matmul",
                             psS[reg][:, off:off + 128],
                             p.identb_s[:, :],
                             p.mask_s[:, (kt % 8) * 128:(kt % 8 + 1) * 128],
                             start=False, stop=True,
                             sem=(sm if (kt, off, N) == TREG[ri][-1] else None))
                    score_mm[j] = p.cnt[sm]

                def emit_exp(j):
                    reg = j % 2
                    p.wait("scalar", sm, score_mm[j])
                    if reg_av[reg]:
                        p.wait("scalar", sm, reg_av[reg])
                    p.op("scalar", "activation", pbuf[reg][:, :],
                         psS[reg][:, :], AF.Exp,
                         bias=p.cst_s[:, 0:1], scale=SCALE, sem=ss)
                    reg_rd[reg] = p.cnt[ss]
                    exp_rd[j] = reg_rd[reg]

                def emit_av(j):
                    hq, ri = R[j]
                    reg = j % 2
                    po = psO[hq % 2]
                    p.wait("tensor", ss, exp_rd[j])
                    if ri == 0 and po_free[hq % 2]:
                        p.wait("tensor", ss, po_free[hq % 2])
                    for (kt, off, N) in TREG[ri]:
                        q0 = 128 * (kt // 8)
                        p.op("tensor", "matmul", po[:, q0:512],
                             vb[:, (hq * 32 + kt) * 65:
                                (hq * 32 + kt) * 65 + 65],
                             pbuf[reg][:, off:off + N],
                             start=(kt == 0),
                             stop=(ri == len(TREG) - 1 and
                                   (kt, off, N) == TREG[ri][-1]),
                             sem=(sm if (kt, off, N) == TREG[ri][-1] else None))
                    reg_av[reg] = p.cnt[sm]
                    if ri == len(TREG) - 1:
                        hb = (hq % 2) * 64
                        hp = hq // 2
                        p.wait("scalar", sm, reg_av[reg])
                        p.op("scalar", "copy",
                             ohgb_raw[hb:hb + 64, hp * 512:(hp + 1) * 512],
                             po[0:64, 0:512])
                        p.op("scalar", "copy",
                             sig_s[0:1, hq * 512:(hq + 1) * 512],
                             po[64:65, 0:512], sem=ss)
                        po_free[hq % 2] = p.cnt[ss]

                emit_scores(0)
                emit_exp(0)
                if len(R) > 1:
                    emit_scores(1)
                    emit_exp(1)
                for j in range(len(R)):
                    emit_av(j)
                    if j + 2 < len(R):
                        emit_scores(j + 2)
                        emit_exp(j + 2)
                av_end = p.cnt[sm]

                if i + 1 < len(hgb):
                    hg2, b2 = hgb[i + 1]
                    p.wait("sync", scc, cc_base + 2 * (hg2 + 1))
                    p.wait("sync", sm, av_end)
                    stg_rdy[i + 1] = stage(i + 1, hg2, b2)

                sig_done = p.cnt[ss]
                p.wait("vector", ss, sig_done)
                p.wait("tensor", ss, max(reg_rd))
                if psS_guard:
                    p.wait("tensor", sv, psS_guard)
                p.wait("tensor", sv, p.cnt[sv])
                nv = 0
                for hq in range(4):
                    hb = (hq % 2) * 64
                    hp = hq // 2
                    rcp = p.op("vector", "reciprocal", rec_row[:, :],
                               sig_s[0:1, hq * 512:(hq + 1) * 512])
                    p.wait("tensor", sv, rcp)
                    if nv:
                        p.wait("tensor", sv, nv)
                    p.op("tensor", "matmul",
                         psP[0][hb:hb + 64, 0:512],
                         p.onesf_s[0:1, 0:64],
                         rec_row[0:1, :],
                         start=True, stop=True, sem=sm)
                    bcv = p.cnt[sm]
                    p.wait("vector", sm, bcv)
                    nv = p.op("vector", "tensor_tensor",
                              ohgb[hb:hb + 64, hp * 512:(hp + 1) * 512],
                              ohgb_raw[hb:hb + 64, hp * 512:(hp + 1) * 512],
                              psP[0][hb:hb + 64, 0:512], op=ALU.mult)
                psS_guard = 0
                if L == 1:
                    p.wait("tensor", sv, nv)
                    for half in range(2):
                        p.op("tensor", "matmul",
                             psS[1][:, half * 512:(half + 1) * 512],
                             p.m2v_s[:, :], ohgb[:, half * 512:(half + 1) * 512],
                             start=True, stop=True,
                             sem=(sm if half == 1 else None))
                    mvv = p.cnt[sm]
                    p.wait("scalar", sm, mvv)
                    p.op("scalar", "copy", ohgb[:, :], psS[1][:, :], sem=ss)
                reg_rd = [p.cnt[ss], p.cnt[ss]]
                p.wait("tensor", sv, p.cnt[sv])
                p.wait("tensor", ss, p.cnt[ss])
                for oc in range(NCH):
                    pp = psP[0]
                    for hp in range(2):
                        p.op("tensor", "matmul", pp[:, :],
                             wpb[:, hp * 1024 + oc * 128: hp * 1024 + (oc + 1) * 128],
                             ohgb[:, hp * 512:(hp + 1) * 512],
                             start=(hp == 0), stop=(hp == 1),
                             sem=(sm if hp == 1 else None))
                    mmv = p.cnt[sm]
                    xsl = p.x_s[:, oc * NT + b * 512: oc * NT + (b + 1) * 512]
                    if hg == 0:
                        p.wait("scalar", sm, mmv)
                        p.op("scalar", "activation", vstg[:, :], pp[:, :],
                             AF.Identity, bias=bp_s[:, oc:oc + 1], sem=ss)
                        prv = p.cnt[ss]
                        p.wait("vector", ss, prv)
                        last_proj_dve = p.op("vector", "tensor_tensor", xsl,
                                             xsl, vstg[:, :], op=ALU.add)
                        p.wait("scalar", sv, last_proj_dve)
                    else:
                        p.wait("vector", sm, mmv)
                        last_proj_dve = p.op("vector", "tensor_tensor", xsl,
                                             xsl, pp[:, :], op=ALU.add)
                    p.wait("tensor", sv, last_proj_dve)
                if b == B - 1 and hg + 1 < 4:
                    p.wait("gpsimd", sm, p.cnt[sm])
                    wp_rdy[hg + 1] = wp_load(hg + 1)
    return (sv, p.cnt[sv])


def bias_cols(b):
    return np.ascontiguousarray(np.asarray(b, np.float32).reshape(8, 128).T)


def attn_host_inputs(L, core, inputs):
    tag = "a%d" % L
    d = {}
    if L == 0:
        d[tag + "Wq"] = np.ascontiguousarray(inputs["L0_attn_Wq"])
        d[tag + "Wk"] = np.ascontiguousarray(inputs["L0_attn_Wk"])
        d[tag + "Wv"] = np.ascontiguousarray(inputs["L0_attn_Wv"])
        d[tag + "Wp"] = np.ascontiguousarray(inputs["L0_attn_Wp"])
        d[tag + "bq"] = bias_cols(inputs["L0_attn_bq"])
        d[tag + "bk"] = bias_cols(inputs["L0_attn_bk"])
        d[tag + "bp"] = bias_cols(inputs["L0_attn_bp"])
        d[tag + "bvr"] = np.asarray(inputs["L0_attn_bv"], np.float32)[None, :]
        pk = ((8 * np.arange(4)[:, None] + core) * 128 +
              np.arange(128)[None, :]).reshape(-1).astype(np.float32)
        pos = np.tile(pk, 4)
        dd = np.arange(128) % 64
        inv = (10000.0 ** (-2.0 * (dd // 2) / 64.0)).astype(np.float32)
        ang = pos[None, :] * inv[:, None]
        sign = np.where(dd % 2 == 0, -1.0, 1.0).astype(np.float32)
        d["cos_t"] = np.cos(ang).astype(BF)
        d["sin_t"] = (sign[:, None] * np.sin(ang)).astype(BF)
        pm = np.zeros((128, 128), np.float32)
        pm[np.arange(128), np.arange(128) ^ 1] = 1.0
        d["perm"] = pm.astype(BF)
    else:
        w = np.asarray(inputs["L1_attn_c_attn_w"], np.float32)
        bb = np.asarray(inputs["L1_attn_c_attn_b"], np.float32)
        d[tag + "Wq"] = np.ascontiguousarray(w[:, 0:1024])
        d[tag + "Wk"] = np.ascontiguousarray(w[:, 1024:2048])
        d[tag + "Wv"] = np.ascontiguousarray(w[:, 2048:3072])
        d[tag + "Wp"] = np.ascontiguousarray(inputs["L1_attn_c_proj_w"])
        d[tag + "bq"] = bias_cols(bb[0:1024])
        d[tag + "bk"] = bias_cols(bb[1024:2048])
        d[tag + "bp"] = bias_cols(inputs["L1_attn_c_proj_b"])
        d[tag + "bvr"] = bb[2048:3072][None, :].copy()
        d["wkcT"] = np.ascontiguousarray(inputs["L1_attn_wk_c"].T)
        d["wke"] = np.ascontiguousarray(inputs["L1_attn_wk_e"])
        d["wvcT"] = np.ascontiguousarray(inputs["L1_attn_wv_c"].T)
        d["wve"] = np.ascontiguousarray(inputs["L1_attn_wv_e"])
    ki = np.arange(128)[:, None]
    qi = np.arange(128)[None, :]
    mk = np.zeros((128, 8 * 128), np.float32)
    for cc in range(8):
        if cc == core:
            mk[:, cc * 128:(cc + 1) * 128] = np.where(ki <= qi, 0.0, NEG)
        elif cc > core:
            mk[:, cc * 128:(cc + 1) * 128] = NEG
    d["maskt"] = mk.astype(BF)
    return d


def build_attn_launch(L):
    p = Prog()
    emit_prelude(p)
    xd = emit_load_x(p)
    pd = emit_attn_prep(p, L)
    ud = emit_ln(p, "ln", [xd, pd])
    done = emit_attn(p, "a%d" % L, L, ud, pd)
    emit_store_x(p, [done])
    return p.finalize()


# version 05
# speedup vs baseline: 1.0131x; 1.0131x over previous
import numpy as np
from contextlib import ExitStack
import concourse.bass as bass
import concourse.mybir as mybir
from concourse.bass_utils import run_bass_kernel_spmd
import ml_dtypes

F32 = mybir.dt.float32
BF16 = mybir.dt.bfloat16
AF = mybir.ActivationFunctionType
ALU = mybir.AluOpType
AX = mybir.AxisListType
BF = ml_dtypes.bfloat16

C = 1024
B = 4
T = 4096
NT = 2048
NCH = 8
NEG = -1e33
EXPB = -20.0
SCALE = 0.125


class Prog:

    ENGINES = ("sync", "gpsimd", "vector", "scalar", "tensor")

    def __init__(self):
        self.nc = bass.Bass("TRN2", target_bir_lowering=False, debug=False)
        self.es = ExitStack()
        self.alloc_scope = self.es
        self.q = {e: [] for e in self.ENGINES}
        self.cnt = {}
        self.sems = {}

    def sem(self, name):
        if name not in self.sems:
            self.sems[name] = self.es.enter_context(self.nc.semaphore(name))
            self.cnt[name] = 0
        return self.sems[name]

    def inc(self, name, k=1):
        self.sem(name)
        self.cnt[name] += k
        return self.sems[name], self.cnt[name]

    def emit(self, eng, fn):
        self.q[eng].append(fn)

    def wait(self, eng, sem_name, v):
        s = self.sem(sem_name)
        self.emit(eng, lambda e, s=s, v=v: e.wait_ge(s, v))

    CHAINED = {"vector": "vch", "scalar": "sch", "gpsimd": "gch"}

    def op(self, eng, meth, *args, sem=None, inc=1, **kw):
        ch = self.CHAINED.get(eng)
        if ch is not None and meth not in ("dma_start", "collective_compute"):
            assert sem is None or sem == ch, (eng, meth, sem)
            s = self.sem(ch)
            prev = self.cnt[ch]
            self.cnt[ch] += 1

            def fn(e, a=args, k=kw, m=meth, s=s, prev=prev):
                if prev > 0:
                    e.wait_ge(s, prev)
                getattr(e, m)(*a, **k).then_inc(s, 1)
            self.emit(eng, fn)
            return self.cnt[ch]
        if sem is not None:
            s, v = self.inc(sem, inc)
            self.emit(eng, lambda e, a=args, k=kw, s=s, m=meth, i=inc:
                      getattr(e, m)(*a, **k).then_inc(s, i))
            return v
        self.emit(eng, lambda e, a=args, k=kw, m=meth: getattr(e, m)(*a, **k))
        return None

    def dma(self, eng, out, in_, sem):
        return self.op(eng, "dma_start", sem=sem, inc=16, out=out, in_=in_)

    def sb(self, name, shape, dt):
        return self.alloc_scope.enter_context(self.nc.sbuf_tensor(name, shape, dt))

    def sb_perm(self, name, shape, dt):
        return self.es.enter_context(self.nc.sbuf_tensor(name, shape, dt))

    def ps(self, name, shape, dt=F32):
        return self.es.enter_context(self.nc.psum_tensor(name, shape, dt))

    def inp(self, name, shape, dt):
        return self.nc.dram_tensor(name, shape, dt, kind="ExternalInput").ap()

    def outp(self, name, shape, dt):
        return self.nc.dram_tensor(name, shape, dt, kind="ExternalOutput").ap()

    def finalize(self):
        qq = self.q
        with self.nc.Block() as block:
            @block.sync
            def _(e):
                for f in qq["sync"]:
                    f(e)

            @block.gpsimd
            def _(e):
                for f in qq["gpsimd"]:
                    f(e)

            @block.vector
            def _(e):
                for f in qq["vector"]:
                    f(e)

            @block.scalar
            def _(e):
                for f in qq["scalar"]:
                    f(e)

            @block.tensor
            def _(e):
                for f in qq["tensor"]:
                    f(e)
        self.es.close()
        return self.nc


def emit_prelude(p):
    p.x_s = p.sb("x_s", [128, NCH * NT], F32)
    p.u_s = p.sb("u_s", [128, NCH * NT], BF16)
    p.cst_s = p.sb("cst_s", [128, 8], F32)
    p.ones_s = p.sb("ones_s", [128, 128], BF16)
    p.onesf_s = p.sb("onesf_s", [1, 128], F32)
    p.identb_s = p.sb("identb_s", [128, 128], BF16)
    p.identf_s = p.sb("identf_s", [128, 128], F32)
    p.mask_s = p.sb("mask_s", [128, 8 * 128], BF16)
    cst = p.inp("cst", [128, 8], F32)
    ones = p.inp("onesb", [128, 128], BF16)
    onesf = p.inp("onesf", [1, 128], F32)
    identb = p.inp("identb", [128, 128], BF16)
    identf = p.inp("identf", [128, 128], F32)
    p.dma("sync", p.cst_s[:, :], cst, "ldc")
    p.dma("sync", p.ones_s[:, :], ones, "ldc")
    p.dma("sync", p.identb_s[:, :], identb, "ldc")
    p.dma("sync", p.identf_s[:, :], identf, "ldc")
    p.ldc_v = p.dma("sync", p.onesf_s[:, :], onesf, "ldc")


def host_consts():
    cst = np.zeros((128, 8), np.float32)
    cst[:, 0] = EXPB
    cst[:, 1] = 1e-5
    cst[:, 2] = 0.5
    cst[:, 3] = 1.0
    return {"cst": cst,
            "onesb": np.ones((128, 128), BF),
            "onesf": np.ones((1, 128), np.float32),
            "identb": np.eye(128, dtype=BF),
            "identf": np.eye(128, dtype=np.float32)}


def emit_load_x(p):
    x_in = p.inp("x", [128, NCH * NT], F32)
    v = p.dma("sync", p.x_s[:, :], x_in, "ldx")
    return ("ldx", v)


def emit_store_x(p, deps):
    x_out = p.outp("y", [128, NCH * NT], F32)
    for s, v in deps:
        p.wait("sync", s, v)
    v = p.dma("sync", x_out, p.x_s[:, :], "stx")
    p.wait("sync", "stx", v)


def emit_ln(p, tag, deps):
    nc = p.nc
    sv, ss, sm = "vch", "sch", "mm"
    with ExitStack() as st:
        xbf = st.enter_context(nc.sbuf_tensor(tag + "xbf", [128, NCH * NT], BF16))
        xsq = st.enter_context(nc.sbuf_tensor(tag + "xsq", [128, NCH * NT], BF16))
        rows = st.enter_context(nc.sbuf_tensor(tag + "rw", [1, 1024], F32))
        pstA = st.enter_context(nc.psum_tensor(tag + "pA", [1, 512], F32))
        pstB = st.enter_context(nc.psum_tensor(tag + "pB", [1, 512], F32))
        pbcM = st.enter_context(nc.psum_tensor(tag + "pM", [128, 512], F32))
        pbcR = st.enter_context(nc.psum_tensor(tag + "pR", [128, 512], F32))

        for s, v in deps:
            p.wait("vector", s, v)
            p.wait("scalar", s, v)
            p.wait("tensor", s, v)
        p.wait("vector", "ldc", p.ldc_v)
        p.wait("scalar", "ldc", p.ldc_v)
        p.wait("tensor", "ldc", p.ldc_v)

        for cc in range(NCH):
            xc = p.x_s[:, cc * NT:(cc + 1) * NT]
            p.op("vector", "tensor_copy", xbf[:, cc * NT:(cc + 1) * NT], xc,
                 sem=sv)
            p.op("scalar", "activation", xsq[:, cc * NT:(cc + 1) * NT], xc,
                 AF.Square, sem=ss)
        vcop, scop = p.cnt[sv], p.cnt[ss]
        p.wait("tensor", sv, vcop)
        p.wait("tensor", ss, scop)

        for tc in range(4):
            for cc in range(NCH):
                off = cc * NT + tc * 512
                p.op("tensor", "matmul", pstA[:, :], p.ones_s[:, 0:1],
                     xbf[:, off:off + 512], start=(cc == 0),
                     stop=(cc == NCH - 1), sem=sm)
                p.op("tensor", "matmul", pstB[:, :], p.ones_s[:, 0:1],
                     xsq[:, off:off + 512], start=(cc == 0),
                     stop=(cc == NCH - 1), sem=sm)
            mmv = p.cnt[sm]
            p.wait("vector", sm, mmv)
            p.op("vector", "tensor_scalar", rows[0:1, 0:512], pstA[0:1, :],
                 1.0 / C, None, op0=ALU.mult, sem=sv)
            p.op("vector", "tensor_tensor", rows[0:1, 512:1024],
                 rows[0:1, 0:512], rows[0:1, 0:512], op=ALU.mult, sem=sv)
            p.op("vector", "scalar_tensor_tensor", rows[0:1, 512:1024],
                 pstB[0:1, :], 1.0 / C, rows[0:1, 512:1024],
                 op0=ALU.mult, op1=ALU.subtract, sem=sv)
            vv = p.cnt[sv]
            p.wait("scalar", sv, vv)
            p.op("scalar", "activation", rows[0:1, 512:1024],
                 rows[0:1, 512:1024], AF.Ln, bias=p.cst_s[0:1, 1:2], sem=ss)
            p.op("scalar", "activation", rows[0:1, 512:1024],
                 rows[0:1, 512:1024], AF.Exp, scale=-0.5, sem=ss)
            sv2 = p.cnt[ss]
            p.wait("tensor", ss, sv2)
            p.op("tensor", "matmul", pbcM[:, :], p.onesf_s[0:1, :],
                 rows[0:1, 0:512], start=True, stop=True, sem=sm)
            p.op("tensor", "matmul", pbcR[:, :], p.onesf_s[0:1, :],
                 rows[0:1, 512:1024], start=True, stop=True, sem=sm)
            mmv2 = p.cnt[sm]
            p.wait("vector", sm, mmv2)
            for cc in range(NCH):
                off = cc * NT + tc * 512
                xc = p.x_s[:, off:off + 512]
                p.op("vector", "tensor_tensor", xbf[:, off:off + 512], xc,
                     pbcM[:, :], op=ALU.subtract)
                p.op("vector", "tensor_tensor", p.u_s[:, off:off + 512],
                     xbf[:, off:off + 512], pbcR[:, :], op=ALU.mult, sem=sv)
            uvv = p.cnt[sv]
            if tc < 3:
                p.wait("tensor", sv, uvv)
    return (sv, p.cnt[sv])




def emit_ffn0_weff(p):
    nc = p.nc
    tag = "wf"
    sv, ss, sm = "vch", "sch", "mm"
    p.weff_s = p.sb("weff_s", [128, NCH * 1024], BF16)
    p.bffn_s = p.sb("bffn_s", [128, 8], F32)
    with ExitStack() as st:
        wt = st.enter_context(nc.sbuf_tensor(tag + "wt", [128, 1024], F32))
        ldt = st.enter_context(nc.sbuf_tensor(tag + "ldt", [100, 4], F32))
        dax = st.enter_context(nc.sbuf_tensor(tag + "dax", [100, 1024], F32))
        lut = st.enter_context(nc.sbuf_tensor(tag + "lut", [4, 50], F32))
        uat = st.enter_context(nc.sbuf_tensor(tag + "uat", [50, 1024], F32))
        aA = st.enter_context(nc.sbuf_tensor(tag + "A", [4, 1024], F32))
        bB = st.enter_context(nc.sbuf_tensor(tag + "B", [50, 1024], F32))
        bcol = st.enter_context(nc.sbuf_tensor(tag + "bc", [128, 8], F32))
        psA = st.enter_context(nc.psum_tensor(tag + "psA", [4, 1024], F32))
        psB = st.enter_context(nc.psum_tensor(tag + "psB", [50, 1024], F32))
        psC = st.enter_context(nc.psum_tensor(tag + "psC", [128, 1024], F32))

        wt_in = p.inp("ffnWT", [1024, 1024], F32)
        b_in = p.inp("ffnb", [128, 8], F32)
        ldt_in = p.inp("ffnldT", [100, 4], F32)
        dax_in = p.inp("ffndax", [100, 1024], F32)
        lut_in = p.inp("ffnluT", [4, 50], F32)
        uat_in = p.inp("ffnuaT", [50, 1024], F32)

        p.dma("sync", ldt[:, :], ldt_in, "ldw")
        p.dma("sync", dax[:, :], dax_in, "ldw")
        p.dma("sync", lut[:, :], lut_in, "ldw")
        p.dma("sync", bcol[:, :], b_in, "ldw")
        v0 = p.dma("sync", uat[:, :], uat_in, "ldw")
        p.wait("tensor", "ldw", v0)
        for half in range(2):
            p.op("tensor", "matmul", psA[:, half * 512:(half + 1) * 512],
                 ldt[:, :], dax[:, half * 512:(half + 1) * 512],
                 start=True, stop=True, sem=sm)
        p.wait("vector", sm, p.cnt[sm])
        p.op("vector", "tensor_copy", aA[:, :], psA[:, :], sem=sv)
        p.op("vector", "tensor_scalar", p.bffn_s[:, :], bcol[:, :], 0.5, None,
             op0=ALU.mult, sem=sv)
        p.wait("tensor", sv, p.cnt[sv])
        for half in range(2):
            p.op("tensor", "matmul", psB[:, half * 512:(half + 1) * 512],
                 lut[:, :], aA[:, half * 512:(half + 1) * 512],
                 start=True, stop=True, sem=sm)
        p.wait("vector", sm, p.cnt[sm])
        p.op("vector", "tensor_copy", bB[:, :], psB[:, :], sem=sv)
        bb_v = p.cnt[sv]
        p.wait("tensor", sv, bb_v)
        for cc in range(NCH):
            wv = p.dma("sync", wt[:, :],
                       wt_in[cc * 128:(cc + 1) * 128, :], "ldw")
            for half in range(2):
                p.op("tensor", "matmul", psC[:, half * 512:(half + 1) * 512],
                     bB[:, cc * 128:(cc + 1) * 128],
                     uat[:, half * 512:(half + 1) * 512],
                     start=True, stop=True, sem=sm)
            mmv = p.cnt[sm]
            p.wait("vector", sm, mmv)
            p.wait("vector", "ldw", wv)
            p.op("vector", "tensor_tensor", p.weff_s[:, cc * 1024:(cc + 1) * 1024],
                 psC[:, :], wt[:, :], op=ALU.add, sem=sv)
            if cc < NCH - 1:
                p.wait("tensor", sv, p.cnt[sv])
                p.wait("sync", sv, p.cnt[sv])
    return (sv, p.cnt[sv])


def emit_ffn0_apply(p, tag, u_dep, weff_dep):
    sv, ss, sm = "vch", "sch", "mm"
    nc = p.nc
    with ExitStack() as st:
        tmp = st.enter_context(nc.sbuf_tensor(tag + "t", [128, 4 * 512], F32))
        pso = [st.enter_context(nc.psum_tensor(tag + "p%d" % i, [128, 512], F32))
               for i in range(4)]
        p.wait("tensor", u_dep[0], u_dep[1])
        p.wait("tensor", weff_dep[0], weff_dep[1])
        p.wait("scalar", weff_dep[0], weff_dep[1])
        slot_act = [None] * 4
        slot_dve = [None] * 4
        nslot = 0
        for oc in range(NCH):
            for tc in range(4):
                slot = nslot % 4
                nslot += 1
                psu = pso[slot]
                if slot_act[slot] is not None:
                    p.wait("tensor", ss, slot_act[slot])
                for cc in range(NCH):
                    p.op("tensor", "matmul", psu[:, :],
                         p.weff_s[:, cc * 1024 + oc * 128: cc * 1024 + (oc + 1) * 128],
                         p.u_s[:, cc * NT + tc * 512: cc * NT + (tc + 1) * 512],
                         start=(cc == 0), stop=(cc == NCH - 1),
                         sem=(sm if cc == NCH - 1 else None))
                mmv = p.cnt[sm]
                p.wait("scalar", sm, mmv)
                if slot_dve[slot] is not None:
                    p.wait("scalar", sv, slot_dve[slot])
                tsl = tmp[:, slot * 512:(slot + 1) * 512]
                p.op("scalar", "activation", tsl, psu[:, :], AF.Identity,
                     bias=p.bffn_s[:, oc:oc + 1], scale=0.5, sem=ss)
                slot_act[slot] = p.cnt[ss]
                p.wait("vector", ss, slot_act[slot])
                xsl = p.x_s[:, oc * NT + tc * 512: oc * NT + (tc + 1) * 512]
                p.op("vector", "tensor_tensor", xsl, xsl, tsl, op=ALU.add,
                     sem=sv)
                slot_dve[slot] = p.cnt[sv]
    return (sv, p.cnt[sv])


def shard_x(x):
    xs = []
    xr = np.ascontiguousarray(x).reshape(B, 32, 128, C)
    for c in range(8):
        v = xr[:, c::8]
        v = v.transpose(3, 0, 1, 2).reshape(C, NT)
        v = v.reshape(NCH, 128, NT).transpose(1, 0, 2).reshape(128, NCH * NT)
        xs.append(np.ascontiguousarray(v))
    return xs


def unshard_x(ys):
    out = np.empty((B, 32, 128, C), np.float32)
    for c in range(8):
        v = ys[c].reshape(128, NCH, NT).transpose(1, 0, 2).reshape(C, NT)
        v = v.reshape(C, B, 4, 128).transpose(1, 2, 3, 0)
        out[:, c::8] = v
    return out.reshape(B, T, C)


def ffn0_host_inputs(W, b, lora_down, lora_up, down_aux, up_aux):
    return {
        "ffnWT": np.ascontiguousarray(W.T),
        "ffnb": np.ascontiguousarray(np.asarray(b, np.float32)
                                     .reshape(NCH, 128).T),
        "ffnldT": np.ascontiguousarray(lora_down.T),
        "ffndax": np.ascontiguousarray(down_aux),
        "ffnluT": np.ascontiguousarray(lora_up.T),
        "ffnuaT": np.ascontiguousarray(up_aux.T),
    }


def moe_host_inputs(inputs):
    return {
        "centT": np.ascontiguousarray(inputs["L1_ffn_centroids"].T),
        "ebias": np.ascontiguousarray(
            np.asarray(inputs["L1_ffn_expert_biases"], np.float32)[None, :]),
        "brT": np.ascontiguousarray(inputs["L1_ffn_br"]),
        "bs2": np.ascontiguousarray(inputs["L1_ffn_bs"]),
        "half2": np.full((2, 1), 0.5, np.float32),
        "wsT": np.ascontiguousarray(inputs["L1_ffn_Ws"].transpose(0, 2, 1)),
        "wrT": np.ascontiguousarray(inputs["L1_ffn_Wr"].transpose(0, 2, 1)),
    }


def build_ffn0_launch():
    p = Prog()
    emit_prelude(p)
    xd = emit_load_x(p)
    wd = emit_ffn0_weff(p)
    ud = emit_ln(p, "ln", [xd, wd])
    done = emit_ffn0_apply(p, "fa", ud, wd)
    emit_store_x(p, [done])
    return p.finalize()


def build_moe_launch():
    p = Prog()
    emit_prelude(p)
    xd = emit_load_x(p)
    wrt_in = p.inp("wrT", [16, 1024, 1024], F32)
    pd = emit_moe_prep(p)
    ud = emit_ln(p, "ln", [xd, pd])
    done = emit_moe_apply(p, "ma", ud, pd, wrt_in)
    emit_store_x(p, [done])
    return p.finalize()


def run_launch(nc, in_maps):
    res = run_bass_kernel_spmd(nc, in_maps, core_ids=list(range(8)))
    return [r["y"] for r in res.results]


_NC_CACHE = {}


def _launch(kind, xs, inputs):
    consts = host_consts()
    if kind == "ffn0":
        if kind not in _NC_CACHE:
            _NC_CACHE[kind] = build_ffn0_launch()
        w = ffn0_host_inputs(
            inputs["L0_ffn_W"], inputs["L0_ffn_b"],
            inputs["L0_ffn_lora_down"], inputs["L0_ffn_lora_up"],
            inputs["L0_ffn_down_aux"], inputs["L0_ffn_up_aux"])
    elif kind == "moe":
        if kind not in _NC_CACHE:
            _NC_CACHE[kind] = build_moe_launch()
        w = moe_host_inputs(inputs)
    elif kind in ("attn0", "attn1"):
        L = int(kind[-1])
        if kind not in _NC_CACHE:
            _NC_CACHE[kind] = build_attn_launch(L)
        in_maps = [dict(consts, x=xs[c], **attn_host_inputs(L, c, inputs))
                   for c in range(8)]
        ys = run_launch(_NC_CACHE[kind], in_maps)
        for c in range(8):
            xs[c] = ys[c]
        return
    else:
        raise KeyError(kind)
    in_maps = [dict(consts, x=xs[c], **w) for c in range(8)]
    ys = run_launch(_NC_CACHE[kind], in_maps)
    for c in range(8):
        xs[c] = ys[c]


def build_fused(n_subs=6):
    p = Prog()
    emit_prelude(p)
    xd = emit_load_x(p)
    wrt_in = p.inp("wrT", [16, 1024, 1024], F32)
    d = xd
    with ExitStack() as s_l0:
        if n_subs >= 1:
            p.alloc_scope = s_l0
            wd = emit_ffn0_weff(p)
            a0p = emit_attn_prep(p, 0, deps=[wd]) if n_subs >= 2 else None
            p.alloc_scope = p.es
            u1 = emit_ln(p, "ln1", [xd, wd] + ([a0p] if a0p else []))
            d = emit_ffn0_apply(p, "f1", u1, wd)
        if n_subs >= 2:
            u2 = emit_ln(p, "ln2", [d])
            d = emit_attn(p, "a0", 0, u2, a0p)
        if n_subs >= 3:
            u3 = emit_ln(p, "ln3", [d])
            d = emit_ffn0_apply(p, "f3", u3, wd)
    with ExitStack() as s_l1:
        if n_subs >= 4:
            p.alloc_scope = s_l1
            mp = emit_moe_prep(p, deps=[d])
            a1p = emit_attn_prep(p, 1, deps=[d, mp]) if n_subs >= 5 else None
            p.alloc_scope = p.es
            u4 = emit_ln(p, "ln4", [d, mp] + ([a1p] if a1p else []))
            d = emit_moe_apply(p, "m4", u4, mp, wrt_in)
        if n_subs >= 5:
            u5 = emit_ln(p, "ln5", [d])
            d = emit_attn(p, "a1", 1, u5, a1p)
        if n_subs >= 6:
            u6 = emit_ln(p, "ln6", [d])
            d = emit_moe_apply(p, "m6", u6, mp, wrt_in)
    emit_store_x(p, [d])
    return p.finalize()


def fused_in_maps(inputs):
    consts = host_consts()
    xs = shard_x(inputs["L0_attn_x"])
    ffn_w = ffn0_host_inputs(
        inputs["L0_ffn_W"], inputs["L0_ffn_b"],
        inputs["L0_ffn_lora_down"], inputs["L0_ffn_lora_up"],
        inputs["L0_ffn_down_aux"], inputs["L0_ffn_up_aux"])
    moe_w = moe_host_inputs(inputs)
    maps = []
    for c in range(8):
        m = dict(consts, x=xs[c], **ffn_w)
        m.update(moe_w)
        m.update(attn_host_inputs(0, c, inputs))
        m.update(attn_host_inputs(1, c, inputs))
        maps.append(m)
    return maps


FUSED = True
CHAIN = ["ffn0", "attn0", "ffn0", "moe", "attn1", "moe"]


def kernel(**inputs):
    inputs = {k: np.asarray(v, np.float32) for k, v in inputs.items()}
    import os
    chain = os.environ.get("MK_CHAIN", "")
    if FUSED and not chain:
        if "fused" not in _NC_CACHE:
            _NC_CACHE["fused"] = build_fused()
        ys = run_launch(_NC_CACHE["fused"], fused_in_maps(inputs))
        return unshard_x(ys)
    xs = shard_x(inputs["L0_attn_x"])
    for kind in (chain.split(",") if chain else CHAIN):
        _launch(kind, xs, inputs)
    return unshard_x(xs)


def emit_moe_prep(p, deps=()):
    nc = p.nc
    tag = "mp"
    sv, ss, sm = "vch", "sch", "mm"
    for s, v in deps:
        for eng in ("sync", "gpsimd", "tensor", "vector", "scalar"):
            p.wait(eng, s, v)
    p.centT_s = p.sb("centT_s", [128, NCH * 16], BF16)
    p.eb_s = p.sb("eb_s", [128, 16], F32)
    p.br_s = p.sb("br_s", [16, 1024], BF16)
    p.bsrow_s = p.sb("bsrow_s", [1, 1024], BF16)
    p.wseff_s = p.sb("wseff_s", [128, NCH * 1024], BF16)

    centT = p.inp("centT", [1024, 16], F32)
    ebias = p.inp("ebias", [1, 16], F32)
    br_in = p.inp("brT", [16, 1024], F32)
    bs_in = p.inp("bs2", [2, 1024], F32)
    half2 = p.inp("half2", [2, 1], F32)
    wst_in = p.inp("wsT", [2, 1024, 1024], F32)
    p.dma("gpsimd", p.centT_s[:, :].rearrange("p (c n) -> p c n", n=16),
          centT.rearrange("(c p) n -> p c n", p=128), "mpw")
    p.dma("gpsimd", p.br_s[:, :], br_in, "mpw")
    with ExitStack() as st:
        ebrow = st.enter_context(nc.sbuf_tensor(tag + "eb", [1, 16], F32))
        bss = st.enter_context(nc.sbuf_tensor(tag + "bs", [2, 1024], F32))
        h2 = st.enter_context(nc.sbuf_tensor(tag + "h2", [2, 1], F32))
        wsa = st.enter_context(nc.sbuf_tensor(tag + "wsa", [128, 1024], F32))
        wsb = st.enter_context(nc.sbuf_tensor(tag + "wsb", [128, 1024], F32))
        pse = st.enter_context(nc.psum_tensor(tag + "pe", [128, 16], F32))
        psb = st.enter_context(nc.psum_tensor(tag + "pb", [1, 1024], F32))
        v0 = p.dma("sync", ebrow[:, :], ebias, "ldw")
        p.dma("sync", bss[:, :], bs_in, "ldw")
        v1 = p.dma("sync", h2[:, :], half2, "ldw")
        p.wait("tensor", "ldw", v1)
        p.wait("tensor", "ldc", p.ldc_v)
        p.op("tensor", "matmul", pse[:, :], p.onesf_s[0:1, :], ebrow[0:1, :],
             start=True, stop=True, sem=sm)
        for half in range(2):
            p.op("tensor", "matmul", psb[:, half * 512:(half + 1) * 512],
                 h2[:, :], bss[:, half * 512:(half + 1) * 512],
                 start=True, stop=True, sem=sm)
        mmv = p.cnt[sm]
        p.wait("scalar", sm, mmv)
        p.op("scalar", "copy", p.eb_s[:, :], pse[:, :], sem=ss)
        p.op("scalar", "copy", p.bsrow_s[:, :], psb[0:1, :], sem=ss)
        for cc in range(NCH):
            va = p.dma("sync", wsa[:, :], wst_in[0, cc * 128:(cc + 1) * 128, :],
                       "mpwa")
            vb = p.dma("sync", wsb[:, :], wst_in[1, cc * 128:(cc + 1) * 128, :],
                       "mpwa")
            p.wait("vector", "mpwa", vb)
            p.op("vector", "tensor_tensor", wsa[:, :], wsa[:, :], wsb[:, :],
                 op=ALU.add)
            p.op("vector", "tensor_scalar",
                 p.wseff_s[:, cc * 1024:(cc + 1) * 1024], wsa[:, :], 0.5, None,
                 op0=ALU.mult, sem=sv)
            if cc < NCH - 1:
                p.wait("sync", sv, p.cnt[sv])
        p.wait("vector", ss, p.cnt[ss])
        p.wait("vector", "mpw", 32)
    return (sv, p.cnt[sv])


def emit_moe_apply(p, tag, u_dep, prep_dep, wrt_in, dbg=()):
    nc = p.nc
    sv, ss, sm, sg = "vch", "sch", "mm", tag + "g"
    with ExitStack() as st:
        g05 = st.enter_context(nc.sbuf_tensor(tag + "g05", [128, 16 * 16], F32))
        gfm = st.enter_context(nc.sbuf_tensor(tag + "gfm", [16, NT], BF16))
        acc = st.enter_context(nc.sbuf_tensor(tag + "acc", [128, 16 * 1024], F32))
        r16 = st.enter_context(nc.sbuf_tensor(tag + "r16", [128, 16], F32))
        r16b = st.enter_context(nc.sbuf_tensor(tag + "r16b", [128, 16], F32))
        rcol = st.enter_context(nc.sbuf_tensor(tag + "rc", [128, 4], F32))
        gbf = st.enter_context(nc.sbuf_tensor(tag + "gbf", [128, 16], BF16))

        for eng in ("tensor", "vector", "scalar", "gpsimd", "sync"):
            p.wait(eng, u_dep[0], u_dep[1])
            p.wait(eng, prep_dep[0], prep_dep[1])

        with ExitStack() as st2:
            psr = st2.enter_context(nc.psum_tensor(tag + "psr", [128, 16], F32))
            psg = st2.enter_context(nc.psum_tensor(tag + "psg", [16, 128], BF16))
            for m in range(16):
                for cc in range(NCH):
                    p.op("tensor", "matmul", psr[:, :],
                         p.u_s[:, cc * NT + m * 128: cc * NT + (m + 1) * 128],
                         p.centT_s[:, cc * 16:(cc + 1) * 16],
                         start=(cc == 0), stop=(cc == NCH - 1),
                         sem=(sm if cc == NCH - 1 else None))
                mmv = p.cnt[sm]
                p.wait("vector", sm, mmv)
                p.op("vector", "tensor_tensor", r16[:, :], psr[:, :],
                     p.eb_s[:, :], op=ALU.add, sem=sv)
                rr_v = p.cnt[sv]
                p.op("vector", "reduce_max", rcol[:, 0:1], r16[:, :], axis=AX.X,
                     negate=True, sem=sv)
                vv = p.cnt[sv]
                p.wait("scalar", sv, vv)
                p.op("scalar", "activation", r16[:, :], r16[:, :], AF.Exp,
                     bias=rcol[:, 0:1], sem=ss)
                ssv = p.cnt[ss]
                p.wait("vector", ss, ssv)
                p.op("vector", "reduce_sum", rcol[:, 1:2], r16[:, :], axis=AX.X)
                p.op("vector", "reciprocal", rcol[:, 2:3], rcol[:, 1:2])
                p.op("vector", "tensor_scalar", r16[:, :], r16[:, :],
                     rcol[:, 2:3], None, op0=ALU.mult)
                p.op("vector", "reduce_max", rcol[:, 0:1], r16[:, :], axis=AX.X,
                     negate=True)
                p.op("vector", "tensor_scalar", r16b[:, :], r16[:, :],
                     rcol[:, 0:1], None, op0=ALU.add)
                p.op("vector", "tensor_scalar", r16b[:, :], r16b[:, :], 0.0,
                     None, op0=ALU.is_equal)
                p.op("vector", "scalar_tensor_tensor", r16b[:, :], r16b[:, :],
                     NEG, r16[:, :], op0=ALU.mult, op1=ALU.add)
                p.op("vector", "reduce_max", rcol[:, 3:4], r16b[:, :], axis=AX.X,
                     negate=True)
                p.op("vector", "tensor_scalar", r16b[:, :], r16[:, :],
                     rcol[:, 3:4], None, op0=ALU.add)
                p.op("vector", "tensor_scalar", r16b[:, :], r16b[:, :], 0.0,
                     None, op0=ALU.is_ge)
                p.op("vector", "tensor_tensor", r16b[:, :], r16b[:, :],
                     r16[:, :], op=ALU.mult)
                p.op("vector", "tensor_scalar", g05[:, m * 16:(m + 1) * 16],
                     r16b[:, :], 0.5, None, op0=ALU.mult)
                p.op("vector", "tensor_copy", gbf[:, :], r16b[:, :], sem=sv)
                gv = p.cnt[sv]
                p.wait("tensor", sv, gv)
                p.op("tensor", "transpose", psg[:, :], gbf[:, :],
                     p.identb_s[:, :], sem=sm)
                tv = p.cnt[sm]
                p.wait("scalar", sm, tv)
                p.op("scalar", "copy", gfm[0:16, m * 128:(m + 1) * 128],
                     psg[:, :], sem=ss)
                p.wait("tensor", ss, p.cnt[ss])
                p.wait("tensor", sv, rr_v)
        router_v = p.cnt[ss]
        if dbg:
            o1 = p.outp("dbg_g05", [128, 16 * 16], F32)
            o2 = p.outp("dbg_gfm", [16, NT], BF16)
            p.wait("sync", ss, router_v)
            p.wait("sync", sv, p.cnt[sv])
            p.dma("sync", o1, g05[:, :], "dbgst")
            v = p.dma("sync", o2, gfm[:, :], "dbgst")
            if dbg == {"router"}:
                p.wait("sync", "dbgst", v)
                return (sv, p.cnt[sv])

        with ExitStack() as st2:
            pssh = [st2.enter_context(
                nc.psum_tensor(tag + "pss%d" % i, [128, 512], F32))
                for i in range(2)]
            slot_ready = [None, None]
            p.wait("tensor", ss, router_v)
            for m in range(16):
                for oh in range(2):
                    slot = (m * 2 + oh) % 2
                    psu = pssh[slot]
                    if slot_ready[slot] is not None:
                        p.wait("tensor", sv, slot_ready[slot])
                    for cc in range(NCH):
                        p.op("tensor", "matmul", psu[:, :],
                             p.u_s[:, cc * NT + m * 128: cc * NT + (m + 1) * 128],
                             p.wseff_s[:, cc * 1024 + oh * 512: cc * 1024 + (oh + 1) * 512],
                             start=(cc == 0), stop=False)
                    p.op("tensor", "matmul", psu[:, :],
                         gfm[0:16, m * 128:(m + 1) * 128],
                         p.br_s[0:16, oh * 512:(oh + 1) * 512],
                         start=False, stop=False)
                    p.op("tensor", "matmul", psu[:, :], p.ones_s[0:1, :],
                         p.bsrow_s[0:1, oh * 512:(oh + 1) * 512],
                         start=False, stop=True, sem=sm)
                    mmv = p.cnt[sm]
                    p.wait("vector", sm, mmv)
                    p.op("vector", "tensor_scalar",
                         acc[:, m * 1024 + oh * 512: m * 1024 + (oh + 1) * 512],
                         psu[:, :], 0.5, None, op0=ALU.mult, sem=sv)
                    slot_ready[slot] = p.cnt[sv]
        shared_v = p.cnt[sv]
        if "shared" in dbg:
            o3 = p.outp("dbg_accS", [128, 16 * 1024], F32)
            p.wait("sync", sv, shared_v)
            v = p.dma("sync", o3, acc[:, :], "dbgst")
            if "experts" not in dbg:
                p.wait("sync", "dbgst", v)
                return (sv, p.cnt[sv])

        with ExitStack() as st2:
            wrbuf = [st2.enter_context(
                nc.sbuf_tensor(tag + "wr%d" % i, [128, NCH * 512], BF16))
                for i in range(2)]
            pse = [st2.enter_context(
                nc.psum_tensor(tag + "pe%d" % i, [128, 512], F32))
                for i in range(8)]
            pairs = [(e, oh) for e in range(16) for oh in range(2)]
            sgs = [sg + "A", sg + "B"]
            buf_ready = {}
            for i in (0, 1):
                e, oh = pairs[i]
                src = wrt_in[e, :, oh * 512:(oh + 1) * 512].rearrange(
                    "(c p) n -> p c n", p=128)
                dst = wrbuf[i % 2][:, :].rearrange("p (c n) -> p c n", n=512)
                buf_ready[i] = p.dma("gpsimd", dst, src, sgs[i % 2])
            bank_free = [None] * 8
            p.wait("tensor", sv, shared_v)
            for i, (e, oh) in enumerate(pairs):
                p.wait("tensor", sgs[i % 2], buf_ready[i])
                wb = wrbuf[i % 2]
                for mg in range(4):
                    half = mg % 2
                    for cc in range(NCH):
                        for mi in range(4):
                            m = mg * 4 + mi
                            bank = half * 4 + mi
                            if cc == 0 and bank_free[bank] is not None:
                                p.wait("tensor", sv, bank_free[bank])
                            p.op("tensor", "matmul", pse[bank][:, :],
                                 p.u_s[:, cc * NT + m * 128: cc * NT + (m + 1) * 128],
                                 wb[:, cc * 512:(cc + 1) * 512],
                                 start=(cc == 0), stop=(cc == NCH - 1),
                                 sem=(sm if (cc == NCH - 1 and mi == 3) else None))
                    mmv = p.cnt[sm]
                    p.wait("vector", sm, mmv)
                    for mi in range(4):
                        m = mg * 4 + mi
                        bank = half * 4 + mi
                        asl = acc[:, m * 1024 + oh * 512: m * 1024 + (oh + 1) * 512]
                        p.op("vector", "scalar_tensor_tensor", asl,
                             pse[bank][:, :], g05[:, m * 16 + e: m * 16 + e + 1],
                             asl, op0=ALU.mult, op1=ALU.add, sem=sv)
                        bank_free[bank] = p.cnt[sv]
                if i + 2 < len(pairs):
                    e2, oh2 = pairs[i + 2]
                    p.wait("gpsimd", sm, p.cnt[sm])
                    src = wrt_in[e2, :, oh2 * 512:(oh2 + 1) * 512].rearrange(
                        "(c p) n -> p c n", p=128)
                    dst = wrbuf[i % 2][:, :].rearrange("p (c n) -> p c n", n=512)
                    buf_ready[i + 2] = p.dma("gpsimd", dst, src, sgs[i % 2])
        expert_v = p.cnt[sv]
        if "experts" in dbg:
            o4 = p.outp("dbg_accE", [128, 16 * 1024], F32)
            p.wait("sync", sv, expert_v)
            v = p.dma("sync", o4, acc[:, :], "dbgst")
            p.wait("sync", "dbgst", v)
            return (sv, p.cnt[sv])

        for cc in range(NCH):
            xc = p.x_s[:, cc * NT:(cc + 1) * NT]
            p.op("vector", "scalar_tensor_tensor", xc,
                 p.u_s[:, cc * NT:(cc + 1) * NT], 0.5, xc,
                 op0=ALU.mult, op1=ALU.add, sem=sv)

        with ExitStack() as st2:
            pst = [st2.enter_context(
                nc.psum_tensor(tag + "pt%d" % i, [128, 512], F32))
                for i in range(2)]
            grp_free = [None, None]
            p.wait("tensor", sv, expert_v)
            for m in range(16):
                for og in range(2):
                    slot = (m * 2 + og) % 2
                    if grp_free[slot] is not None:
                        p.wait("tensor", sv, grp_free[slot])
                    for oi in range(4):
                        oc = og * 4 + oi
                        p.op("tensor", "transpose",
                             pst[slot][:, oi * 128:(oi + 1) * 128],
                             acc[:, m * 1024 + oc * 128: m * 1024 + (oc + 1) * 128],
                             p.identf_s[:, :],
                             sem=(sm if oi == 3 else None))
                    mmv = p.cnt[sm]
                    p.wait("vector", sm, mmv)
                    for oi in range(4):
                        oc = og * 4 + oi
                        xsl = p.x_s[:, oc * NT + m * 128: oc * NT + (m + 1) * 128]
                        p.op("vector", "tensor_tensor", xsl, xsl,
                             pst[slot][:, oi * 128:(oi + 1) * 128], op=ALU.add,
                             sem=sv)
                    grp_free[slot] = p.cnt[sv]
    return (sv, p.cnt[sv])


def emit_attn_prep(p, L, deps=()):
    nc = p.nc
    tag = "ap%d" % L
    sv, ss, sm = "vch", "sch", "mm"
    for s, v in deps:
        for eng in ("sync", "gpsimd", "tensor", "vector", "scalar"):
            p.wait(eng, s, v)
    if not hasattr(p, "kc_d"):
        p.kc_d = [p.nc.dram_tensor("kc%d" % g, [256, NT], BF16) for g in range(4)]
        p.vc_d = [p.nc.dram_tensor("vc%d" % g, [NT, 256], BF16) for g in range(4)]
        p.gk_d = [p.nc.dram_tensor("gk%d" % g, [8 * 256, NT], BF16) for g in range(4)]
        p.gv_d = [p.nc.dram_tensor("gv%d" % g, [8 * NT, 256], BF16) for g in range(4)]
        mk = p.inp("maskt", [128, 8 * 128], BF16)
        p.mask_v = p.dma("sync", p.mask_s[:, :], mk, "ldc")
    if L == 0:
        p.cos_s = p.sb("cos_s", [128, NT], BF16)
        p.sin_s = p.sb("sin_s", [128, NT], BF16)
        p.perm_s = p.sb("perm_s", [128, 128], BF16)
        p.dma("sync", p.cos_s[:, :], p.inp("cos_t", [128, NT], BF16), "ldc")
        p.dma("sync", p.sin_s[:, :], p.inp("sin_t", [128, NT], BF16), "ldc")
        v = p.dma("sync", p.perm_s[:, :], p.inp("perm", [128, 128], BF16), "ldc")
        return ("ldc", v)
    p.m2k_s = p.sb("m2k_s", [128, 128], BF16)
    p.m2v_s = p.sb("m2v_s", [128, 128], BF16)
    with ExitStack() as st:
        wkc = st.enter_context(nc.sbuf_tensor(tag + "wkc", [32, 64], F32))
        wke = st.enter_context(nc.sbuf_tensor(tag + "wke", [32, 64], F32))
        wvc = st.enter_context(nc.sbuf_tensor(tag + "wvc", [32, 64], F32))
        wve = st.enter_context(nc.sbuf_tensor(tag + "wve", [32, 64], F32))
        psk = st.enter_context(nc.psum_tensor(tag + "psk", [64, 64], F32))
        psv = st.enter_context(nc.psum_tensor(tag + "psv", [64, 64], F32))
        p.dma("sync", wkc[:, :], p.inp("wkcT", [32, 64], F32), "ldw")
        p.dma("sync", wke[:, :], p.inp("wke", [32, 64], F32), "ldw")
        p.dma("sync", wvc[:, :], p.inp("wvcT", [32, 64], F32), "ldw")
        v0 = p.dma("sync", wve[:, :], p.inp("wve", [32, 64], F32), "ldw")
        mz = p.op("gpsimd", "memset", p.m2k_s[:, :], 0.0)
        mz = p.op("gpsimd", "memset", p.m2v_s[:, :], 0.0)
        p.wait("tensor", "ldw", v0)
        p.op("tensor", "matmul", psk[:, :], wkc[:, :], wke[:, :],
             start=True, stop=True, sem=sm)
        p.op("tensor", "matmul", psv[:, :], wvc[:, :], wve[:, :],
             start=True, stop=True, sem=sm)
        mmv = p.cnt[sm]
        p.wait("scalar", sm, mmv)
        p.wait("scalar", "gch", mz)
        p.op("scalar", "copy", p.m2k_s[0:64, 0:64], psk[:, :], sem=ss)
        p.op("scalar", "copy", p.m2k_s[64:128, 64:128], psk[:, :], sem=ss)
        p.op("scalar", "copy", p.m2v_s[0:64, 0:64], psv[:, :], sem=ss)
        p.op("scalar", "copy", p.m2v_s[64:128, 64:128], psv[:, :], sem=ss)
    return (ss, p.cnt[ss])


def emit_attn(p, tag, L, u_dep, prep_dep):
    nc = p.nc
    sv, ss, sm = "vch", "sch", "mm"
    scc = tag + "c"
    W = {}
    for nm in ("Wq", "Wk", "Wv", "Wp"):
        W[nm] = p.inp(tag + nm, [1024, 1024], F32)
    bq = p.inp(tag + "bq", [128, 8], F32)
    bk = p.inp(tag + "bk", [128, 8], F32)
    bp = p.inp(tag + "bp", [128, 8], F32)
    bvr = p.inp(tag + "bvr", [1, 1024], F32)

    for eng in ("tensor", "vector", "scalar", "gpsimd", "sync"):
        p.wait(eng, u_dep[0], u_dep[1])
        p.wait(eng, prep_dep[0], prep_dep[1])
        p.wait(eng, "ldc", p.mask_v)

    with ExitStack() as st:
        sb = lambda n, sh, dt: st.enter_context(nc.sbuf_tensor(tag + "_" + n, sh, dt))
        q_s = sb("q", [128, NCH * NT], BF16)
        bq_s = sb("bq", [128, 8], F32)
        bk_s = sb("bk", [128, 8], F32)
        bp_s = sb("bp", [128, 8], F32)
        bvr_s = sb("bvr", [1, 1024], BF16)
        vstg = sb("vstg", [128, 512], BF16)

        p.dma("sync", bq_s[:, :], bq, "ldw")
        p.dma("sync", bk_s[:, :], bk, "ldw")
        p.dma("sync", bp_s[:, :], bp, "ldw")
        bv_v = p.dma("gpsimd", bvr_s[:, :], bvr, tag + "bv")

        with ExitStack() as st2:
            psqk = [st2.enter_context(nc.psum_tensor(tag + "pq%d" % i,
                                                     [128, 512], F32))
                    for i in range(2)]
            psw = st2.enter_context(nc.psum_tensor(tag + "psw", [128, 512], F32))
            wsl = [st2.enter_context(nc.sbuf_tensor(tag + "_wsl%d" % i,
                                                    [128, 8 * 128], BF16))
                   for i in range(2)]
            kraw = st2.enter_context(nc.sbuf_tensor(tag + "_kraw", [128, 512], BF16))
            ksw = st2.enter_context(nc.sbuf_tensor(tag + "_ksw", [128, 512], BF16))
            t1 = st2.enter_context(nc.sbuf_tensor(tag + "_t1", [128, 512], BF16))
            wq_jobs = [("Wq", oc) for oc in range(NCH)] + \
                      [("Wk", oc) for oc in range(NCH)]
            wsem = [tag + "wA", tag + "wB"]
            wrdy = {}
            for j in (0, 1):
                nm, oc = wq_jobs[j]
                src = W[nm][:, oc * 128:(oc + 1) * 128].rearrange(
                    "(c p) n -> p c n", p=128)
                wrdy[j] = p.dma("gpsimd", wsl[j % 2][:, :].rearrange(
                    "p (c n) -> p c n", n=128), src, wsem[j % 2])
            slot_rd = [None, None]
            for j, (nm, oc) in enumerate(wq_jobs):
                p.wait("tensor", wsem[j % 2], wrdy[j])
                wb = wsl[j % 2]
                for tc in range(4):
                    slot = tc % 2
                    if slot_rd[slot] is not None:
                        p.wait("tensor", ss, slot_rd[slot])
                    for cc in range(NCH):
                        p.op("tensor", "matmul", psqk[slot][:, :],
                             wb[:, cc * 128:(cc + 1) * 128],
                             p.u_s[:, cc * NT + tc * 512: cc * NT + (tc + 1) * 512],
                             start=(cc == 0), stop=(cc == NCH - 1),
                             sem=(sm if cc == NCH - 1 else None))
                    mmv = p.cnt[sm]
                    p.wait("scalar", sm, mmv)
                    isq = (nm == "Wq")
                    bias = (bq_s if isq else bk_s)[:, oc:oc + 1]
                    if L == 0:
                        p.op("scalar", "activation", kraw[:, :],
                             psqk[slot][:, :], AF.Identity, bias=bias, sem=ss)
                        kv = p.cnt[ss]
                        slot_rd[slot] = kv
                        p.wait("tensor", ss, kv)
                        p.op("tensor", "matmul", psw[:, :], p.perm_s[:, :],
                             kraw[:, :], start=True, stop=True, sem=sm)
                        swv = p.cnt[sm]
                        p.wait("scalar", sm, swv)
                        p.op("scalar", "copy", ksw[:, :], psw[:, :], sem=ss)
                        swc = p.cnt[ss]
                        cs = slice(tc * 512, (tc + 1) * 512)
                        p.wait("vector", ss, kv)
                        p.op("vector", "tensor_tensor", t1[:, :], kraw[:, :],
                             p.cos_s[:, cs], op=ALU.mult)
                        p.wait("vector", ss, swc)
                        p.op("vector", "tensor_tensor", ksw[:, :], ksw[:, :],
                             p.sin_s[:, cs], op=ALU.mult)
                        dst = (q_s[:, oc * NT + tc * 512: oc * NT + (tc + 1) * 512]
                               if isq else kraw[:, :])
                        rv = p.op("vector", "tensor_tensor", dst, t1[:, :],
                                  ksw[:, :], op=ALU.add)
                        p.wait("scalar", sv, rv)
                        if not isq:
                            hg, half = oc // 2, oc % 2
                            p.wait("sync", sv, rv)
                            p.dma("sync",
                                  p.kc_d[hg].ap()[half * 128:(half + 1) * 128,
                                                  tc * 512:(tc + 1) * 512],
                                  kraw[:, :], tag + "kv%d" % hg)
                            p.wait("scalar", tag + "kv%d" % hg,
                                   p.cnt[tag + "kv%d" % hg])
                    else:
                        if isq:
                            p.op("scalar", "activation",
                                 q_s[:, oc * NT + tc * 512: oc * NT + (tc + 1) * 512],
                                 psqk[slot][:, :], AF.Identity, bias=bias,
                                 sem=ss)
                            slot_rd[slot] = p.cnt[ss]
                        else:
                            p.op("scalar", "activation", kraw[:, :],
                                 psqk[slot][:, :], AF.Identity, bias=bias,
                                 sem=ss)
                            kv = p.cnt[ss]
                            slot_rd[slot] = kv
                            p.wait("tensor", ss, kv)
                            p.op("tensor", "matmul", psw[:, :], p.m2k_s[:, :],
                                 kraw[:, :], start=True, stop=True, sem=sm)
                            swv = p.cnt[sm]
                            p.wait("scalar", sm, swv)
                            p.op("scalar", "copy", ksw[:, :], psw[:, :],
                                 sem=ss)
                            kxv = p.cnt[ss]
                            hg, half = oc // 2, oc % 2
                            p.wait("sync", ss, kxv)
                            p.dma("sync",
                                  p.kc_d[hg].ap()[half * 128:(half + 1) * 128,
                                                  tc * 512:(tc + 1) * 512],
                                  ksw[:, :], tag + "kv%d" % hg)
                            p.wait("scalar", tag + "kv%d" % hg,
                                   p.cnt[tag + "kv%d" % hg])
                if j + 2 < len(wq_jobs):
                    nm2, oc2 = wq_jobs[j + 2]
                    p.wait("gpsimd", sm, p.cnt[sm])
                    src = W[nm2][:, oc2 * 128:(oc2 + 1) * 128].rearrange(
                        "(c p) n -> p c n", p=128)
                    wrdy[j + 2] = p.dma("gpsimd", wsl[j % 2][:, :].rearrange(
                        "p (c n) -> p c n", n=128), src, wsem[j % 2])

        with ExitStack() as st2:
            psv = [st2.enter_context(nc.psum_tensor(tag + "pv%d" % i,
                                                    [128, 512], F32))
                   for i in range(2)]
            wvh = [st2.enter_context(nc.sbuf_tensor(tag + "_wvh%d" % i,
                                                    [128, 8 * 512], BF16))
                   for i in range(2)]
            vsem = [tag + "vA", tag + "vB"]
            vrdy = {}
            p.wait("gpsimd", ss, p.cnt[ss])
            p.wait("gpsimd", sv, p.cnt[sv])
            p.wait("gpsimd", sm, p.cnt[sm])
            for g in range(4):
                if (tag + "kv%d" % g) in p.cnt:
                    p.wait("gpsimd", tag + "kv%d" % g, p.cnt[tag + "kv%d" % g])
            for j in (0, 1):
                src = W["Wv"][:, j * 512:(j + 1) * 512].rearrange(
                    "(c p) n -> p c n", p=128)
                vrdy[j] = p.dma("gpsimd", wvh[j][:, :].rearrange(
                    "p (c n) -> p c n", n=512), src, vsem[j])
            p.wait("tensor", tag + "bv", bv_v)
            p.wait("tensor", ss, p.cnt[ss])
            p.wait("tensor", sv, p.cnt[sv])
            for g in range(4):
                if (tag + "kv%d" % g) in p.cnt:
                    p.wait("tensor", tag + "kv%d" % g, p.cnt[tag + "kv%d" % g])
            slot_rd = [None, None]
            for oh in range(2):
                p.wait("tensor", vsem[oh], vrdy[oh])
                for m in range(16):
                    slot = m % 2
                    if slot_rd[slot] is not None:
                        p.wait("tensor", ss, slot_rd[slot])
                    for cc in range(NCH):
                        p.op("tensor", "matmul", psv[slot][:, :],
                             p.u_s[:, cc * NT + m * 128: cc * NT + (m + 1) * 128],
                             wvh[oh][:, cc * 512:(cc + 1) * 512],
                             start=(cc == 0), stop=False)
                    p.op("tensor", "matmul", psv[slot][:, :],
                         p.ones_s[0:1, :],
                         bvr_s[0:1, oh * 512:(oh + 1) * 512],
                         start=False, stop=True, sem=sm)
                    mmv = p.cnt[sm]
                    p.wait("scalar", sm, mmv)
                    p.op("scalar", "copy", vstg[:, :], psv[slot][:, :], sem=ss)
                    vv = p.cnt[ss]
                    slot_rd[slot] = vv
                    p.wait("sync", ss, vv)
                    for hh in range(2):
                        hg = oh * 2 + hh
                        p.dma("sync",
                              p.vc_d[hg].ap()[m * 128:(m + 1) * 128, :],
                              vstg[:, hh * 256:(hh + 1) * 256],
                              tag + "kv%d" % hg)
                    for hh in range(2):
                        hgx = oh * 2 + hh
                        p.wait("scalar", tag + "kv%d" % hgx,
                               p.cnt[tag + "kv%d" % hgx])

        for hg in range(4):
            p.wait("gpsimd", tag + "kv%d" % hg, p.cnt[tag + "kv%d" % hg])
            p.op("gpsimd", "collective_compute",
                 "AllGather", ALU.bypass,
                 replica_groups=[list(range(8))],
                 ins=[p.kc_d[hg].ap().opt()], outs=[p.gk_d[hg].ap().opt()],
                 sem=scc)
            p.op("gpsimd", "collective_compute",
                 "AllGather", ALU.bypass,
                 replica_groups=[list(range(8))],
                 ins=[p.vc_d[hg].ap().opt()], outs=[p.gv_d[hg].ap().opt()],
                 sem=scc)
        cc_base = p.cnt[scc] - 8

        khg = [sb("khg0", [128, 2 * T], BF16)]
        vhg = [sb("vhg0", [128, 4 * 32 * 65], BF16)]
        ohgb = sb("ohgb", [128, 2 * 512], BF16)
        ohgb_raw = ohgb
        sig_s = sb("sig", [1, 2048], BF16)
        rec_row = sb("rcr", [1, 512], F32)
        pbuf = [sb("pb%d" % i, [128, 1024], BF16) for i in range(2)]
        wp_s = sb("wp", [128, 2 * 1024], BF16)
        p.wait("gpsimd", ss, p.cnt[ss])
        p.wait("gpsimd", sm, p.cnt[sm])
        mz1 = p.op("gpsimd", "memset", vhg[0][:, :], 1.0)
        p.wait("sync", "gch", mz1)

        with ExitStack() as st2:
            psS = [st2.enter_context(nc.psum_tensor(tag + "pS%d" % i,
                                                    [128, 1024], F32))
                   for i in range(2)]
            psO = [st2.enter_context(nc.psum_tensor(tag + "pO%d" % i,
                                                    [65, 512], F32))
                   for i in range(2)]
            psP = [st2.enter_context(nc.psum_tensor(tag + "pP0",
                                                    [128, 512], F32))]
            stg = [tag + "sA", tag + "sB"]
            wpsem = [tag + "pA2", tag + "pB2"]

            def stage(i, hg, b):
                sem = stg[0]
                gk = p.gk_d[hg].ap()
                gv = p.gv_d[hg].ap()
                n0 = p.cnt.get(sem, 0)
                for r in range(8):
                    for hp in range(2):
                        src = gk[256 * r + 128 * hp: 256 * r + 128 * (hp + 1),
                                 b * 512:(b + 1) * 512]
                        src = src.rearrange("p (t n) -> p t n", n=128)
                        dst = khg[0][:, hp * T: (hp + 1) * T].rearrange(
                            "p (t n) -> p t n", n=128)[:, r::8, :]
                        p.dma("sync", dst, src, sem)
                    srcv = gv[2048 * r + 512 * b: 2048 * r + 512 * (b + 1), :]
                    srcv = srcv.rearrange("(t p) (h d) -> p h t d", p=128, d=64)
                    dstv = vhg[0][:, :].rearrange(
                        "p (h k e) -> p h k e", h=4, e=65)[:, :, r::8, 0:64]
                    p.dma("sync", dstv, srcv, sem)
                return p.cnt[sem]

            def wp_load(hg):
                src = W["Wp"][2 * hg * 128:(2 * hg + 2) * 128, :].rearrange(
                    "(c p) n -> p c n", p=128)
                return p.dma("gpsimd", wp_s[:, :].rearrange(
                    "p (c n) -> p c n", n=1024), src, wpsem[0])

            hgb = [(hg, b) for hg in range(4) for b in range(B)]
            stg_rdy = {}
            wp_rdy = {}
            reg_rd = [0, 0]
            reg_av = [0, 0]
            po_free = [0, 0]
            psS_guard = 0
            wp_rdy[0] = wp_load(0)
            last_proj_dve = None
            pe_prev = 0
            for i, (hg, b) in enumerate(hgb):
                if i == 0:
                    p.wait("sync", scc, cc_base + 2)
                    stg_rdy[0] = stage(0, hg, b)
                p.wait("tensor", stg[0], stg_rdy[i])
                if b == 0:
                    p.wait("tensor", wpsem[0], wp_rdy[hg])
                kb, vb = khg[0], vhg[0]
                wpb = wp_s
                def tile_regions():
                    regs, cur, off = [], [], 0
                    for kt in range(32):
                        N = 512 - 128 * (kt // 8)
                        if off % 512 + N > 512 and off % 512 != 0:
                            off = (off // 512 + 1) * 512
                        if off + N > 1024:
                            regs.append(cur)
                            cur, off = [], 0
                        cur.append((kt, off, N))
                        off += N
                    regs.append(cur)
                    return regs
                TREG = tile_regions()
                R = [(hq, ri) for hq in range(4) for ri in range(len(TREG))]
                score_mm, exp_rd, msk_rd = {}, {}, {}

                def emit_scores(j):
                    hq, ri = R[j]
                    reg = j % 2
                    hb = (hq % 2) * 64
                    hp = hq // 2
                    qbase = (hg * 2 + hp) * NT + b * 512
                    if reg_rd[reg]:
                        p.wait("tensor", ss, reg_rd[reg])
                    if reg == 0 and psS_guard:
                        p.wait("tensor", sv, psS_guard)
                    for (kt, off, N) in TREG[ri]:
                        q0 = 128 * (kt // 8)
                        p.op("tensor", "matmul",
                             psS[reg][:, off:off + N],
                             kb[hb:hb + 64,
                                hp * T + kt * 128: hp * T + (kt + 1) * 128],
                             q_s[hb:hb + 64, qbase + q0: qbase + 512],
                             start=(off % 512 == 0),
                             stop=((kt, off, N) == TREG[ri][-1]),
                             sem=(sm if (kt, off, N) == TREG[ri][-1] else None))
                    score_mm[j] = p.cnt[sm]

                def emit_exp(j):
                    hq, ri = R[j]
                    reg = j % 2
                    p.wait("scalar", sm, score_mm[j])
                    if reg_av[reg]:
                        p.wait("scalar", sm, reg_av[reg])
                    p.op("scalar", "activation", pbuf[reg][:, :],
                         psS[reg][:, :], AF.Exp,
                         bias=p.cst_s[:, 0:1], scale=SCALE, sem=ss)
                    reg_rd[reg] = p.cnt[ss]
                    exp_rd[j] = reg_rd[reg]
                    p.wait("vector", ss, exp_rd[j])
                    for (kt, off, N) in TREG[ri]:
                        mv2 = p.op("vector", "tensor_tensor",
                                   pbuf[reg][:, off:off + 128],
                                   pbuf[reg][:, off:off + 128],
                                   p.mask_s[:, (kt % 8) * 128:(kt % 8 + 1) * 128],
                                   op=ALU.mult)
                    msk_rd[j] = mv2

                def emit_av(j):
                    hq, ri = R[j]
                    reg = j % 2
                    po = psO[hq % 2]
                    p.wait("tensor", ss, exp_rd[j])
                    p.wait("tensor", sv, msk_rd[j])
                    if ri == 0 and po_free[hq % 2]:
                        p.wait("tensor", ss, po_free[hq % 2])
                    for (kt, off, N) in TREG[ri]:
                        q0 = 128 * (kt // 8)
                        p.op("tensor", "matmul", po[:, q0:512],
                             vb[:, (hq * 32 + kt) * 65:
                                (hq * 32 + kt) * 65 + 65],
                             pbuf[reg][:, off:off + N],
                             start=(kt == 0),
                             stop=(ri == len(TREG) - 1 and
                                   (kt, off, N) == TREG[ri][-1]),
                             sem=(sm if (kt, off, N) == TREG[ri][-1] else None))
                    reg_av[reg] = p.cnt[sm]
                    if ri == len(TREG) - 1:
                        hb = (hq % 2) * 64
                        hp = hq // 2
                        p.wait("scalar", sm, reg_av[reg])
                        p.op("scalar", "copy",
                             ohgb_raw[hb:hb + 64, hp * 512:(hp + 1) * 512],
                             po[0:64, 0:512])
                        p.op("scalar", "copy",
                             sig_s[0:1, hq * 512:(hq + 1) * 512],
                             po[64:65, 0:512], sem=ss)
                        po_free[hq % 2] = p.cnt[ss]

                emit_scores(0)
                emit_exp(0)
                if len(R) > 1:
                    emit_scores(1)
                    emit_exp(1)
                for j in range(len(R)):
                    emit_av(j)
                    if j + 2 < len(R):
                        emit_scores(j + 2)
                        emit_exp(j + 2)
                av_end = p.cnt[sm]

                if i + 1 < len(hgb):
                    hg2, b2 = hgb[i + 1]
                    p.wait("sync", scc, cc_base + 2 * (hg2 + 1))
                    p.wait("sync", sm, av_end)
                    stg_rdy[i + 1] = stage(i + 1, hg2, b2)

                sig_done = p.cnt[ss]
                p.wait("vector", ss, sig_done)
                p.wait("tensor", ss, max(reg_rd))
                if psS_guard:
                    p.wait("tensor", sv, psS_guard)
                p.wait("tensor", sv, p.cnt[sv])
                nv = 0
                for hq in range(4):
                    hb = (hq % 2) * 64
                    hp = hq // 2
                    rcp = p.op("vector", "reciprocal", rec_row[:, :],
                               sig_s[0:1, hq * 512:(hq + 1) * 512])
                    p.wait("tensor", sv, rcp)
                    if nv:
                        p.wait("tensor", sv, nv)
                    p.op("tensor", "matmul",
                         psP[0][hb:hb + 64, 0:512],
                         p.onesf_s[0:1, 0:64],
                         rec_row[0:1, :],
                         start=True, stop=True, sem=sm)
                    bcv = p.cnt[sm]
                    p.wait("vector", sm, bcv)
                    nv = p.op("vector", "tensor_tensor",
                              ohgb[hb:hb + 64, hp * 512:(hp + 1) * 512],
                              ohgb_raw[hb:hb + 64, hp * 512:(hp + 1) * 512],
                              psP[0][hb:hb + 64, 0:512], op=ALU.mult)
                psS_guard = 0
                if L == 1:
                    p.wait("tensor", sv, nv)
                    for half in range(2):
                        p.op("tensor", "matmul",
                             psS[1][:, half * 512:(half + 1) * 512],
                             p.m2v_s[:, :], ohgb[:, half * 512:(half + 1) * 512],
                             start=True, stop=True,
                             sem=(sm if half == 1 else None))
                    mvv = p.cnt[sm]
                    p.wait("scalar", sm, mvv)
                    p.op("scalar", "copy", ohgb[:, :], psS[1][:, :], sem=ss)
                reg_rd = [p.cnt[ss], p.cnt[ss]]
                p.wait("tensor", sv, p.cnt[sv])
                p.wait("tensor", ss, p.cnt[ss])
                for oc in range(NCH):
                    pp = psP[0]
                    for hp in range(2):
                        p.op("tensor", "matmul", pp[:, :],
                             wpb[:, hp * 1024 + oc * 128: hp * 1024 + (oc + 1) * 128],
                             ohgb[:, hp * 512:(hp + 1) * 512],
                             start=(hp == 0), stop=(hp == 1),
                             sem=(sm if hp == 1 else None))
                    mmv = p.cnt[sm]
                    xsl = p.x_s[:, oc * NT + b * 512: oc * NT + (b + 1) * 512]
                    if hg == 0:
                        p.wait("scalar", sm, mmv)
                        p.op("scalar", "activation", vstg[:, :], pp[:, :],
                             AF.Identity, bias=bp_s[:, oc:oc + 1], sem=ss)
                        prv = p.cnt[ss]
                        p.wait("vector", ss, prv)
                        last_proj_dve = p.op("vector", "tensor_tensor", xsl,
                                             xsl, vstg[:, :], op=ALU.add)
                        p.wait("scalar", sv, last_proj_dve)
                    else:
                        p.wait("vector", sm, mmv)
                        last_proj_dve = p.op("vector", "tensor_tensor", xsl,
                                             xsl, pp[:, :], op=ALU.add)
                    p.wait("tensor", sv, last_proj_dve)
                if b == B - 1 and hg + 1 < 4:
                    p.wait("gpsimd", sm, p.cnt[sm])
                    wp_rdy[hg + 1] = wp_load(hg + 1)
    return (sv, p.cnt[sv])


def bias_cols(b):
    return np.ascontiguousarray(np.asarray(b, np.float32).reshape(8, 128).T)


def attn_host_inputs(L, core, inputs):
    tag = "a%d" % L
    d = {}
    if L == 0:
        d[tag + "Wq"] = np.ascontiguousarray(inputs["L0_attn_Wq"])
        d[tag + "Wk"] = np.ascontiguousarray(inputs["L0_attn_Wk"])
        d[tag + "Wv"] = np.ascontiguousarray(inputs["L0_attn_Wv"])
        d[tag + "Wp"] = np.ascontiguousarray(inputs["L0_attn_Wp"])
        d[tag + "bq"] = bias_cols(inputs["L0_attn_bq"])
        d[tag + "bk"] = bias_cols(inputs["L0_attn_bk"])
        d[tag + "bp"] = bias_cols(inputs["L0_attn_bp"])
        d[tag + "bvr"] = np.asarray(inputs["L0_attn_bv"], np.float32)[None, :]
        pk = ((8 * np.arange(4)[:, None] + core) * 128 +
              np.arange(128)[None, :]).reshape(-1).astype(np.float32)
        pos = np.tile(pk, 4)
        dd = np.arange(128) % 64
        inv = (10000.0 ** (-2.0 * (dd // 2) / 64.0)).astype(np.float32)
        ang = pos[None, :] * inv[:, None]
        sign = np.where(dd % 2 == 0, -1.0, 1.0).astype(np.float32)
        d["cos_t"] = np.cos(ang).astype(BF)
        d["sin_t"] = (sign[:, None] * np.sin(ang)).astype(BF)
        pm = np.zeros((128, 128), np.float32)
        pm[np.arange(128), np.arange(128) ^ 1] = 1.0
        d["perm"] = pm.astype(BF)
    else:
        w = np.asarray(inputs["L1_attn_c_attn_w"], np.float32)
        bb = np.asarray(inputs["L1_attn_c_attn_b"], np.float32)
        d[tag + "Wq"] = np.ascontiguousarray(w[:, 0:1024])
        d[tag + "Wk"] = np.ascontiguousarray(w[:, 1024:2048])
        d[tag + "Wv"] = np.ascontiguousarray(w[:, 2048:3072])
        d[tag + "Wp"] = np.ascontiguousarray(inputs["L1_attn_c_proj_w"])
        d[tag + "bq"] = bias_cols(bb[0:1024])
        d[tag + "bk"] = bias_cols(bb[1024:2048])
        d[tag + "bp"] = bias_cols(inputs["L1_attn_c_proj_b"])
        d[tag + "bvr"] = bb[2048:3072][None, :].copy()
        d["wkcT"] = np.ascontiguousarray(inputs["L1_attn_wk_c"].T)
        d["wke"] = np.ascontiguousarray(inputs["L1_attn_wk_e"])
        d["wvcT"] = np.ascontiguousarray(inputs["L1_attn_wv_c"].T)
        d["wve"] = np.ascontiguousarray(inputs["L1_attn_wv_e"])
    ki = np.arange(128)[:, None]
    qi = np.arange(128)[None, :]
    mk = np.ones((128, 8 * 128), np.float32)
    for cc in range(8):
        if cc == core:
            mk[:, cc * 128:(cc + 1) * 128] = np.where(ki <= qi, 1.0, 0.0)
        elif cc > core:
            mk[:, cc * 128:(cc + 1) * 128] = 0.0
    d["maskt"] = mk.astype(BF)
    return d


def build_attn_launch(L):
    p = Prog()
    emit_prelude(p)
    xd = emit_load_x(p)
    pd = emit_attn_prep(p, L)
    ud = emit_ln(p, "ln", [xd, pd])
    done = emit_attn(p, "a%d" % L, L, ud, pd)
    emit_store_x(p, [done])
    return p.finalize()


# version 06
# speedup vs baseline: 1.0922x; 1.0781x over previous
import numpy as np
from contextlib import ExitStack
import concourse.bass as bass
import concourse.mybir as mybir
from concourse.bass_utils import run_bass_kernel_spmd
import ml_dtypes

F32 = mybir.dt.float32
BF16 = mybir.dt.bfloat16
AF = mybir.ActivationFunctionType
ALU = mybir.AluOpType
AX = mybir.AxisListType
BF = ml_dtypes.bfloat16

C = 1024
B = 4
T = 4096
NT = 2048
NCH = 8
NEG = -1e33
EXPB = -20.0
SCALE = 0.125


class Prog:

    ENGINES = ("sync", "gpsimd", "vector", "scalar", "tensor")

    def __init__(self):
        self.nc = bass.Bass("TRN2", target_bir_lowering=False, debug=False)
        self.es = ExitStack()
        self.alloc_scope = self.es
        self.q = {e: [] for e in self.ENGINES}
        self.cnt = {}
        self.sems = {}

    def sem(self, name):
        if name not in self.sems:
            self.sems[name] = self.es.enter_context(self.nc.semaphore(name))
            self.cnt[name] = 0
        return self.sems[name]

    def inc(self, name, k=1):
        self.sem(name)
        self.cnt[name] += k
        return self.sems[name], self.cnt[name]

    def emit(self, eng, fn):
        self.q[eng].append(fn)

    def wait(self, eng, sem_name, v):
        s = self.sem(sem_name)
        self.emit(eng, lambda e, s=s, v=v: e.wait_ge(s, v))

    CHAINED = {"vector": "vch", "scalar": "sch", "gpsimd": "gch"}

    def op(self, eng, meth, *args, sem=None, inc=1, **kw):
        ch = self.CHAINED.get(eng)
        if ch is not None and meth not in ("dma_start", "collective_compute"):
            assert sem is None or sem == ch, (eng, meth, sem)
            s = self.sem(ch)
            prev = self.cnt[ch]
            self.cnt[ch] += 1

            lp = kw.pop("lp", False)

            def fn(e, a=args, k=kw, m=meth, s=s, prev=prev, lp=lp):
                if prev > 0:
                    e.wait_ge(s, prev)
                if lp:
                    with self.nc.allow_low_precision(reason="bf16 aux math"):
                        getattr(e, m)(*a, **k).then_inc(s, 1)
                else:
                    getattr(e, m)(*a, **k).then_inc(s, 1)
            self.emit(eng, fn)
            return self.cnt[ch]
        if sem is not None:
            s, v = self.inc(sem, inc)
            self.emit(eng, lambda e, a=args, k=kw, s=s, m=meth, i=inc:
                      getattr(e, m)(*a, **k).then_inc(s, i))
            return v
        self.emit(eng, lambda e, a=args, k=kw, m=meth: getattr(e, m)(*a, **k))
        return None

    def dma(self, eng, out, in_, sem):
        return self.op(eng, "dma_start", sem=sem, inc=16, out=out, in_=in_)

    def sb(self, name, shape, dt):
        return self.alloc_scope.enter_context(self.nc.sbuf_tensor(name, shape, dt))

    def sb_perm(self, name, shape, dt):
        return self.es.enter_context(self.nc.sbuf_tensor(name, shape, dt))

    def ps(self, name, shape, dt=F32):
        return self.es.enter_context(self.nc.psum_tensor(name, shape, dt))

    def inp(self, name, shape, dt):
        return self.nc.dram_tensor(name, shape, dt, kind="ExternalInput").ap()

    def outp(self, name, shape, dt):
        return self.nc.dram_tensor(name, shape, dt, kind="ExternalOutput").ap()

    def finalize(self):
        qq = self.q
        with self.nc.Block() as block:
            @block.sync
            def _(e):
                for f in qq["sync"]:
                    f(e)

            @block.gpsimd
            def _(e):
                for f in qq["gpsimd"]:
                    f(e)

            @block.vector
            def _(e):
                for f in qq["vector"]:
                    f(e)

            @block.scalar
            def _(e):
                for f in qq["scalar"]:
                    f(e)

            @block.tensor
            def _(e):
                for f in qq["tensor"]:
                    f(e)
        self.es.close()
        return self.nc


def emit_prelude(p):
    p.x_s = p.sb("x_s", [128, NCH * NT], F32)
    p.u_s = p.sb("u_s", [128, NCH * NT], BF16)
    p.cst_s = p.sb("cst_s", [128, 8], F32)
    p.ones_s = p.sb("ones_s", [128, 128], BF16)
    p.onesf_s = p.sb("onesf_s", [1, 128], F32)
    p.identb_s = p.sb("identb_s", [128, 128], BF16)
    p.identf_s = p.sb("identf_s", [128, 128], F32)
    p.mask_s = p.sb("mask_s", [128, 8 * 128], BF16)
    cst = p.inp("cst", [128, 8], F32)
    ones = p.inp("onesb", [128, 128], BF16)
    onesf = p.inp("onesf", [1, 128], F32)
    identb = p.inp("identb", [128, 128], BF16)
    identf = p.inp("identf", [128, 128], F32)
    p.dma("sync", p.cst_s[:, :], cst, "ldc")
    p.dma("sync", p.ones_s[:, :], ones, "ldc")
    p.dma("sync", p.identb_s[:, :], identb, "ldc")
    p.dma("sync", p.identf_s[:, :], identf, "ldc")
    p.ldc_v = p.dma("sync", p.onesf_s[:, :], onesf, "ldc")


def host_consts():
    cst = np.zeros((128, 8), np.float32)
    cst[:, 0] = EXPB
    cst[:, 1] = 1e-5
    cst[:, 2] = 0.5
    cst[:, 3] = 1.0
    return {"cst": cst,
            "onesb": np.ones((128, 128), BF),
            "onesf": np.ones((1, 128), np.float32),
            "identb": np.eye(128, dtype=BF),
            "identf": np.eye(128, dtype=np.float32)}


def emit_load_x(p):
    x_in = p.inp("x", [128, NCH * NT], F32)
    v = p.dma("sync", p.x_s[:, :], x_in, "ldx")
    return ("ldx", v)


def emit_store_x(p, deps):
    x_out = p.outp("y", [128, NCH * NT], F32)
    for s, v in deps:
        p.wait("sync", s, v)
    v = p.dma("sync", x_out, p.x_s[:, :], "stx")
    p.wait("sync", "stx", v)


def emit_ln(p, tag, deps):
    nc = p.nc
    sv, ss, sm = "vch", "sch", "mm"
    with ExitStack() as st:
        xbf = st.enter_context(nc.sbuf_tensor(tag + "xbf", [128, NCH * NT], BF16))
        xsq = st.enter_context(nc.sbuf_tensor(tag + "xsq", [128, NCH * NT], BF16))
        rows = st.enter_context(nc.sbuf_tensor(tag + "rw", [1, 1024], F32))
        pstA = st.enter_context(nc.psum_tensor(tag + "pA", [1, 512], F32))
        pstB = st.enter_context(nc.psum_tensor(tag + "pB", [1, 512], F32))
        pbcM = st.enter_context(nc.psum_tensor(tag + "pM", [128, 512], F32))
        pbcR = st.enter_context(nc.psum_tensor(tag + "pR", [128, 512], F32))

        for s, v in deps:
            p.wait("vector", s, v)
            p.wait("scalar", s, v)
            p.wait("tensor", s, v)
        p.wait("vector", "ldc", p.ldc_v)
        p.wait("scalar", "ldc", p.ldc_v)
        p.wait("tensor", "ldc", p.ldc_v)

        for cc in range(NCH):
            xc = p.x_s[:, cc * NT:(cc + 1) * NT]
            p.op("vector", "tensor_copy", xbf[:, cc * NT:(cc + 1) * NT], xc,
                 sem=sv)
            p.op("scalar", "activation", xsq[:, cc * NT:(cc + 1) * NT], xc,
                 AF.Square, sem=ss)
        vcop, scop = p.cnt[sv], p.cnt[ss]
        p.wait("tensor", sv, vcop)
        p.wait("tensor", ss, scop)

        for tc in range(4):
            for cc in range(NCH):
                off = cc * NT + tc * 512
                p.op("tensor", "matmul", pstA[:, :], p.ones_s[:, 0:1],
                     xbf[:, off:off + 512], start=(cc == 0),
                     stop=(cc == NCH - 1), sem=sm)
                p.op("tensor", "matmul", pstB[:, :], p.ones_s[:, 0:1],
                     xsq[:, off:off + 512], start=(cc == 0),
                     stop=(cc == NCH - 1), sem=sm)
            mmv = p.cnt[sm]
            p.wait("vector", sm, mmv)
            p.op("vector", "tensor_scalar", rows[0:1, 0:512], pstA[0:1, :],
                 1.0 / C, None, op0=ALU.mult, sem=sv)
            p.op("vector", "tensor_tensor", rows[0:1, 512:1024],
                 rows[0:1, 0:512], rows[0:1, 0:512], op=ALU.mult, sem=sv)
            p.op("vector", "scalar_tensor_tensor", rows[0:1, 512:1024],
                 pstB[0:1, :], 1.0 / C, rows[0:1, 512:1024],
                 op0=ALU.mult, op1=ALU.subtract, sem=sv)
            vv = p.cnt[sv]
            p.wait("scalar", sv, vv)
            p.op("scalar", "activation", rows[0:1, 512:1024],
                 rows[0:1, 512:1024], AF.Ln, bias=p.cst_s[0:1, 1:2], sem=ss)
            p.op("scalar", "activation", rows[0:1, 512:1024],
                 rows[0:1, 512:1024], AF.Exp, scale=-0.5, sem=ss)
            sv2 = p.cnt[ss]
            p.wait("tensor", ss, sv2)
            p.op("tensor", "matmul", pbcM[:, :], p.onesf_s[0:1, :],
                 rows[0:1, 0:512], start=True, stop=True, sem=sm)
            p.op("tensor", "matmul", pbcR[:, :], p.onesf_s[0:1, :],
                 rows[0:1, 512:1024], start=True, stop=True, sem=sm)
            mmv2 = p.cnt[sm]
            p.wait("vector", sm, mmv2)
            for cc in range(NCH):
                off = cc * NT + tc * 512
                xc = p.x_s[:, off:off + 512]
                p.op("vector", "tensor_tensor", xbf[:, off:off + 512], xc,
                     pbcM[:, :], op=ALU.subtract)
                p.op("vector", "tensor_tensor", p.u_s[:, off:off + 512],
                     xbf[:, off:off + 512], pbcR[:, :], op=ALU.mult, sem=sv)
            uvv = p.cnt[sv]
            if tc < 3:
                p.wait("tensor", sv, uvv)
    return (sv, p.cnt[sv])




def emit_ffn0_weff(p):
    nc = p.nc
    tag = "wf"
    sv, ss, sm = "vch", "sch", "mm"
    p.weff_s = p.sb("weff_s", [128, NCH * 1024], BF16)
    p.bffn_s = p.sb("bffn_s", [128, 8], F32)
    with ExitStack() as st:
        wt = st.enter_context(nc.sbuf_tensor(tag + "wt", [128, 1024], F32))
        ldt = st.enter_context(nc.sbuf_tensor(tag + "ldt", [100, 4], F32))
        dax = st.enter_context(nc.sbuf_tensor(tag + "dax", [100, 1024], F32))
        lut = st.enter_context(nc.sbuf_tensor(tag + "lut", [4, 50], F32))
        uat = st.enter_context(nc.sbuf_tensor(tag + "uat", [50, 1024], F32))
        aA = st.enter_context(nc.sbuf_tensor(tag + "A", [4, 1024], F32))
        bB = st.enter_context(nc.sbuf_tensor(tag + "B", [50, 1024], F32))
        bcol = st.enter_context(nc.sbuf_tensor(tag + "bc", [128, 8], F32))
        psA = st.enter_context(nc.psum_tensor(tag + "psA", [4, 1024], F32))
        psB = st.enter_context(nc.psum_tensor(tag + "psB", [50, 1024], F32))
        psC = st.enter_context(nc.psum_tensor(tag + "psC", [128, 1024], F32))

        wt_in = p.inp("ffnWT", [1024, 1024], F32)
        b_in = p.inp("ffnb", [128, 8], F32)
        ldt_in = p.inp("ffnldT", [100, 4], F32)
        dax_in = p.inp("ffndax", [100, 1024], F32)
        lut_in = p.inp("ffnluT", [4, 50], F32)
        uat_in = p.inp("ffnuaT", [50, 1024], F32)

        p.dma("sync", ldt[:, :], ldt_in, "ldw")
        p.dma("sync", dax[:, :], dax_in, "ldw")
        p.dma("sync", lut[:, :], lut_in, "ldw")
        p.dma("sync", bcol[:, :], b_in, "ldw")
        v0 = p.dma("sync", uat[:, :], uat_in, "ldw")
        p.wait("tensor", "ldw", v0)
        for half in range(2):
            p.op("tensor", "matmul", psA[:, half * 512:(half + 1) * 512],
                 ldt[:, :], dax[:, half * 512:(half + 1) * 512],
                 start=True, stop=True, sem=sm)
        p.wait("vector", sm, p.cnt[sm])
        p.op("vector", "tensor_copy", aA[:, :], psA[:, :], sem=sv)
        p.op("vector", "tensor_scalar", p.bffn_s[:, :], bcol[:, :], 0.5, None,
             op0=ALU.mult, sem=sv)
        p.wait("tensor", sv, p.cnt[sv])
        for half in range(2):
            p.op("tensor", "matmul", psB[:, half * 512:(half + 1) * 512],
                 lut[:, :], aA[:, half * 512:(half + 1) * 512],
                 start=True, stop=True, sem=sm)
        p.wait("vector", sm, p.cnt[sm])
        p.op("vector", "tensor_copy", bB[:, :], psB[:, :], sem=sv)
        bb_v = p.cnt[sv]
        p.wait("tensor", sv, bb_v)
        for cc in range(NCH):
            wv = p.dma("sync", wt[:, :],
                       wt_in[cc * 128:(cc + 1) * 128, :], "ldw")
            for half in range(2):
                p.op("tensor", "matmul", psC[:, half * 512:(half + 1) * 512],
                     bB[:, cc * 128:(cc + 1) * 128],
                     uat[:, half * 512:(half + 1) * 512],
                     start=True, stop=True, sem=sm)
            mmv = p.cnt[sm]
            p.wait("vector", sm, mmv)
            p.wait("vector", "ldw", wv)
            p.op("vector", "tensor_tensor", p.weff_s[:, cc * 1024:(cc + 1) * 1024],
                 psC[:, :], wt[:, :], op=ALU.add, sem=sv)
            if cc < NCH - 1:
                p.wait("tensor", sv, p.cnt[sv])
                p.wait("sync", sv, p.cnt[sv])
    return (sv, p.cnt[sv])


def emit_ffn0_apply(p, tag, u_dep, weff_dep):
    sv, ss, sm = "vch", "sch", "mm"
    nc = p.nc
    with ExitStack() as st:
        tmp = st.enter_context(nc.sbuf_tensor(tag + "t", [128, 4 * 512], F32))
        pso = [st.enter_context(nc.psum_tensor(tag + "p%d" % i, [128, 512], F32))
               for i in range(4)]
        p.wait("tensor", u_dep[0], u_dep[1])
        p.wait("tensor", weff_dep[0], weff_dep[1])
        p.wait("scalar", weff_dep[0], weff_dep[1])
        slot_act = [None] * 4
        slot_dve = [None] * 4
        nslot = 0
        for oc in range(NCH):
            for tc in range(4):
                slot = nslot % 4
                nslot += 1
                psu = pso[slot]
                if slot_act[slot] is not None:
                    p.wait("tensor", ss, slot_act[slot])
                for cc in range(NCH):
                    p.op("tensor", "matmul", psu[:, :],
                         p.weff_s[:, cc * 1024 + oc * 128: cc * 1024 + (oc + 1) * 128],
                         p.u_s[:, cc * NT + tc * 512: cc * NT + (tc + 1) * 512],
                         start=(cc == 0), stop=(cc == NCH - 1),
                         sem=(sm if cc == NCH - 1 else None))
                mmv = p.cnt[sm]
                p.wait("scalar", sm, mmv)
                if slot_dve[slot] is not None:
                    p.wait("scalar", sv, slot_dve[slot])
                tsl = tmp[:, slot * 512:(slot + 1) * 512]
                p.op("scalar", "activation", tsl, psu[:, :], AF.Identity,
                     bias=p.bffn_s[:, oc:oc + 1], scale=0.5, sem=ss)
                slot_act[slot] = p.cnt[ss]
                p.wait("vector", ss, slot_act[slot])
                xsl = p.x_s[:, oc * NT + tc * 512: oc * NT + (tc + 1) * 512]
                p.op("vector", "tensor_tensor", xsl, xsl, tsl, op=ALU.add,
                     sem=sv)
                slot_dve[slot] = p.cnt[sv]
    return (sv, p.cnt[sv])


def shard_x(x):
    xs = []
    xr = np.ascontiguousarray(x).reshape(B, 32, 128, C)
    for c in range(8):
        v = xr[:, c::8]
        v = v.transpose(3, 0, 1, 2).reshape(C, NT)
        v = v.reshape(NCH, 128, NT).transpose(1, 0, 2).reshape(128, NCH * NT)
        xs.append(np.ascontiguousarray(v))
    return xs


def unshard_x(ys):
    out = np.empty((B, 32, 128, C), np.float32)
    for c in range(8):
        v = ys[c].reshape(128, NCH, NT).transpose(1, 0, 2).reshape(C, NT)
        v = v.reshape(C, B, 4, 128).transpose(1, 2, 3, 0)
        out[:, c::8] = v
    return out.reshape(B, T, C)


def ffn0_host_inputs(W, b, lora_down, lora_up, down_aux, up_aux):
    return {
        "ffnWT": np.ascontiguousarray(W.T),
        "ffnb": np.ascontiguousarray(np.asarray(b, np.float32)
                                     .reshape(NCH, 128).T),
        "ffnldT": np.ascontiguousarray(lora_down.T),
        "ffndax": np.ascontiguousarray(down_aux),
        "ffnluT": np.ascontiguousarray(lora_up.T),
        "ffnuaT": np.ascontiguousarray(up_aux.T),
    }


def moe_host_inputs(inputs):
    return {
        "centT": np.ascontiguousarray(inputs["L1_ffn_centroids"].T),
        "ebias": np.ascontiguousarray(
            np.asarray(inputs["L1_ffn_expert_biases"], np.float32)[None, :]),
        "brT": np.ascontiguousarray(inputs["L1_ffn_br"]),
        "bs2": np.ascontiguousarray(inputs["L1_ffn_bs"]),
        "half2": np.full((2, 1), 0.5, np.float32),
        "wsT": np.ascontiguousarray(inputs["L1_ffn_Ws"].transpose(0, 2, 1)),
        "wrT": np.ascontiguousarray(inputs["L1_ffn_Wr"].transpose(0, 2, 1)),
    }


def build_ffn0_launch():
    p = Prog()
    emit_prelude(p)
    xd = emit_load_x(p)
    wd = emit_ffn0_weff(p)
    ud = emit_ln(p, "ln", [xd, wd])
    done = emit_ffn0_apply(p, "fa", ud, wd)
    emit_store_x(p, [done])
    return p.finalize()


def build_moe_launch():
    p = Prog()
    emit_prelude(p)
    xd = emit_load_x(p)
    wrt_in = p.inp("wrT", [16, 1024, 1024], F32)
    pd = emit_moe_prep(p)
    ud = emit_ln(p, "ln", [xd, pd])
    done = emit_moe_apply(p, "ma", ud, pd, wrt_in)
    emit_store_x(p, [done])
    return p.finalize()


def run_launch(nc, in_maps):
    res = run_bass_kernel_spmd(nc, in_maps, core_ids=list(range(8)))
    return [r["y"] for r in res.results]


_NC_CACHE = {}


def _launch(kind, xs, inputs):
    consts = host_consts()
    if kind == "ffn0":
        if kind not in _NC_CACHE:
            _NC_CACHE[kind] = build_ffn0_launch()
        w = ffn0_host_inputs(
            inputs["L0_ffn_W"], inputs["L0_ffn_b"],
            inputs["L0_ffn_lora_down"], inputs["L0_ffn_lora_up"],
            inputs["L0_ffn_down_aux"], inputs["L0_ffn_up_aux"])
    elif kind == "moe":
        if kind not in _NC_CACHE:
            _NC_CACHE[kind] = build_moe_launch()
        w = moe_host_inputs(inputs)
    elif kind in ("attn0", "attn1"):
        L = int(kind[-1])
        if kind not in _NC_CACHE:
            _NC_CACHE[kind] = build_attn_launch(L)
        in_maps = [dict(consts, x=xs[c], **attn_host_inputs(L, c, inputs))
                   for c in range(8)]
        ys = run_launch(_NC_CACHE[kind], in_maps)
        for c in range(8):
            xs[c] = ys[c]
        return
    else:
        raise KeyError(kind)
    in_maps = [dict(consts, x=xs[c], **w) for c in range(8)]
    ys = run_launch(_NC_CACHE[kind], in_maps)
    for c in range(8):
        xs[c] = ys[c]


def build_fused(n_subs=6):
    p = Prog()
    emit_prelude(p)
    xd = emit_load_x(p)
    wrt_in = p.inp("wrT", [16, 1024, 1024], F32)
    d = xd
    with ExitStack() as s_l0:
        if n_subs >= 1:
            p.alloc_scope = s_l0
            wd = emit_ffn0_weff(p)
            a0p = emit_attn_prep(p, 0, deps=[wd]) if n_subs >= 2 else None
            p.alloc_scope = p.es
            u1 = emit_ln(p, "ln1", [xd, wd] + ([a0p] if a0p else []))
            d = emit_ffn0_apply(p, "f1", u1, wd)
        if n_subs >= 2:
            u2 = emit_ln(p, "ln2", [d])
            d = emit_attn(p, "a0", 0, u2, a0p)
        if n_subs >= 3:
            u3 = emit_ln(p, "ln3", [d])
            d = emit_ffn0_apply(p, "f3", u3, wd)
    with ExitStack() as s_l1:
        if n_subs >= 4:
            p.alloc_scope = s_l1
            mp = emit_moe_prep(p, deps=[d])
            a1p = emit_attn_prep(p, 1, deps=[d, mp]) if n_subs >= 5 else None
            p.alloc_scope = p.es
            u4 = emit_ln(p, "ln4", [d, mp] + ([a1p] if a1p else []))
            d = emit_moe_apply(p, "m4", u4, mp, wrt_in)
        if n_subs >= 5:
            u5 = emit_ln(p, "ln5", [d])
            d = emit_attn(p, "a1", 1, u5, a1p)
        if n_subs >= 6:
            u6 = emit_ln(p, "ln6", [d])
            d = emit_moe_apply(p, "m6", u6, mp, wrt_in)
    emit_store_x(p, [d])
    return p.finalize()


def fused_in_maps(inputs):
    consts = host_consts()
    xs = shard_x(inputs["L0_attn_x"])
    ffn_w = ffn0_host_inputs(
        inputs["L0_ffn_W"], inputs["L0_ffn_b"],
        inputs["L0_ffn_lora_down"], inputs["L0_ffn_lora_up"],
        inputs["L0_ffn_down_aux"], inputs["L0_ffn_up_aux"])
    moe_w = moe_host_inputs(inputs)
    maps = []
    for c in range(8):
        m = dict(consts, x=xs[c], **ffn_w)
        m.update(moe_w)
        m.update(attn_host_inputs(0, c, inputs))
        m.update(attn_host_inputs(1, c, inputs))
        maps.append(m)
    return maps


FUSED = True
CHAIN = ["ffn0", "attn0", "ffn0", "moe", "attn1", "moe"]


def kernel(**inputs):
    inputs = {k: np.asarray(v, np.float32) for k, v in inputs.items()}
    import os
    chain = os.environ.get("MK_CHAIN", "")
    if FUSED and not chain:
        if "fused" not in _NC_CACHE:
            _NC_CACHE["fused"] = build_fused()
        ys = run_launch(_NC_CACHE["fused"], fused_in_maps(inputs))
        return unshard_x(ys)
    xs = shard_x(inputs["L0_attn_x"])
    for kind in (chain.split(",") if chain else CHAIN):
        _launch(kind, xs, inputs)
    return unshard_x(xs)


def emit_moe_prep(p, deps=()):
    nc = p.nc
    tag = "mp"
    sv, ss, sm = "vch", "sch", "mm"
    for s, v in deps:
        for eng in ("sync", "gpsimd", "tensor", "vector", "scalar"):
            p.wait(eng, s, v)
    p.centT_s = p.sb("centT_s", [128, NCH * 16], BF16)
    p.eb_s = p.sb("eb_s", [128, 16], F32)
    p.br_s = p.sb("br_s", [16, 1024], BF16)
    p.bsrow_s = p.sb("bsrow_s", [1, 1024], BF16)
    p.wseff_s = p.sb("wseff_s", [128, NCH * 1024], BF16)

    centT = p.inp("centT", [1024, 16], F32)
    ebias = p.inp("ebias", [1, 16], F32)
    br_in = p.inp("brT", [16, 1024], F32)
    bs_in = p.inp("bs2", [2, 1024], F32)
    half2 = p.inp("half2", [2, 1], F32)
    wst_in = p.inp("wsT", [2, 1024, 1024], F32)
    p.dma("gpsimd", p.centT_s[:, :].rearrange("p (c n) -> p c n", n=16),
          centT.rearrange("(c p) n -> p c n", p=128), "mpw")
    p.dma("gpsimd", p.br_s[:, :], br_in, "mpw")
    with ExitStack() as st:
        ebrow = st.enter_context(nc.sbuf_tensor(tag + "eb", [1, 16], F32))
        bss = st.enter_context(nc.sbuf_tensor(tag + "bs", [2, 1024], F32))
        h2 = st.enter_context(nc.sbuf_tensor(tag + "h2", [2, 1], F32))
        wsa = st.enter_context(nc.sbuf_tensor(tag + "wsa", [128, 1024], F32))
        wsb = st.enter_context(nc.sbuf_tensor(tag + "wsb", [128, 1024], F32))
        pse = st.enter_context(nc.psum_tensor(tag + "pe", [128, 16], F32))
        psb = st.enter_context(nc.psum_tensor(tag + "pb", [1, 1024], F32))
        v0 = p.dma("sync", ebrow[:, :], ebias, "ldw")
        p.dma("sync", bss[:, :], bs_in, "ldw")
        v1 = p.dma("sync", h2[:, :], half2, "ldw")
        p.wait("tensor", "ldw", v1)
        p.wait("tensor", "ldc", p.ldc_v)
        p.op("tensor", "matmul", pse[:, :], p.onesf_s[0:1, :], ebrow[0:1, :],
             start=True, stop=True, sem=sm)
        for half in range(2):
            p.op("tensor", "matmul", psb[:, half * 512:(half + 1) * 512],
                 h2[:, :], bss[:, half * 512:(half + 1) * 512],
                 start=True, stop=True, sem=sm)
        mmv = p.cnt[sm]
        p.wait("scalar", sm, mmv)
        p.op("scalar", "copy", p.eb_s[:, :], pse[:, :], sem=ss)
        p.op("scalar", "copy", p.bsrow_s[:, :], psb[0:1, :], sem=ss)
        for cc in range(NCH):
            va = p.dma("sync", wsa[:, :], wst_in[0, cc * 128:(cc + 1) * 128, :],
                       "mpwa")
            vb = p.dma("sync", wsb[:, :], wst_in[1, cc * 128:(cc + 1) * 128, :],
                       "mpwa")
            p.wait("vector", "mpwa", vb)
            p.op("vector", "tensor_tensor", wsa[:, :], wsa[:, :], wsb[:, :],
                 op=ALU.add)
            p.op("vector", "tensor_scalar",
                 p.wseff_s[:, cc * 1024:(cc + 1) * 1024], wsa[:, :], 0.5, None,
                 op0=ALU.mult, sem=sv)
            if cc < NCH - 1:
                p.wait("sync", sv, p.cnt[sv])
        p.wait("vector", ss, p.cnt[ss])
        p.wait("vector", "mpw", 32)
    return (sv, p.cnt[sv])


def emit_moe_apply(p, tag, u_dep, prep_dep, wrt_in, dbg=()):
    nc = p.nc
    sv, ss, sm, sg = "vch", "sch", "mm", tag + "g"
    with ExitStack() as st:
        g05 = st.enter_context(nc.sbuf_tensor(tag + "g05", [128, 16 * 16], F32))
        gfm = st.enter_context(nc.sbuf_tensor(tag + "gfm", [16, NT], BF16))
        acc = st.enter_context(nc.sbuf_tensor(tag + "acc", [128, 16 * 1024], F32))
        r16 = st.enter_context(nc.sbuf_tensor(tag + "r16", [128, 16], F32))
        r16b = st.enter_context(nc.sbuf_tensor(tag + "r16b", [128, 16], F32))
        rcol = st.enter_context(nc.sbuf_tensor(tag + "rc", [128, 4], F32))
        gbf = st.enter_context(nc.sbuf_tensor(tag + "gbf", [128, 16], BF16))

        for eng in ("tensor", "vector", "scalar", "gpsimd", "sync"):
            p.wait(eng, u_dep[0], u_dep[1])
            p.wait(eng, prep_dep[0], prep_dep[1])

        with ExitStack() as st2:
            psr = st2.enter_context(nc.psum_tensor(tag + "psr", [128, 16], F32))
            psg = st2.enter_context(nc.psum_tensor(tag + "psg", [16, 128], BF16))
            for m in range(16):
                for cc in range(NCH):
                    p.op("tensor", "matmul", psr[:, :],
                         p.u_s[:, cc * NT + m * 128: cc * NT + (m + 1) * 128],
                         p.centT_s[:, cc * 16:(cc + 1) * 16],
                         start=(cc == 0), stop=(cc == NCH - 1),
                         sem=(sm if cc == NCH - 1 else None))
                mmv = p.cnt[sm]
                p.wait("vector", sm, mmv)
                p.op("vector", "tensor_tensor", r16[:, :], psr[:, :],
                     p.eb_s[:, :], op=ALU.add, sem=sv)
                rr_v = p.cnt[sv]
                p.op("vector", "reduce_max", rcol[:, 0:1], r16[:, :], axis=AX.X,
                     negate=True, sem=sv)
                vv = p.cnt[sv]
                p.wait("scalar", sv, vv)
                p.op("scalar", "activation", r16[:, :], r16[:, :], AF.Exp,
                     bias=rcol[:, 0:1], sem=ss)
                ssv = p.cnt[ss]
                p.wait("vector", ss, ssv)
                p.op("vector", "reduce_sum", rcol[:, 1:2], r16[:, :], axis=AX.X)
                p.op("vector", "reciprocal", rcol[:, 2:3], rcol[:, 1:2])
                p.op("vector", "tensor_scalar", r16[:, :], r16[:, :],
                     rcol[:, 2:3], None, op0=ALU.mult)
                p.op("vector", "reduce_max", rcol[:, 0:1], r16[:, :], axis=AX.X,
                     negate=True)
                p.op("vector", "tensor_scalar", r16b[:, :], r16[:, :],
                     rcol[:, 0:1], None, op0=ALU.add)
                p.op("vector", "tensor_scalar", r16b[:, :], r16b[:, :], 0.0,
                     None, op0=ALU.is_equal)
                p.op("vector", "scalar_tensor_tensor", r16b[:, :], r16b[:, :],
                     NEG, r16[:, :], op0=ALU.mult, op1=ALU.add)
                p.op("vector", "reduce_max", rcol[:, 3:4], r16b[:, :], axis=AX.X,
                     negate=True)
                p.op("vector", "tensor_scalar", r16b[:, :], r16[:, :],
                     rcol[:, 3:4], None, op0=ALU.add)
                p.op("vector", "tensor_scalar", r16b[:, :], r16b[:, :], 0.0,
                     None, op0=ALU.is_ge)
                p.op("vector", "tensor_tensor", r16b[:, :], r16b[:, :],
                     r16[:, :], op=ALU.mult)
                p.op("vector", "tensor_scalar", g05[:, m * 16:(m + 1) * 16],
                     r16b[:, :], 0.5, None, op0=ALU.mult)
                p.op("vector", "tensor_copy", gbf[:, :], r16b[:, :], sem=sv)
                gv = p.cnt[sv]
                p.wait("tensor", sv, gv)
                p.op("tensor", "transpose", psg[:, :], gbf[:, :],
                     p.identb_s[:, :], sem=sm)
                tv = p.cnt[sm]
                p.wait("scalar", sm, tv)
                p.op("scalar", "copy", gfm[0:16, m * 128:(m + 1) * 128],
                     psg[:, :], sem=ss)
                p.wait("tensor", ss, p.cnt[ss])
                p.wait("tensor", sv, rr_v)
        router_v = p.cnt[ss]
        if dbg:
            o1 = p.outp("dbg_g05", [128, 16 * 16], F32)
            o2 = p.outp("dbg_gfm", [16, NT], BF16)
            p.wait("sync", ss, router_v)
            p.wait("sync", sv, p.cnt[sv])
            p.dma("sync", o1, g05[:, :], "dbgst")
            v = p.dma("sync", o2, gfm[:, :], "dbgst")
            if dbg == {"router"}:
                p.wait("sync", "dbgst", v)
                return (sv, p.cnt[sv])

        with ExitStack() as st2:
            pssh = [st2.enter_context(
                nc.psum_tensor(tag + "pss%d" % i, [128, 512], F32))
                for i in range(2)]
            slot_ready = [None, None]
            p.wait("tensor", ss, router_v)
            for m in range(16):
                for oh in range(2):
                    slot = (m * 2 + oh) % 2
                    psu = pssh[slot]
                    if slot_ready[slot] is not None:
                        p.wait("tensor", sv, slot_ready[slot])
                    for cc in range(NCH):
                        p.op("tensor", "matmul", psu[:, :],
                             p.u_s[:, cc * NT + m * 128: cc * NT + (m + 1) * 128],
                             p.wseff_s[:, cc * 1024 + oh * 512: cc * 1024 + (oh + 1) * 512],
                             start=(cc == 0), stop=False)
                    p.op("tensor", "matmul", psu[:, :],
                         gfm[0:16, m * 128:(m + 1) * 128],
                         p.br_s[0:16, oh * 512:(oh + 1) * 512],
                         start=False, stop=False)
                    p.op("tensor", "matmul", psu[:, :], p.ones_s[0:1, :],
                         p.bsrow_s[0:1, oh * 512:(oh + 1) * 512],
                         start=False, stop=True, sem=sm)
                    mmv = p.cnt[sm]
                    p.wait("vector", sm, mmv)
                    p.op("vector", "tensor_scalar",
                         acc[:, m * 1024 + oh * 512: m * 1024 + (oh + 1) * 512],
                         psu[:, :], 0.5, None, op0=ALU.mult, sem=sv)
                    slot_ready[slot] = p.cnt[sv]
        shared_v = p.cnt[sv]
        if "shared" in dbg:
            o3 = p.outp("dbg_accS", [128, 16 * 1024], F32)
            p.wait("sync", sv, shared_v)
            v = p.dma("sync", o3, acc[:, :], "dbgst")
            if "experts" not in dbg:
                p.wait("sync", "dbgst", v)
                return (sv, p.cnt[sv])

        with ExitStack() as st2:
            wrbuf = [st2.enter_context(
                nc.sbuf_tensor(tag + "wr%d" % i, [128, NCH * 512], BF16))
                for i in range(2)]
            pse = [st2.enter_context(
                nc.psum_tensor(tag + "pe%d" % i, [128, 512], F32))
                for i in range(8)]
            pairs = [(e, oh) for e in range(16) for oh in range(2)]
            sgs = [sg + "A", sg + "B"]
            buf_ready = {}
            for i in (0, 1):
                e, oh = pairs[i]
                src = wrt_in[e, :, oh * 512:(oh + 1) * 512].rearrange(
                    "(c p) n -> p c n", p=128)
                dst = wrbuf[i % 2][:, :].rearrange("p (c n) -> p c n", n=512)
                buf_ready[i] = p.dma("gpsimd", dst, src, sgs[i % 2])
            bank_free = [None] * 8
            p.wait("tensor", sv, shared_v)
            for i, (e, oh) in enumerate(pairs):
                p.wait("tensor", sgs[i % 2], buf_ready[i])
                wb = wrbuf[i % 2]
                for mg in range(4):
                    half = mg % 2
                    for cc in range(NCH):
                        for mi in range(4):
                            m = mg * 4 + mi
                            bank = half * 4 + mi
                            if cc == 0 and bank_free[bank] is not None:
                                p.wait("tensor", sv, bank_free[bank])
                            p.op("tensor", "matmul", pse[bank][:, :],
                                 p.u_s[:, cc * NT + m * 128: cc * NT + (m + 1) * 128],
                                 wb[:, cc * 512:(cc + 1) * 512],
                                 start=(cc == 0), stop=(cc == NCH - 1),
                                 sem=(sm if (cc == NCH - 1 and mi == 3) else None))
                    mmv = p.cnt[sm]
                    p.wait("vector", sm, mmv)
                    for mi in range(4):
                        m = mg * 4 + mi
                        bank = half * 4 + mi
                        asl = acc[:, m * 1024 + oh * 512: m * 1024 + (oh + 1) * 512]
                        p.op("vector", "scalar_tensor_tensor", asl,
                             pse[bank][:, :], g05[:, m * 16 + e: m * 16 + e + 1],
                             asl, op0=ALU.mult, op1=ALU.add, sem=sv)
                        bank_free[bank] = p.cnt[sv]
                if i + 2 < len(pairs):
                    e2, oh2 = pairs[i + 2]
                    p.wait("gpsimd", sm, p.cnt[sm])
                    src = wrt_in[e2, :, oh2 * 512:(oh2 + 1) * 512].rearrange(
                        "(c p) n -> p c n", p=128)
                    dst = wrbuf[i % 2][:, :].rearrange("p (c n) -> p c n", n=512)
                    buf_ready[i + 2] = p.dma("gpsimd", dst, src, sgs[i % 2])
        expert_v = p.cnt[sv]
        if "experts" in dbg:
            o4 = p.outp("dbg_accE", [128, 16 * 1024], F32)
            p.wait("sync", sv, expert_v)
            v = p.dma("sync", o4, acc[:, :], "dbgst")
            p.wait("sync", "dbgst", v)
            return (sv, p.cnt[sv])

        for cc in range(NCH):
            xc = p.x_s[:, cc * NT:(cc + 1) * NT]
            p.op("vector", "scalar_tensor_tensor", xc,
                 p.u_s[:, cc * NT:(cc + 1) * NT], 0.5, xc,
                 op0=ALU.mult, op1=ALU.add, sem=sv)

        with ExitStack() as st2:
            pst = [st2.enter_context(
                nc.psum_tensor(tag + "pt%d" % i, [128, 512], F32))
                for i in range(2)]
            grp_free = [None, None]
            p.wait("tensor", sv, expert_v)
            for m in range(16):
                for og in range(2):
                    slot = (m * 2 + og) % 2
                    if grp_free[slot] is not None:
                        p.wait("tensor", sv, grp_free[slot])
                    for oi in range(4):
                        oc = og * 4 + oi
                        p.op("tensor", "transpose",
                             pst[slot][:, oi * 128:(oi + 1) * 128],
                             acc[:, m * 1024 + oc * 128: m * 1024 + (oc + 1) * 128],
                             p.identf_s[:, :],
                             sem=(sm if oi == 3 else None))
                    mmv = p.cnt[sm]
                    p.wait("vector", sm, mmv)
                    for oi in range(4):
                        oc = og * 4 + oi
                        xsl = p.x_s[:, oc * NT + m * 128: oc * NT + (m + 1) * 128]
                        p.op("vector", "tensor_tensor", xsl, xsl,
                             pst[slot][:, oi * 128:(oi + 1) * 128], op=ALU.add,
                             sem=sv)
                    grp_free[slot] = p.cnt[sv]
    return (sv, p.cnt[sv])


def emit_attn_prep(p, L, deps=()):
    nc = p.nc
    tag = "ap%d" % L
    sv, ss, sm = "vch", "sch", "mm"
    for s, v in deps:
        for eng in ("sync", "gpsimd", "tensor", "vector", "scalar"):
            p.wait(eng, s, v)
    if not hasattr(p, "kc_d"):
        p.kc_d = [p.nc.dram_tensor("kc%d" % g, [256, NT], BF16) for g in range(4)]
        p.vc_d = [p.nc.dram_tensor("vc%d" % g, [NT, 256], BF16) for g in range(4)]
        p.gk_d = [p.nc.dram_tensor("gk%d" % g, [8 * 256, NT], BF16) for g in range(4)]
        p.gv_d = [p.nc.dram_tensor("gv%d" % g, [8 * NT, 256], BF16) for g in range(4)]
        mk = p.inp("maskt", [128, 8 * 128], BF16)
        p.mask_v = p.dma("sync", p.mask_s[:, :], mk, "ldc")
    if L == 0:
        p.cos_s = p.sb("cos_s", [128, NT], BF16)
        p.sin_s = p.sb("sin_s", [128, NT], BF16)
        p.perm_s = p.sb("perm_s", [128, 128], BF16)
        p.dma("sync", p.cos_s[:, :], p.inp("cos_t", [128, NT], BF16), "ldc")
        p.dma("sync", p.sin_s[:, :], p.inp("sin_t", [128, NT], BF16), "ldc")
        v = p.dma("sync", p.perm_s[:, :], p.inp("perm", [128, 128], BF16), "ldc")
        return ("ldc", v)
    p.m2k_s = p.sb("m2k_s", [128, 128], BF16)
    p.m2v_s = p.sb("m2v_s", [128, 128], BF16)
    with ExitStack() as st:
        wkc = st.enter_context(nc.sbuf_tensor(tag + "wkc", [32, 64], F32))
        wke = st.enter_context(nc.sbuf_tensor(tag + "wke", [32, 64], F32))
        wvc = st.enter_context(nc.sbuf_tensor(tag + "wvc", [32, 64], F32))
        wve = st.enter_context(nc.sbuf_tensor(tag + "wve", [32, 64], F32))
        psk = st.enter_context(nc.psum_tensor(tag + "psk", [64, 64], F32))
        psv = st.enter_context(nc.psum_tensor(tag + "psv", [64, 64], F32))
        p.dma("sync", wkc[:, :], p.inp("wkcT", [32, 64], F32), "ldw")
        p.dma("sync", wke[:, :], p.inp("wke", [32, 64], F32), "ldw")
        p.dma("sync", wvc[:, :], p.inp("wvcT", [32, 64], F32), "ldw")
        v0 = p.dma("sync", wve[:, :], p.inp("wve", [32, 64], F32), "ldw")
        mz = p.op("gpsimd", "memset", p.m2k_s[:, :], 0.0)
        mz = p.op("gpsimd", "memset", p.m2v_s[:, :], 0.0)
        p.wait("tensor", "ldw", v0)
        p.op("tensor", "matmul", psk[:, :], wkc[:, :], wke[:, :],
             start=True, stop=True, sem=sm)
        p.op("tensor", "matmul", psv[:, :], wvc[:, :], wve[:, :],
             start=True, stop=True, sem=sm)
        mmv = p.cnt[sm]
        p.wait("scalar", sm, mmv)
        p.wait("scalar", "gch", mz)
        p.op("scalar", "copy", p.m2k_s[0:64, 0:64], psk[:, :], sem=ss)
        p.op("scalar", "copy", p.m2k_s[64:128, 64:128], psk[:, :], sem=ss)
        p.op("scalar", "copy", p.m2v_s[0:64, 0:64], psv[:, :], sem=ss)
        p.op("scalar", "copy", p.m2v_s[64:128, 64:128], psv[:, :], sem=ss)
    return (ss, p.cnt[ss])


def emit_attn(p, tag, L, u_dep, prep_dep):
    nc = p.nc
    sv, ss, sm = "vch", "sch", "mm"
    scc = tag + "c"
    W = {}
    for nm in ("Wq", "Wk", "Wv", "Wp"):
        W[nm] = p.inp(tag + nm, [1024, 1024], F32)
    bq = p.inp(tag + "bq", [128, 8], F32)
    bk = p.inp(tag + "bk", [128, 8], F32)
    bp = p.inp(tag + "bp", [128, 8], F32)
    bvr = p.inp(tag + "bvr", [1, 1024], F32)

    for eng in ("tensor", "vector", "scalar", "gpsimd", "sync"):
        p.wait(eng, u_dep[0], u_dep[1])
        p.wait(eng, prep_dep[0], prep_dep[1])
        p.wait(eng, "ldc", p.mask_v)

    with ExitStack() as st:
        sb = lambda n, sh, dt: st.enter_context(nc.sbuf_tensor(tag + "_" + n, sh, dt))
        q_s = sb("q", [128, NCH * NT], BF16)
        bq_s = sb("bq", [128, 8], F32)
        bk_s = sb("bk", [128, 8], F32)
        bp_s = sb("bp", [128, 8], F32)
        bvr_s = sb("bvr", [1, 1024], BF16)
        vstg = sb("vstg", [128, 512], BF16)

        p.dma("sync", bq_s[:, :], bq, "ldw")
        p.dma("sync", bk_s[:, :], bk, "ldw")
        p.dma("sync", bp_s[:, :], bp, "ldw")
        bv_v = p.dma("gpsimd", bvr_s[:, :], bvr, tag + "bv")

        with ExitStack() as st2:
            psqk = [st2.enter_context(nc.psum_tensor(tag + "pq%d" % i,
                                                     [128, 512], F32))
                    for i in range(2)]
            psw = st2.enter_context(nc.psum_tensor(tag + "psw", [128, 512], F32))
            wsl = [st2.enter_context(nc.sbuf_tensor(tag + "_wsl%d" % i,
                                                    [128, 8 * 128], BF16))
                   for i in range(2)]
            kraw = st2.enter_context(nc.sbuf_tensor(tag + "_kraw", [128, 512], BF16))
            ksw = st2.enter_context(nc.sbuf_tensor(tag + "_ksw", [128, 512], BF16))
            t1 = st2.enter_context(nc.sbuf_tensor(tag + "_t1", [128, 512], BF16))
            wq_jobs = [("Wq", oc) for oc in range(NCH)] + \
                      [("Wk", oc) for oc in range(NCH)]
            wsem = [tag + "wA", tag + "wB"]
            wrdy = {}
            for j in (0, 1):
                nm, oc = wq_jobs[j]
                src = W[nm][:, oc * 128:(oc + 1) * 128].rearrange(
                    "(c p) n -> p c n", p=128)
                wrdy[j] = p.dma("gpsimd", wsl[j % 2][:, :].rearrange(
                    "p (c n) -> p c n", n=128), src, wsem[j % 2])
            slot_rd = [None, None]
            for j, (nm, oc) in enumerate(wq_jobs):
                p.wait("tensor", wsem[j % 2], wrdy[j])
                wb = wsl[j % 2]
                for tc in range(4):
                    slot = tc % 2
                    if slot_rd[slot] is not None:
                        p.wait("tensor", ss, slot_rd[slot])
                    for cc in range(NCH):
                        p.op("tensor", "matmul", psqk[slot][:, :],
                             wb[:, cc * 128:(cc + 1) * 128],
                             p.u_s[:, cc * NT + tc * 512: cc * NT + (tc + 1) * 512],
                             start=(cc == 0), stop=(cc == NCH - 1),
                             sem=(sm if cc == NCH - 1 else None))
                    mmv = p.cnt[sm]
                    p.wait("scalar", sm, mmv)
                    isq = (nm == "Wq")
                    bias = (bq_s if isq else bk_s)[:, oc:oc + 1]
                    if L == 0:
                        p.op("scalar", "activation", kraw[:, :],
                             psqk[slot][:, :], AF.Identity, bias=bias, sem=ss)
                        kv = p.cnt[ss]
                        slot_rd[slot] = kv
                        p.wait("tensor", ss, kv)
                        p.op("tensor", "matmul", psw[:, :], p.perm_s[:, :],
                             kraw[:, :], start=True, stop=True, sem=sm)
                        swv = p.cnt[sm]
                        p.wait("scalar", sm, swv)
                        p.op("scalar", "copy", ksw[:, :], psw[:, :], sem=ss)
                        swc = p.cnt[ss]
                        cs = slice(tc * 512, (tc + 1) * 512)
                        p.wait("vector", ss, kv)
                        p.op("vector", "tensor_tensor", t1[:, :], kraw[:, :],
                             p.cos_s[:, cs], op=ALU.mult)
                        p.wait("vector", ss, swc)
                        p.op("vector", "tensor_tensor", ksw[:, :], ksw[:, :],
                             p.sin_s[:, cs], op=ALU.mult)
                        dst = (q_s[:, oc * NT + tc * 512: oc * NT + (tc + 1) * 512]
                               if isq else kraw[:, :])
                        rv = p.op("vector", "tensor_tensor", dst, t1[:, :],
                                  ksw[:, :], op=ALU.add)
                        p.wait("scalar", sv, rv)
                        if not isq:
                            hg, half = oc // 2, oc % 2
                            p.wait("sync", sv, rv)
                            p.dma("sync",
                                  p.kc_d[hg].ap()[half * 128:(half + 1) * 128,
                                                  tc * 512:(tc + 1) * 512],
                                  kraw[:, :], tag + "kv%d" % hg)
                            p.wait("scalar", tag + "kv%d" % hg,
                                   p.cnt[tag + "kv%d" % hg])
                    else:
                        if isq:
                            p.op("scalar", "activation",
                                 q_s[:, oc * NT + tc * 512: oc * NT + (tc + 1) * 512],
                                 psqk[slot][:, :], AF.Identity, bias=bias,
                                 sem=ss)
                            slot_rd[slot] = p.cnt[ss]
                        else:
                            p.op("scalar", "activation", kraw[:, :],
                                 psqk[slot][:, :], AF.Identity, bias=bias,
                                 sem=ss)
                            kv = p.cnt[ss]
                            slot_rd[slot] = kv
                            p.wait("tensor", ss, kv)
                            p.op("tensor", "matmul", psw[:, :], p.m2k_s[:, :],
                                 kraw[:, :], start=True, stop=True, sem=sm)
                            swv = p.cnt[sm]
                            p.wait("scalar", sm, swv)
                            p.op("scalar", "copy", ksw[:, :], psw[:, :],
                                 sem=ss)
                            kxv = p.cnt[ss]
                            hg, half = oc // 2, oc % 2
                            p.wait("sync", ss, kxv)
                            p.dma("sync",
                                  p.kc_d[hg].ap()[half * 128:(half + 1) * 128,
                                                  tc * 512:(tc + 1) * 512],
                                  ksw[:, :], tag + "kv%d" % hg)
                            p.wait("scalar", tag + "kv%d" % hg,
                                   p.cnt[tag + "kv%d" % hg])
                if j + 2 < len(wq_jobs):
                    nm2, oc2 = wq_jobs[j + 2]
                    p.wait("gpsimd", sm, p.cnt[sm])
                    src = W[nm2][:, oc2 * 128:(oc2 + 1) * 128].rearrange(
                        "(c p) n -> p c n", p=128)
                    wrdy[j + 2] = p.dma("gpsimd", wsl[j % 2][:, :].rearrange(
                        "p (c n) -> p c n", n=128), src, wsem[j % 2])

        with ExitStack() as st2:
            psv = [st2.enter_context(nc.psum_tensor(tag + "pv%d" % i,
                                                    [128, 512], F32))
                   for i in range(2)]
            wvh = [st2.enter_context(nc.sbuf_tensor(tag + "_wvh%d" % i,
                                                    [128, 8 * 512], BF16))
                   for i in range(2)]
            vsem = [tag + "vA", tag + "vB"]
            vrdy = {}
            p.wait("gpsimd", ss, p.cnt[ss])
            p.wait("gpsimd", sv, p.cnt[sv])
            p.wait("gpsimd", sm, p.cnt[sm])
            for g in range(4):
                if (tag + "kv%d" % g) in p.cnt:
                    p.wait("gpsimd", tag + "kv%d" % g, p.cnt[tag + "kv%d" % g])
            for j in (0, 1):
                src = W["Wv"][:, j * 512:(j + 1) * 512].rearrange(
                    "(c p) n -> p c n", p=128)
                vrdy[j] = p.dma("gpsimd", wvh[j][:, :].rearrange(
                    "p (c n) -> p c n", n=512), src, vsem[j])
            p.wait("tensor", tag + "bv", bv_v)
            p.wait("tensor", ss, p.cnt[ss])
            p.wait("tensor", sv, p.cnt[sv])
            for g in range(4):
                if (tag + "kv%d" % g) in p.cnt:
                    p.wait("tensor", tag + "kv%d" % g, p.cnt[tag + "kv%d" % g])
            slot_rd = [None, None]
            for oh in range(2):
                p.wait("tensor", vsem[oh], vrdy[oh])
                for m in range(16):
                    slot = m % 2
                    if slot_rd[slot] is not None:
                        p.wait("tensor", ss, slot_rd[slot])
                    for cc in range(NCH):
                        p.op("tensor", "matmul", psv[slot][:, :],
                             p.u_s[:, cc * NT + m * 128: cc * NT + (m + 1) * 128],
                             wvh[oh][:, cc * 512:(cc + 1) * 512],
                             start=(cc == 0), stop=False)
                    p.op("tensor", "matmul", psv[slot][:, :],
                         p.ones_s[0:1, :],
                         bvr_s[0:1, oh * 512:(oh + 1) * 512],
                         start=False, stop=True, sem=sm)
                    mmv = p.cnt[sm]
                    p.wait("scalar", sm, mmv)
                    p.op("scalar", "copy", vstg[:, :], psv[slot][:, :], sem=ss)
                    vv = p.cnt[ss]
                    slot_rd[slot] = vv
                    p.wait("sync", ss, vv)
                    for hh in range(2):
                        hg = oh * 2 + hh
                        p.dma("sync",
                              p.vc_d[hg].ap()[m * 128:(m + 1) * 128, :],
                              vstg[:, hh * 256:(hh + 1) * 256],
                              tag + "kv%d" % hg)
                    for hh in range(2):
                        hgx = oh * 2 + hh
                        p.wait("scalar", tag + "kv%d" % hgx,
                               p.cnt[tag + "kv%d" % hgx])

        for hg in range(4):
            p.wait("gpsimd", tag + "kv%d" % hg, p.cnt[tag + "kv%d" % hg])
            p.op("gpsimd", "collective_compute",
                 "AllGather", ALU.bypass,
                 replica_groups=[list(range(8))],
                 ins=[p.kc_d[hg].ap().opt()], outs=[p.gk_d[hg].ap().opt()],
                 sem=scc)
            p.op("gpsimd", "collective_compute",
                 "AllGather", ALU.bypass,
                 replica_groups=[list(range(8))],
                 ins=[p.vc_d[hg].ap().opt()], outs=[p.gv_d[hg].ap().opt()],
                 sem=scc)
        cc_base = p.cnt[scc] - 8

        khg = [sb("khg0", [128, 2 * T], BF16)]
        vhg = [sb("vhg0", [128, 4 * 32 * 65], BF16)]
        ohgb = sb("ohgb", [128, 2 * 512], BF16)
        ohgb_raw = ohgb
        sig_s = sb("sig", [1, 2048], BF16)

        pbuf = [sb("pb%d" % i, [128, 1024], BF16) for i in range(2)]
        wp_s = sb("wp", [128, 2 * 1024], BF16)
        p.wait("gpsimd", ss, p.cnt[ss])
        p.wait("gpsimd", sm, p.cnt[sm])
        mz1 = p.op("gpsimd", "memset", vhg[0][:, :], 1.0)
        p.wait("sync", "gch", mz1)

        with ExitStack() as st2:
            psS = [st2.enter_context(nc.psum_tensor(tag + "pS%d" % i,
                                                    [128, 1024], F32))
                   for i in range(2)]
            psO = [st2.enter_context(nc.psum_tensor(tag + "pO%d" % i,
                                                    [65, 512], F32))
                   for i in range(2)]
            psP = [st2.enter_context(nc.psum_tensor(tag + "pP%d" % i,
                                                    [128, 512], F32))
                   for i in range(2)]
            stg = [tag + "sA", tag + "sB"]
            wpsem = [tag + "pA2", tag + "pB2"]

            def stage(i, hg, b):
                sem = stg[0]
                gk = p.gk_d[hg].ap()
                gv = p.gv_d[hg].ap()
                n0 = p.cnt.get(sem, 0)
                for r in range(8):
                    for hp in range(2):
                        src = gk[256 * r + 128 * hp: 256 * r + 128 * (hp + 1),
                                 b * 512:(b + 1) * 512]
                        src = src.rearrange("p (t n) -> p t n", n=128)
                        dst = khg[0][:, hp * T: (hp + 1) * T].rearrange(
                            "p (t n) -> p t n", n=128)[:, r::8, :]
                        p.dma("sync", dst, src, sem)
                    srcv = gv[2048 * r + 512 * b: 2048 * r + 512 * (b + 1), :]
                    srcv = srcv.rearrange("(t p) (h d) -> p h t d", p=128, d=64)
                    dstv = vhg[0][:, :].rearrange(
                        "p (h k e) -> p h k e", h=4, e=65)[:, :, r::8, 0:64]
                    p.dma("sync", dstv, srcv, sem)
                return p.cnt[sem]

            def wp_load(hg):
                src = W["Wp"][2 * hg * 128:(2 * hg + 2) * 128, :].rearrange(
                    "(c p) n -> p c n", p=128)
                return p.dma("gpsimd", wp_s[:, :].rearrange(
                    "p (c n) -> p c n", n=1024), src, wpsem[0])

            hgb = [(hg, b) for hg in range(4) for b in range(B)]
            stg_rdy = {}
            wp_rdy = {}
            reg_rd = [0, 0]
            reg_av = [0, 0]
            po_free = [0, 0]
            psS_guard = 0
            wp_rdy[0] = wp_load(0)
            last_proj_dve = None
            pe_prev = 0
            for i, (hg, b) in enumerate(hgb):
                if i == 0:
                    p.wait("sync", scc, cc_base + 2)
                    stg_rdy[0] = stage(0, hg, b)
                p.wait("tensor", stg[0], stg_rdy[i])
                if b == 0:
                    p.wait("tensor", wpsem[0], wp_rdy[hg])
                kb, vb = khg[0], vhg[0]
                wpb = wp_s
                def tile_regions():
                    regs, cur, off = [], [], 0
                    for kt in range(32):
                        N = 512 - 128 * (kt // 8)
                        if off % 512 + N > 512 and off % 512 != 0:
                            off = (off // 512 + 1) * 512
                        if off + N > 1024:
                            regs.append(cur)
                            cur, off = [], 0
                        cur.append((kt, off, N))
                        off += N
                    regs.append(cur)
                    return regs
                TREG = tile_regions()
                R = [(hq, ri) for hq in range(4) for ri in range(len(TREG))]
                score_mm, exp_rd, msk_rd, rcps = {}, {}, {}, {}

                def emit_scores(j):
                    hq, ri = R[j]
                    reg = j % 2
                    hb = (hq % 2) * 64
                    hp = hq // 2
                    qbase = (hg * 2 + hp) * NT + b * 512
                    if reg_rd[reg]:
                        p.wait("tensor", ss, reg_rd[reg])
                    if reg == 0 and psS_guard:
                        p.wait("tensor", sv, psS_guard)
                    for (kt, off, N) in TREG[ri]:
                        q0 = 128 * (kt // 8)
                        p.op("tensor", "matmul",
                             psS[reg][:, off:off + N],
                             kb[hb:hb + 64,
                                hp * T + kt * 128: hp * T + (kt + 1) * 128],
                             q_s[hb:hb + 64, qbase + q0: qbase + 512],
                             start=(off % 512 == 0),
                             stop=((kt, off, N) == TREG[ri][-1]),
                             sem=(sm if (kt, off, N) == TREG[ri][-1] else None))
                    score_mm[j] = p.cnt[sm]

                def emit_exp(j):
                    hq, ri = R[j]
                    reg = j % 2
                    p.wait("scalar", sm, score_mm[j])
                    if reg_av[reg]:
                        p.wait("scalar", sm, reg_av[reg])
                    p.op("scalar", "activation", pbuf[reg][:, :],
                         psS[reg][:, :], AF.Exp,
                         bias=p.cst_s[:, 0:1], scale=SCALE, sem=ss)
                    reg_rd[reg] = p.cnt[ss]
                    exp_rd[j] = reg_rd[reg]
                    p.wait("vector", ss, exp_rd[j])
                    tiles = TREG[ri]
                    nt = len(tiles)
                    step = tiles[1][1] - tiles[0][1] if nt > 1 else 128
                    c0 = tiles[0][0] % 8
                    pv = pbuf[reg][:, 0:nt * step].rearrange(
                        "p (t n) -> p t n", n=step)[:, :, 0:128]
                    mkv = p.mask_s[:, c0 * 128:(c0 + nt) * 128].rearrange(
                        "p (t n) -> p t n", n=128)
                    msk_rd[j] = p.op("vector", "tensor_tensor", pv, pv, mkv,
                                     op=ALU.mult)

                def emit_av(j):
                    hq, ri = R[j]
                    reg = j % 2
                    po = psO[hq % 2]
                    p.wait("tensor", ss, exp_rd[j])
                    p.wait("tensor", sv, msk_rd[j])
                    if ri == 0 and po_free[hq % 2]:
                        p.wait("tensor", ss, po_free[hq % 2])
                    for (kt, off, N) in TREG[ri]:
                        q0 = 128 * (kt // 8)
                        p.op("tensor", "matmul", po[:, q0:512],
                             vb[:, (hq * 32 + kt) * 65:
                                (hq * 32 + kt) * 65 + 65],
                             pbuf[reg][:, off:off + N],
                             start=(kt == 0),
                             stop=(ri == len(TREG) - 1 and
                                   (kt, off, N) == TREG[ri][-1]),
                             sem=(sm if (kt, off, N) == TREG[ri][-1] else None))
                    reg_av[reg] = p.cnt[sm]
                    if ri == len(TREG) - 1:
                        hb = (hq % 2) * 64
                        hp = hq // 2
                        p.wait("scalar", sm, reg_av[reg])
                        p.op("scalar", "copy",
                             ohgb_raw[hb:hb + 64, hp * 512:(hp + 1) * 512],
                             po[0:64, 0:512])
                        p.op("scalar", "copy",
                             sig_s[0:1, hq * 512:(hq + 1) * 512],
                             po[64:65, 0:512], sem=ss)
                        po_free[hq % 2] = p.cnt[ss]
                        p.wait("vector", ss, po_free[hq % 2])
                        rcps[hq] = p.op("vector", "reciprocal",
                                        sig_s[0:1, hq * 512:(hq + 1) * 512],
                                        sig_s[0:1, hq * 512:(hq + 1) * 512],
                                        lp=True)

                emit_scores(0)
                emit_exp(0)
                if len(R) > 1:
                    emit_scores(1)
                    emit_exp(1)
                for j in range(len(R)):
                    emit_av(j)
                    if j + 2 < len(R):
                        emit_scores(j + 2)
                        emit_exp(j + 2)
                av_end = p.cnt[sm]

                if i + 1 < len(hgb):
                    hg2, b2 = hgb[i + 1]
                    p.wait("sync", scc, cc_base + 2 * (hg2 + 1))
                    p.wait("sync", sm, av_end)
                    stg_rdy[i + 1] = stage(i + 1, hg2, b2)

                sig_done = p.cnt[ss]
                p.wait("vector", ss, sig_done)
                p.wait("tensor", ss, max(reg_rd))
                if psS_guard:
                    p.wait("tensor", sv, psS_guard)
                p.wait("tensor", sv, p.cnt[sv])
                nv = 0
                for hq in range(4):
                    hb = (hq % 2) * 64
                    hp = hq // 2
                    p.wait("tensor", sv, rcps[hq])
                    if nv:
                        p.wait("tensor", sv, nv)
                    p.op("tensor", "matmul",
                         psP[0][hb:hb + 64, 0:512],
                         p.ones_s[0:1, 0:64],
                         sig_s[0:1, hq * 512:(hq + 1) * 512],
                         start=True, stop=True, sem=sm)
                    bcv = p.cnt[sm]
                    p.wait("vector", sm, bcv)
                    nv = p.op("vector", "tensor_tensor",
                              ohgb[hb:hb + 64, hp * 512:(hp + 1) * 512],
                              ohgb_raw[hb:hb + 64, hp * 512:(hp + 1) * 512],
                              psP[0][hb:hb + 64, 0:512], op=ALU.mult)
                psS_guard = 0
                if L == 1:
                    p.wait("tensor", sv, nv)
                    for half in range(2):
                        p.op("tensor", "matmul",
                             psS[1][:, half * 512:(half + 1) * 512],
                             p.m2v_s[:, :], ohgb[:, half * 512:(half + 1) * 512],
                             start=True, stop=True,
                             sem=(sm if half == 1 else None))
                    mvv = p.cnt[sm]
                    p.wait("scalar", sm, mvv)
                    p.op("scalar", "copy", ohgb[:, :], psS[1][:, :], sem=ss)
                reg_rd = [p.cnt[ss], p.cnt[ss]]
                p.wait("tensor", sv, p.cnt[sv])
                p.wait("tensor", ss, p.cnt[ss])
                for oc in range(NCH):
                    pp = psP[oc % 2]
                    for hp in range(2):
                        p.op("tensor", "matmul", pp[:, :],
                             wpb[:, hp * 1024 + oc * 128: hp * 1024 + (oc + 1) * 128],
                             ohgb[:, hp * 512:(hp + 1) * 512],
                             start=(hp == 0), stop=(hp == 1),
                             sem=(sm if hp == 1 else None))
                    mmv = p.cnt[sm]
                    xsl = p.x_s[:, oc * NT + b * 512: oc * NT + (b + 1) * 512]
                    if hg == 0:
                        p.wait("scalar", sm, mmv)
                        p.op("scalar", "activation", vstg[:, :], pp[:, :],
                             AF.Identity, bias=bp_s[:, oc:oc + 1], sem=ss)
                        prv = p.cnt[ss]
                        p.wait("vector", ss, prv)
                        last_proj_dve = p.op("vector", "tensor_tensor", xsl,
                                             xsl, vstg[:, :], op=ALU.add)
                        p.wait("scalar", sv, last_proj_dve)
                    else:
                        p.wait("vector", sm, mmv)
                        last_proj_dve = p.op("vector", "tensor_tensor", xsl,
                                             xsl, pp[:, :], op=ALU.add)
                    p.wait("tensor", sv, last_proj_dve)
                if b == B - 1 and hg + 1 < 4:
                    p.wait("gpsimd", sm, p.cnt[sm])
                    wp_rdy[hg + 1] = wp_load(hg + 1)
    return (sv, p.cnt[sv])


def bias_cols(b):
    return np.ascontiguousarray(np.asarray(b, np.float32).reshape(8, 128).T)


def attn_host_inputs(L, core, inputs):
    tag = "a%d" % L
    d = {}
    if L == 0:
        d[tag + "Wq"] = np.ascontiguousarray(inputs["L0_attn_Wq"])
        d[tag + "Wk"] = np.ascontiguousarray(inputs["L0_attn_Wk"])
        d[tag + "Wv"] = np.ascontiguousarray(inputs["L0_attn_Wv"])
        d[tag + "Wp"] = np.ascontiguousarray(inputs["L0_attn_Wp"])
        d[tag + "bq"] = bias_cols(inputs["L0_attn_bq"])
        d[tag + "bk"] = bias_cols(inputs["L0_attn_bk"])
        d[tag + "bp"] = bias_cols(inputs["L0_attn_bp"])
        d[tag + "bvr"] = np.asarray(inputs["L0_attn_bv"], np.float32)[None, :]
        pk = ((8 * np.arange(4)[:, None] + core) * 128 +
              np.arange(128)[None, :]).reshape(-1).astype(np.float32)
        pos = np.tile(pk, 4)
        dd = np.arange(128) % 64
        inv = (10000.0 ** (-2.0 * (dd // 2) / 64.0)).astype(np.float32)
        ang = pos[None, :] * inv[:, None]
        sign = np.where(dd % 2 == 0, -1.0, 1.0).astype(np.float32)
        d["cos_t"] = np.cos(ang).astype(BF)
        d["sin_t"] = (sign[:, None] * np.sin(ang)).astype(BF)
        pm = np.zeros((128, 128), np.float32)
        pm[np.arange(128), np.arange(128) ^ 1] = 1.0
        d["perm"] = pm.astype(BF)
    else:
        w = np.asarray(inputs["L1_attn_c_attn_w"], np.float32)
        bb = np.asarray(inputs["L1_attn_c_attn_b"], np.float32)
        d[tag + "Wq"] = np.ascontiguousarray(w[:, 0:1024])
        d[tag + "Wk"] = np.ascontiguousarray(w[:, 1024:2048])
        d[tag + "Wv"] = np.ascontiguousarray(w[:, 2048:3072])
        d[tag + "Wp"] = np.ascontiguousarray(inputs["L1_attn_c_proj_w"])
        d[tag + "bq"] = bias_cols(bb[0:1024])
        d[tag + "bk"] = bias_cols(bb[1024:2048])
        d[tag + "bp"] = bias_cols(inputs["L1_attn_c_proj_b"])
        d[tag + "bvr"] = bb[2048:3072][None, :].copy()
        d["wkcT"] = np.ascontiguousarray(inputs["L1_attn_wk_c"].T)
        d["wke"] = np.ascontiguousarray(inputs["L1_attn_wk_e"])
        d["wvcT"] = np.ascontiguousarray(inputs["L1_attn_wv_c"].T)
        d["wve"] = np.ascontiguousarray(inputs["L1_attn_wv_e"])
    ki = np.arange(128)[:, None]
    qi = np.arange(128)[None, :]
    mk = np.ones((128, 8 * 128), np.float32)
    for cc in range(8):
        if cc == core:
            mk[:, cc * 128:(cc + 1) * 128] = np.where(ki <= qi, 1.0, 0.0)
        elif cc > core:
            mk[:, cc * 128:(cc + 1) * 128] = 0.0
    d["maskt"] = mk.astype(BF)
    return d


def build_attn_launch(L):
    p = Prog()
    emit_prelude(p)
    xd = emit_load_x(p)
    pd = emit_attn_prep(p, L)
    ud = emit_ln(p, "ln", [xd, pd])
    done = emit_attn(p, "a%d" % L, L, ud, pd)
    emit_store_x(p, [done])
    return p.finalize()


# version 07 (final)
# speedup vs baseline: 1.0929x; 1.0006x over previous
import numpy as np
from contextlib import ExitStack
import concourse.bass as bass
import concourse.mybir as mybir
from concourse.bass_utils import run_bass_kernel_spmd
import ml_dtypes

F32 = mybir.dt.float32
BF16 = mybir.dt.bfloat16
AF = mybir.ActivationFunctionType
ALU = mybir.AluOpType
AX = mybir.AxisListType
BF = ml_dtypes.bfloat16

C = 1024
B = 4
T = 4096
NT = 2048
NCH = 8
NEG = -1e33
EXPB = -20.0
SCALE = 0.125


class Prog:

    ENGINES = ("sync", "gpsimd", "vector", "scalar", "tensor")

    def __init__(self):
        self.nc = bass.Bass("TRN2", target_bir_lowering=False, debug=False)
        self.es = ExitStack()
        self.alloc_scope = self.es
        self.q = {e: [] for e in self.ENGINES}
        self.cnt = {}
        self.sems = {}

    def sem(self, name):
        if name not in self.sems:
            self.sems[name] = self.es.enter_context(self.nc.semaphore(name))
            self.cnt[name] = 0
        return self.sems[name]

    def inc(self, name, k=1):
        self.sem(name)
        self.cnt[name] += k
        return self.sems[name], self.cnt[name]

    def emit(self, eng, fn):
        self.q[eng].append(fn)

    def wait(self, eng, sem_name, v):
        s = self.sem(sem_name)
        self.emit(eng, lambda e, s=s, v=v: e.wait_ge(s, v))

    CHAINED = {"vector": "vch", "scalar": "sch", "gpsimd": "gch"}

    def op(self, eng, meth, *args, sem=None, inc=1, **kw):
        ch = self.CHAINED.get(eng)
        if ch is not None and meth not in ("dma_start", "collective_compute"):
            assert sem is None or sem == ch, (eng, meth, sem)
            s = self.sem(ch)
            prev = self.cnt[ch]
            self.cnt[ch] += 1

            lp = kw.pop("lp", False)

            def fn(e, a=args, k=kw, m=meth, s=s, prev=prev, lp=lp):
                if prev > 0:
                    e.wait_ge(s, prev)
                if lp:
                    with self.nc.allow_low_precision(reason="bf16 aux math"):
                        getattr(e, m)(*a, **k).then_inc(s, 1)
                else:
                    getattr(e, m)(*a, **k).then_inc(s, 1)
            self.emit(eng, fn)
            return self.cnt[ch]
        if sem is not None:
            s, v = self.inc(sem, inc)
            self.emit(eng, lambda e, a=args, k=kw, s=s, m=meth, i=inc:
                      getattr(e, m)(*a, **k).then_inc(s, i))
            return v
        self.emit(eng, lambda e, a=args, k=kw, m=meth: getattr(e, m)(*a, **k))
        return None

    def dma(self, eng, out, in_, sem):
        return self.op(eng, "dma_start", sem=sem, inc=16, out=out, in_=in_)

    def sb(self, name, shape, dt):
        return self.alloc_scope.enter_context(self.nc.sbuf_tensor(name, shape, dt))

    def sb_perm(self, name, shape, dt):
        return self.es.enter_context(self.nc.sbuf_tensor(name, shape, dt))

    def ps(self, name, shape, dt=F32):
        return self.es.enter_context(self.nc.psum_tensor(name, shape, dt))

    def inp(self, name, shape, dt):
        return self.nc.dram_tensor(name, shape, dt, kind="ExternalInput").ap()

    def outp(self, name, shape, dt):
        return self.nc.dram_tensor(name, shape, dt, kind="ExternalOutput").ap()

    def finalize(self):
        qq = self.q
        with self.nc.Block() as block:
            @block.sync
            def _(e):
                for f in qq["sync"]:
                    f(e)

            @block.gpsimd
            def _(e):
                for f in qq["gpsimd"]:
                    f(e)

            @block.vector
            def _(e):
                for f in qq["vector"]:
                    f(e)

            @block.scalar
            def _(e):
                for f in qq["scalar"]:
                    f(e)

            @block.tensor
            def _(e):
                for f in qq["tensor"]:
                    f(e)
        self.es.close()
        return self.nc


def emit_prelude(p):
    p.x_s = p.sb("x_s", [128, NCH * NT], F32)
    p.u_s = p.sb("u_s", [128, NCH * NT], BF16)
    p.cst_s = p.sb("cst_s", [128, 8], F32)
    p.ones_s = p.sb("ones_s", [128, 128], BF16)
    p.onesf_s = p.sb("onesf_s", [1, 128], F32)
    p.identb_s = p.sb("identb_s", [128, 128], BF16)
    p.identf_s = p.sb("identf_s", [128, 128], F32)
    p.mask_s = p.sb("mask_s", [128, 8 * 128], BF16)
    cst = p.inp("cst", [128, 8], F32)
    ones = p.inp("onesb", [128, 128], BF16)
    onesf = p.inp("onesf", [1, 128], F32)
    identb = p.inp("identb", [128, 128], BF16)
    identf = p.inp("identf", [128, 128], F32)
    p.dma("sync", p.cst_s[:, :], cst, "ldc")
    p.dma("sync", p.ones_s[:, :], ones, "ldc")
    p.dma("sync", p.identb_s[:, :], identb, "ldc")
    p.dma("sync", p.identf_s[:, :], identf, "ldc")
    p.ldc_v = p.dma("sync", p.onesf_s[:, :], onesf, "ldc")


def host_consts():
    cst = np.zeros((128, 8), np.float32)
    cst[:, 0] = EXPB
    cst[:, 1] = 1e-5
    cst[:, 2] = 0.5
    cst[:, 3] = 1.0
    return {"cst": cst,
            "onesb": np.ones((128, 128), BF),
            "onesf": np.ones((1, 128), np.float32),
            "identb": np.eye(128, dtype=BF),
            "identf": np.eye(128, dtype=np.float32)}


def emit_load_x(p):
    x_in = p.inp("x", [128, NCH * NT], F32)
    v = p.dma("sync", p.x_s[:, :], x_in, "ldx")
    return ("ldx", v)


def emit_store_x(p, deps):
    x_out = p.outp("y", [128, NCH * NT], F32)
    for s, v in deps:
        p.wait("sync", s, v)
    v = p.dma("sync", x_out, p.x_s[:, :], "stx")
    p.wait("sync", "stx", v)


def emit_ln(p, tag, deps):
    nc = p.nc
    sv, ss, sm = "vch", "sch", "mm"
    with ExitStack() as st:
        xbf = st.enter_context(nc.sbuf_tensor(tag + "xbf", [128, NCH * NT], BF16))
        xsq = st.enter_context(nc.sbuf_tensor(tag + "xsq", [128, NCH * NT], BF16))
        rows = st.enter_context(nc.sbuf_tensor(tag + "rw", [1, 1024], F32))
        pstA = st.enter_context(nc.psum_tensor(tag + "pA", [1, 512], F32))
        pstB = st.enter_context(nc.psum_tensor(tag + "pB", [1, 512], F32))
        pbcM = st.enter_context(nc.psum_tensor(tag + "pM", [128, 512], F32))
        pbcR = st.enter_context(nc.psum_tensor(tag + "pR", [128, 512], F32))

        for s, v in deps:
            p.wait("vector", s, v)
            p.wait("scalar", s, v)
            p.wait("tensor", s, v)
        p.wait("vector", "ldc", p.ldc_v)
        p.wait("scalar", "ldc", p.ldc_v)
        p.wait("tensor", "ldc", p.ldc_v)

        for cc in range(NCH):
            xc = p.x_s[:, cc * NT:(cc + 1) * NT]
            p.op("vector", "tensor_copy", xbf[:, cc * NT:(cc + 1) * NT], xc,
                 sem=sv)
            p.op("scalar", "activation", xsq[:, cc * NT:(cc + 1) * NT], xc,
                 AF.Square, sem=ss)
        vcop, scop = p.cnt[sv], p.cnt[ss]
        p.wait("tensor", sv, vcop)
        p.wait("tensor", ss, scop)

        for tc in range(4):
            for cc in range(NCH):
                off = cc * NT + tc * 512
                p.op("tensor", "matmul", pstA[:, :], p.ones_s[:, 0:1],
                     xbf[:, off:off + 512], start=(cc == 0),
                     stop=(cc == NCH - 1), sem=sm)
                p.op("tensor", "matmul", pstB[:, :], p.ones_s[:, 0:1],
                     xsq[:, off:off + 512], start=(cc == 0),
                     stop=(cc == NCH - 1), sem=sm)
            mmv = p.cnt[sm]
            p.wait("vector", sm, mmv)
            p.op("vector", "tensor_scalar", rows[0:1, 0:512], pstA[0:1, :],
                 1.0 / C, None, op0=ALU.mult, sem=sv)
            p.op("vector", "tensor_tensor", rows[0:1, 512:1024],
                 rows[0:1, 0:512], rows[0:1, 0:512], op=ALU.mult, sem=sv)
            p.op("vector", "scalar_tensor_tensor", rows[0:1, 512:1024],
                 pstB[0:1, :], 1.0 / C, rows[0:1, 512:1024],
                 op0=ALU.mult, op1=ALU.subtract, sem=sv)
            vv = p.cnt[sv]
            p.wait("scalar", sv, vv)
            p.op("scalar", "activation", rows[0:1, 512:1024],
                 rows[0:1, 512:1024], AF.Ln, bias=p.cst_s[0:1, 1:2], sem=ss)
            p.op("scalar", "activation", rows[0:1, 512:1024],
                 rows[0:1, 512:1024], AF.Exp, scale=-0.5, sem=ss)
            sv2 = p.cnt[ss]
            p.wait("tensor", ss, sv2)
            p.op("tensor", "matmul", pbcM[:, :], p.onesf_s[0:1, :],
                 rows[0:1, 0:512], start=True, stop=True, sem=sm)
            p.op("tensor", "matmul", pbcR[:, :], p.onesf_s[0:1, :],
                 rows[0:1, 512:1024], start=True, stop=True, sem=sm)
            mmv2 = p.cnt[sm]
            p.wait("vector", sm, mmv2)
            for cc in range(NCH):
                off = cc * NT + tc * 512
                xc = p.x_s[:, off:off + 512]
                p.op("vector", "tensor_tensor", xbf[:, off:off + 512], xc,
                     pbcM[:, :], op=ALU.subtract)
                p.op("vector", "tensor_tensor", p.u_s[:, off:off + 512],
                     xbf[:, off:off + 512], pbcR[:, :], op=ALU.mult, sem=sv)
            uvv = p.cnt[sv]
            if tc < 3:
                p.wait("tensor", sv, uvv)
    return (sv, p.cnt[sv])




def emit_ffn0_weff(p):
    nc = p.nc
    tag = "wf"
    sv, ss, sm = "vch", "sch", "mm"
    p.weff_s = p.sb("weff_s", [128, NCH * 1024], BF16)
    p.bffn_s = p.sb("bffn_s", [128, 8], F32)
    with ExitStack() as st:
        wt = st.enter_context(nc.sbuf_tensor(tag + "wt", [128, 1024], F32))
        ldt = st.enter_context(nc.sbuf_tensor(tag + "ldt", [100, 4], F32))
        dax = st.enter_context(nc.sbuf_tensor(tag + "dax", [100, 1024], F32))
        lut = st.enter_context(nc.sbuf_tensor(tag + "lut", [4, 50], F32))
        uat = st.enter_context(nc.sbuf_tensor(tag + "uat", [50, 1024], F32))
        aA = st.enter_context(nc.sbuf_tensor(tag + "A", [4, 1024], F32))
        bB = st.enter_context(nc.sbuf_tensor(tag + "B", [50, 1024], F32))
        bcol = st.enter_context(nc.sbuf_tensor(tag + "bc", [128, 8], F32))
        psA = st.enter_context(nc.psum_tensor(tag + "psA", [4, 1024], F32))
        psB = st.enter_context(nc.psum_tensor(tag + "psB", [50, 1024], F32))
        psC = st.enter_context(nc.psum_tensor(tag + "psC", [128, 1024], F32))

        wt_in = p.inp("ffnWT", [1024, 1024], F32)
        b_in = p.inp("ffnb", [128, 8], F32)
        ldt_in = p.inp("ffnldT", [100, 4], F32)
        dax_in = p.inp("ffndax", [100, 1024], F32)
        lut_in = p.inp("ffnluT", [4, 50], F32)
        uat_in = p.inp("ffnuaT", [50, 1024], F32)

        p.dma("sync", ldt[:, :], ldt_in, "ldw")
        p.dma("sync", dax[:, :], dax_in, "ldw")
        p.dma("sync", lut[:, :], lut_in, "ldw")
        p.dma("sync", bcol[:, :], b_in, "ldw")
        v0 = p.dma("sync", uat[:, :], uat_in, "ldw")
        p.wait("tensor", "ldw", v0)
        for half in range(2):
            p.op("tensor", "matmul", psA[:, half * 512:(half + 1) * 512],
                 ldt[:, :], dax[:, half * 512:(half + 1) * 512],
                 start=True, stop=True, sem=sm)
        p.wait("vector", sm, p.cnt[sm])
        p.op("vector", "tensor_copy", aA[:, :], psA[:, :], sem=sv)
        p.op("vector", "tensor_scalar", p.bffn_s[:, :], bcol[:, :], 0.5, None,
             op0=ALU.mult, sem=sv)
        p.wait("tensor", sv, p.cnt[sv])
        for half in range(2):
            p.op("tensor", "matmul", psB[:, half * 512:(half + 1) * 512],
                 lut[:, :], aA[:, half * 512:(half + 1) * 512],
                 start=True, stop=True, sem=sm)
        p.wait("vector", sm, p.cnt[sm])
        p.op("vector", "tensor_copy", bB[:, :], psB[:, :], sem=sv)
        bb_v = p.cnt[sv]
        p.wait("tensor", sv, bb_v)
        for cc in range(NCH):
            wv = p.dma("sync", wt[:, :],
                       wt_in[cc * 128:(cc + 1) * 128, :], "ldw")
            for half in range(2):
                p.op("tensor", "matmul", psC[:, half * 512:(half + 1) * 512],
                     bB[:, cc * 128:(cc + 1) * 128],
                     uat[:, half * 512:(half + 1) * 512],
                     start=True, stop=True, sem=sm)
            mmv = p.cnt[sm]
            p.wait("vector", sm, mmv)
            p.wait("vector", "ldw", wv)
            p.op("vector", "tensor_tensor", p.weff_s[:, cc * 1024:(cc + 1) * 1024],
                 psC[:, :], wt[:, :], op=ALU.add, sem=sv)
            if cc < NCH - 1:
                p.wait("tensor", sv, p.cnt[sv])
                p.wait("sync", sv, p.cnt[sv])
    return (sv, p.cnt[sv])


def emit_ffn0_apply(p, tag, u_dep, weff_dep):
    sv, ss, sm = "vch", "sch", "mm"
    nc = p.nc
    with ExitStack() as st:
        tmp = st.enter_context(nc.sbuf_tensor(tag + "t", [128, 4 * 512], F32))
        pso = [st.enter_context(nc.psum_tensor(tag + "p%d" % i, [128, 512], F32))
               for i in range(4)]
        p.wait("tensor", u_dep[0], u_dep[1])
        p.wait("tensor", weff_dep[0], weff_dep[1])
        p.wait("scalar", weff_dep[0], weff_dep[1])
        slot_act = [None] * 4
        slot_dve = [None] * 4
        nslot = 0
        for oc in range(NCH):
            for tc in range(4):
                slot = nslot % 4
                nslot += 1
                psu = pso[slot]
                if slot_act[slot] is not None:
                    p.wait("tensor", ss, slot_act[slot])
                for cc in range(NCH):
                    p.op("tensor", "matmul", psu[:, :],
                         p.weff_s[:, cc * 1024 + oc * 128: cc * 1024 + (oc + 1) * 128],
                         p.u_s[:, cc * NT + tc * 512: cc * NT + (tc + 1) * 512],
                         start=(cc == 0), stop=(cc == NCH - 1),
                         sem=(sm if cc == NCH - 1 else None))
                mmv = p.cnt[sm]
                p.wait("scalar", sm, mmv)
                if slot_dve[slot] is not None:
                    p.wait("scalar", sv, slot_dve[slot])
                tsl = tmp[:, slot * 512:(slot + 1) * 512]
                p.op("scalar", "activation", tsl, psu[:, :], AF.Identity,
                     bias=p.bffn_s[:, oc:oc + 1], scale=0.5, sem=ss)
                slot_act[slot] = p.cnt[ss]
                p.wait("vector", ss, slot_act[slot])
                xsl = p.x_s[:, oc * NT + tc * 512: oc * NT + (tc + 1) * 512]
                p.op("vector", "tensor_tensor", xsl, xsl, tsl, op=ALU.add,
                     sem=sv)
                slot_dve[slot] = p.cnt[sv]
    return (sv, p.cnt[sv])


def shard_x(x):
    xs = []
    xr = np.ascontiguousarray(x).reshape(B, 32, 128, C)
    for c in range(8):
        v = xr[:, c::8]
        v = v.transpose(3, 0, 1, 2).reshape(C, NT)
        v = v.reshape(NCH, 128, NT).transpose(1, 0, 2).reshape(128, NCH * NT)
        xs.append(np.ascontiguousarray(v))
    return xs


def unshard_x(ys):
    out = np.empty((B, 32, 128, C), np.float32)
    for c in range(8):
        v = ys[c].reshape(128, NCH, NT).transpose(1, 0, 2).reshape(C, NT)
        v = v.reshape(C, B, 4, 128).transpose(1, 2, 3, 0)
        out[:, c::8] = v
    return out.reshape(B, T, C)


def ffn0_host_inputs(W, b, lora_down, lora_up, down_aux, up_aux):
    return {
        "ffnWT": np.ascontiguousarray(W.T),
        "ffnb": np.ascontiguousarray(np.asarray(b, np.float32)
                                     .reshape(NCH, 128).T),
        "ffnldT": np.ascontiguousarray(lora_down.T),
        "ffndax": np.ascontiguousarray(down_aux),
        "ffnluT": np.ascontiguousarray(lora_up.T),
        "ffnuaT": np.ascontiguousarray(up_aux.T),
    }


def moe_host_inputs(inputs):
    return {
        "centT": np.ascontiguousarray(inputs["L1_ffn_centroids"].T),
        "ebias": np.ascontiguousarray(
            np.asarray(inputs["L1_ffn_expert_biases"], np.float32)[None, :]),
        "brT": np.ascontiguousarray(inputs["L1_ffn_br"]),
        "bs2": np.ascontiguousarray(inputs["L1_ffn_bs"]),
        "half2": np.full((2, 1), 0.5, np.float32),
        "wsT": np.ascontiguousarray(inputs["L1_ffn_Ws"].transpose(0, 2, 1)),
        "wrT": np.ascontiguousarray(inputs["L1_ffn_Wr"].transpose(0, 2, 1)),
    }


def build_ffn0_launch():
    p = Prog()
    emit_prelude(p)
    xd = emit_load_x(p)
    wd = emit_ffn0_weff(p)
    ud = emit_ln(p, "ln", [xd, wd])
    done = emit_ffn0_apply(p, "fa", ud, wd)
    emit_store_x(p, [done])
    return p.finalize()


def build_moe_launch():
    p = Prog()
    emit_prelude(p)
    xd = emit_load_x(p)
    wrt_in = p.inp("wrT", [16, 1024, 1024], F32)
    pd = emit_moe_prep(p)
    ud = emit_ln(p, "ln", [xd, pd])
    done = emit_moe_apply(p, "ma", ud, pd, wrt_in)
    emit_store_x(p, [done])
    return p.finalize()


def run_launch(nc, in_maps):
    res = run_bass_kernel_spmd(nc, in_maps, core_ids=list(range(8)))
    return [r["y"] for r in res.results]


_NC_CACHE = {}


def _launch(kind, xs, inputs):
    consts = host_consts()
    if kind == "ffn0":
        if kind not in _NC_CACHE:
            _NC_CACHE[kind] = build_ffn0_launch()
        w = ffn0_host_inputs(
            inputs["L0_ffn_W"], inputs["L0_ffn_b"],
            inputs["L0_ffn_lora_down"], inputs["L0_ffn_lora_up"],
            inputs["L0_ffn_down_aux"], inputs["L0_ffn_up_aux"])
    elif kind == "moe":
        if kind not in _NC_CACHE:
            _NC_CACHE[kind] = build_moe_launch()
        w = moe_host_inputs(inputs)
    elif kind in ("attn0", "attn1"):
        L = int(kind[-1])
        if kind not in _NC_CACHE:
            _NC_CACHE[kind] = build_attn_launch(L)
        in_maps = [dict(consts, x=xs[c], **attn_host_inputs(L, c, inputs))
                   for c in range(8)]
        ys = run_launch(_NC_CACHE[kind], in_maps)
        for c in range(8):
            xs[c] = ys[c]
        return
    else:
        raise KeyError(kind)
    in_maps = [dict(consts, x=xs[c], **w) for c in range(8)]
    ys = run_launch(_NC_CACHE[kind], in_maps)
    for c in range(8):
        xs[c] = ys[c]


def build_fused(n_subs=6):
    p = Prog()
    emit_prelude(p)
    xd = emit_load_x(p)
    wrt_in = p.inp("wrT", [16, 1024, 1024], F32)
    d = xd
    with ExitStack() as s_l0:
        if n_subs >= 1:
            p.alloc_scope = s_l0
            wd = emit_ffn0_weff(p)
            a0p = emit_attn_prep(p, 0, deps=[wd]) if n_subs >= 2 else None
            p.alloc_scope = p.es
            u1 = emit_ln(p, "ln1", [xd, wd] + ([a0p] if a0p else []))
            d = emit_ffn0_apply(p, "f1", u1, wd)
        if n_subs >= 2:
            u2 = emit_ln(p, "ln2", [d])
            d = emit_attn(p, "a0", 0, u2, a0p)
        if n_subs >= 3:
            u3 = emit_ln(p, "ln3", [d])
            d = emit_ffn0_apply(p, "f3", u3, wd)
    with ExitStack() as s_l1:
        if n_subs >= 4:
            p.alloc_scope = s_l1
            mp = emit_moe_prep(p, deps=[d])
            a1p = emit_attn_prep(p, 1, deps=[d, mp]) if n_subs >= 5 else None
            p.alloc_scope = p.es
            u4 = emit_ln(p, "ln4", [d, mp] + ([a1p] if a1p else []))
            d = emit_moe_apply(p, "m4", u4, mp, wrt_in)
        if n_subs >= 5:
            u5 = emit_ln(p, "ln5", [d])
            d = emit_attn(p, "a1", 1, u5, a1p)
        if n_subs >= 6:
            u6 = emit_ln(p, "ln6", [d])
            d = emit_moe_apply(p, "m6", u6, mp, wrt_in)
    emit_store_x(p, [d])
    return p.finalize()


def fused_in_maps(inputs):
    consts = host_consts()
    xs = shard_x(inputs["L0_attn_x"])
    ffn_w = ffn0_host_inputs(
        inputs["L0_ffn_W"], inputs["L0_ffn_b"],
        inputs["L0_ffn_lora_down"], inputs["L0_ffn_lora_up"],
        inputs["L0_ffn_down_aux"], inputs["L0_ffn_up_aux"])
    moe_w = moe_host_inputs(inputs)
    maps = []
    for c in range(8):
        m = dict(consts, x=xs[c], **ffn_w)
        m.update(moe_w)
        m.update(attn_host_inputs(0, c, inputs))
        m.update(attn_host_inputs(1, c, inputs))
        maps.append(m)
    return maps


FUSED = True
CHAIN = ["ffn0", "attn0", "ffn0", "moe", "attn1", "moe"]


def kernel(**inputs):
    inputs = {k: np.asarray(v, np.float32) for k, v in inputs.items()}
    import os
    chain = os.environ.get("MK_CHAIN", "")
    if FUSED and not chain:
        if "fused" not in _NC_CACHE:
            _NC_CACHE["fused"] = build_fused()
        ys = run_launch(_NC_CACHE["fused"], fused_in_maps(inputs))
        return unshard_x(ys)
    xs = shard_x(inputs["L0_attn_x"])
    for kind in (chain.split(",") if chain else CHAIN):
        _launch(kind, xs, inputs)
    return unshard_x(xs)


def emit_moe_prep(p, deps=()):
    nc = p.nc
    tag = "mp"
    sv, ss, sm = "vch", "sch", "mm"
    for s, v in deps:
        for eng in ("sync", "gpsimd", "tensor", "vector", "scalar"):
            p.wait(eng, s, v)
    p.centT_s = p.sb("centT_s", [128, NCH * 16], BF16)
    p.eb_s = p.sb("eb_s", [128, 16], F32)
    p.br_s = p.sb("br_s", [16, 1024], BF16)
    p.bsrow_s = p.sb("bsrow_s", [1, 1024], BF16)
    p.wseff_s = p.sb("wseff_s", [128, NCH * 1024], BF16)

    centT = p.inp("centT", [1024, 16], F32)
    ebias = p.inp("ebias", [1, 16], F32)
    br_in = p.inp("brT", [16, 1024], F32)
    bs_in = p.inp("bs2", [2, 1024], F32)
    half2 = p.inp("half2", [2, 1], F32)
    wst_in = p.inp("wsT", [2, 1024, 1024], F32)
    p.dma("gpsimd", p.centT_s[:, :].rearrange("p (c n) -> p c n", n=16),
          centT.rearrange("(c p) n -> p c n", p=128), "mpw")
    p.dma("gpsimd", p.br_s[:, :], br_in, "mpw")
    with ExitStack() as st:
        ebrow = st.enter_context(nc.sbuf_tensor(tag + "eb", [1, 16], F32))
        bss = st.enter_context(nc.sbuf_tensor(tag + "bs", [2, 1024], F32))
        h2 = st.enter_context(nc.sbuf_tensor(tag + "h2", [2, 1], F32))
        wsa = st.enter_context(nc.sbuf_tensor(tag + "wsa", [128, 1024], F32))
        wsb = st.enter_context(nc.sbuf_tensor(tag + "wsb", [128, 1024], F32))
        pse = st.enter_context(nc.psum_tensor(tag + "pe", [128, 16], F32))
        psb = st.enter_context(nc.psum_tensor(tag + "pb", [1, 1024], F32))
        v0 = p.dma("sync", ebrow[:, :], ebias, "ldw")
        p.dma("sync", bss[:, :], bs_in, "ldw")
        v1 = p.dma("sync", h2[:, :], half2, "ldw")
        p.wait("tensor", "ldw", v1)
        p.wait("tensor", "ldc", p.ldc_v)
        p.op("tensor", "matmul", pse[:, :], p.onesf_s[0:1, :], ebrow[0:1, :],
             start=True, stop=True, sem=sm)
        for half in range(2):
            p.op("tensor", "matmul", psb[:, half * 512:(half + 1) * 512],
                 h2[:, :], bss[:, half * 512:(half + 1) * 512],
                 start=True, stop=True, sem=sm)
        mmv = p.cnt[sm]
        p.wait("scalar", sm, mmv)
        p.op("scalar", "copy", p.eb_s[:, :], pse[:, :], sem=ss)
        p.op("scalar", "copy", p.bsrow_s[:, :], psb[0:1, :], sem=ss)
        for cc in range(NCH):
            va = p.dma("sync", wsa[:, :], wst_in[0, cc * 128:(cc + 1) * 128, :],
                       "mpwa")
            vb = p.dma("sync", wsb[:, :], wst_in[1, cc * 128:(cc + 1) * 128, :],
                       "mpwa")
            p.wait("vector", "mpwa", vb)
            p.op("vector", "tensor_tensor", wsa[:, :], wsa[:, :], wsb[:, :],
                 op=ALU.add)
            p.op("vector", "tensor_scalar",
                 p.wseff_s[:, cc * 1024:(cc + 1) * 1024], wsa[:, :], 0.5, None,
                 op0=ALU.mult, sem=sv)
            if cc < NCH - 1:
                p.wait("sync", sv, p.cnt[sv])
        p.wait("vector", ss, p.cnt[ss])
        p.wait("vector", "mpw", 32)
    return (sv, p.cnt[sv])


def emit_moe_apply(p, tag, u_dep, prep_dep, wrt_in, dbg=()):
    nc = p.nc
    sv, ss, sm, sg = "vch", "sch", "mm", tag + "g"
    with ExitStack() as st:
        g05 = st.enter_context(nc.sbuf_tensor(tag + "g05", [128, 16 * 16], F32))
        gfm = st.enter_context(nc.sbuf_tensor(tag + "gfm", [16, NT], BF16))
        acc = st.enter_context(nc.sbuf_tensor(tag + "acc", [128, 16 * 1024], F32))
        r16 = st.enter_context(nc.sbuf_tensor(tag + "r16", [128, 16], F32))
        r16b = st.enter_context(nc.sbuf_tensor(tag + "r16b", [128, 16], F32))
        rcol = st.enter_context(nc.sbuf_tensor(tag + "rc", [128, 4], F32))
        gbf = st.enter_context(nc.sbuf_tensor(tag + "gbf", [128, 16], BF16))

        for eng in ("tensor", "vector", "scalar", "gpsimd", "sync"):
            p.wait(eng, u_dep[0], u_dep[1])
            p.wait(eng, prep_dep[0], prep_dep[1])

        with ExitStack() as st2:
            psr = st2.enter_context(nc.psum_tensor(tag + "psr", [128, 16], F32))
            psg = st2.enter_context(nc.psum_tensor(tag + "psg", [16, 128], BF16))
            for m in range(16):
                for cc in range(NCH):
                    p.op("tensor", "matmul", psr[:, :],
                         p.u_s[:, cc * NT + m * 128: cc * NT + (m + 1) * 128],
                         p.centT_s[:, cc * 16:(cc + 1) * 16],
                         start=(cc == 0), stop=(cc == NCH - 1),
                         sem=(sm if cc == NCH - 1 else None))
                mmv = p.cnt[sm]
                p.wait("vector", sm, mmv)
                p.op("vector", "tensor_tensor", r16[:, :], psr[:, :],
                     p.eb_s[:, :], op=ALU.add, sem=sv)
                rr_v = p.cnt[sv]
                p.op("vector", "reduce_max", rcol[:, 0:1], r16[:, :], axis=AX.X,
                     negate=True, sem=sv)
                vv = p.cnt[sv]
                p.wait("scalar", sv, vv)
                p.op("scalar", "activation", r16[:, :], r16[:, :], AF.Exp,
                     bias=rcol[:, 0:1], sem=ss)
                ssv = p.cnt[ss]
                p.wait("vector", ss, ssv)
                p.op("vector", "reduce_sum", rcol[:, 1:2], r16[:, :], axis=AX.X)
                p.op("vector", "reciprocal", rcol[:, 2:3], rcol[:, 1:2])
                p.op("vector", "tensor_scalar", r16[:, :], r16[:, :],
                     rcol[:, 2:3], None, op0=ALU.mult)
                p.op("vector", "reduce_max", rcol[:, 0:1], r16[:, :], axis=AX.X,
                     negate=True)
                p.op("vector", "tensor_scalar", r16b[:, :], r16[:, :],
                     rcol[:, 0:1], None, op0=ALU.add)
                p.op("vector", "tensor_scalar", r16b[:, :], r16b[:, :], 0.0,
                     None, op0=ALU.is_equal)
                p.op("vector", "scalar_tensor_tensor", r16b[:, :], r16b[:, :],
                     NEG, r16[:, :], op0=ALU.mult, op1=ALU.add)
                p.op("vector", "reduce_max", rcol[:, 3:4], r16b[:, :], axis=AX.X,
                     negate=True)
                p.op("vector", "tensor_scalar", r16b[:, :], r16[:, :],
                     rcol[:, 3:4], None, op0=ALU.add)
                p.op("vector", "tensor_scalar", r16b[:, :], r16b[:, :], 0.0,
                     None, op0=ALU.is_ge)
                p.op("vector", "tensor_tensor", r16b[:, :], r16b[:, :],
                     r16[:, :], op=ALU.mult)
                p.op("vector", "tensor_scalar", g05[:, m * 16:(m + 1) * 16],
                     r16b[:, :], 0.5, None, op0=ALU.mult)
                p.op("vector", "tensor_copy", gbf[:, :], r16b[:, :], sem=sv)
                gv = p.cnt[sv]
                p.wait("tensor", sv, gv)
                p.op("tensor", "transpose", psg[:, :], gbf[:, :],
                     p.identb_s[:, :], sem=sm)
                tv = p.cnt[sm]
                p.wait("scalar", sm, tv)
                p.op("scalar", "copy", gfm[0:16, m * 128:(m + 1) * 128],
                     psg[:, :], sem=ss)
                p.wait("tensor", ss, p.cnt[ss])
                p.wait("tensor", sv, rr_v)
        router_v = p.cnt[ss]
        if dbg:
            o1 = p.outp("dbg_g05", [128, 16 * 16], F32)
            o2 = p.outp("dbg_gfm", [16, NT], BF16)
            p.wait("sync", ss, router_v)
            p.wait("sync", sv, p.cnt[sv])
            p.dma("sync", o1, g05[:, :], "dbgst")
            v = p.dma("sync", o2, gfm[:, :], "dbgst")
            if dbg == {"router"}:
                p.wait("sync", "dbgst", v)
                return (sv, p.cnt[sv])

        with ExitStack() as st2:
            pssh = [st2.enter_context(
                nc.psum_tensor(tag + "pss%d" % i, [128, 512], F32))
                for i in range(2)]
            slot_ready = [None, None]
            p.wait("tensor", ss, router_v)
            for m in range(16):
                for oh in range(2):
                    slot = (m * 2 + oh) % 2
                    psu = pssh[slot]
                    if slot_ready[slot] is not None:
                        p.wait("tensor", sv, slot_ready[slot])
                    for cc in range(NCH):
                        p.op("tensor", "matmul", psu[:, :],
                             p.u_s[:, cc * NT + m * 128: cc * NT + (m + 1) * 128],
                             p.wseff_s[:, cc * 1024 + oh * 512: cc * 1024 + (oh + 1) * 512],
                             start=(cc == 0), stop=False)
                    p.op("tensor", "matmul", psu[:, :],
                         gfm[0:16, m * 128:(m + 1) * 128],
                         p.br_s[0:16, oh * 512:(oh + 1) * 512],
                         start=False, stop=False)
                    p.op("tensor", "matmul", psu[:, :], p.ones_s[0:1, :],
                         p.bsrow_s[0:1, oh * 512:(oh + 1) * 512],
                         start=False, stop=True, sem=sm)
                    mmv = p.cnt[sm]
                    p.wait("vector", sm, mmv)
                    p.op("vector", "tensor_scalar",
                         acc[:, m * 1024 + oh * 512: m * 1024 + (oh + 1) * 512],
                         psu[:, :], 0.5, None, op0=ALU.mult, sem=sv)
                    slot_ready[slot] = p.cnt[sv]
        shared_v = p.cnt[sv]
        if "shared" in dbg:
            o3 = p.outp("dbg_accS", [128, 16 * 1024], F32)
            p.wait("sync", sv, shared_v)
            v = p.dma("sync", o3, acc[:, :], "dbgst")
            if "experts" not in dbg:
                p.wait("sync", "dbgst", v)
                return (sv, p.cnt[sv])

        with ExitStack() as st2:
            wrbuf = [st2.enter_context(
                nc.sbuf_tensor(tag + "wr%d" % i, [128, NCH * 512], BF16))
                for i in range(2)]
            pse = [st2.enter_context(
                nc.psum_tensor(tag + "pe%d" % i, [128, 512], F32))
                for i in range(8)]
            pairs = [(e, oh) for e in range(16) for oh in range(2)]
            sgs = [sg + "A", sg + "B"]
            buf_ready = {}
            for i in (0, 1):
                e, oh = pairs[i]
                src = wrt_in[e, :, oh * 512:(oh + 1) * 512].rearrange(
                    "(c p) n -> p c n", p=128)
                dst = wrbuf[i % 2][:, :].rearrange("p (c n) -> p c n", n=512)
                buf_ready[i] = p.dma("gpsimd", dst, src, sgs[i % 2])
            bank_free = [None] * 8
            p.wait("tensor", sv, shared_v)
            for i, (e, oh) in enumerate(pairs):
                p.wait("tensor", sgs[i % 2], buf_ready[i])
                wb = wrbuf[i % 2]
                for mg in range(4):
                    half = mg % 2
                    for cc in range(NCH):
                        for mi in range(4):
                            m = mg * 4 + mi
                            bank = half * 4 + mi
                            if cc == 0 and bank_free[bank] is not None:
                                p.wait("tensor", sv, bank_free[bank])
                            p.op("tensor", "matmul", pse[bank][:, :],
                                 p.u_s[:, cc * NT + m * 128: cc * NT + (m + 1) * 128],
                                 wb[:, cc * 512:(cc + 1) * 512],
                                 start=(cc == 0), stop=(cc == NCH - 1),
                                 sem=(sm if (cc == NCH - 1 and mi == 3) else None))
                    mmv = p.cnt[sm]
                    p.wait("vector", sm, mmv)
                    for mi in range(4):
                        m = mg * 4 + mi
                        bank = half * 4 + mi
                        asl = acc[:, m * 1024 + oh * 512: m * 1024 + (oh + 1) * 512]
                        p.op("vector", "scalar_tensor_tensor", asl,
                             pse[bank][:, :], g05[:, m * 16 + e: m * 16 + e + 1],
                             asl, op0=ALU.mult, op1=ALU.add, sem=sv)
                        bank_free[bank] = p.cnt[sv]
                if i + 2 < len(pairs):
                    e2, oh2 = pairs[i + 2]
                    p.wait("gpsimd", sm, p.cnt[sm])
                    src = wrt_in[e2, :, oh2 * 512:(oh2 + 1) * 512].rearrange(
                        "(c p) n -> p c n", p=128)
                    dst = wrbuf[i % 2][:, :].rearrange("p (c n) -> p c n", n=512)
                    buf_ready[i + 2] = p.dma("gpsimd", dst, src, sgs[i % 2])
        expert_v = p.cnt[sv]
        if "experts" in dbg:
            o4 = p.outp("dbg_accE", [128, 16 * 1024], F32)
            p.wait("sync", sv, expert_v)
            v = p.dma("sync", o4, acc[:, :], "dbgst")
            p.wait("sync", "dbgst", v)
            return (sv, p.cnt[sv])

        for cc in range(NCH):
            xc = p.x_s[:, cc * NT:(cc + 1) * NT]
            p.op("vector", "scalar_tensor_tensor", xc,
                 p.u_s[:, cc * NT:(cc + 1) * NT], 0.5, xc,
                 op0=ALU.mult, op1=ALU.add, sem=sv)

        with ExitStack() as st2:
            pst = [st2.enter_context(
                nc.psum_tensor(tag + "pt%d" % i, [128, 512], F32))
                for i in range(2)]
            grp_free = [None, None]
            p.wait("tensor", sv, expert_v)
            for m in range(16):
                for og in range(2):
                    slot = (m * 2 + og) % 2
                    if grp_free[slot] is not None:
                        p.wait("tensor", sv, grp_free[slot])
                    for oi in range(4):
                        oc = og * 4 + oi
                        p.op("tensor", "transpose",
                             pst[slot][:, oi * 128:(oi + 1) * 128],
                             acc[:, m * 1024 + oc * 128: m * 1024 + (oc + 1) * 128],
                             p.identf_s[:, :],
                             sem=(sm if oi == 3 else None))
                    mmv = p.cnt[sm]
                    p.wait("vector", sm, mmv)
                    for oi in range(4):
                        oc = og * 4 + oi
                        xsl = p.x_s[:, oc * NT + m * 128: oc * NT + (m + 1) * 128]
                        p.op("vector", "tensor_tensor", xsl, xsl,
                             pst[slot][:, oi * 128:(oi + 1) * 128], op=ALU.add,
                             sem=sv)
                    grp_free[slot] = p.cnt[sv]
    return (sv, p.cnt[sv])


def emit_attn_prep(p, L, deps=()):
    nc = p.nc
    tag = "ap%d" % L
    sv, ss, sm = "vch", "sch", "mm"
    for s, v in deps:
        for eng in ("sync", "gpsimd", "tensor", "vector", "scalar"):
            p.wait(eng, s, v)
    if not hasattr(p, "kc_d"):
        p.kc_d = [p.nc.dram_tensor("kc%d" % g, [256, NT], BF16) for g in range(4)]
        p.vc_d = [p.nc.dram_tensor("vc%d" % g, [NT, 256], BF16) for g in range(4)]
        p.gk_d = [p.nc.dram_tensor("gk%d" % g, [8 * 256, NT], BF16) for g in range(4)]
        p.gv_d = [p.nc.dram_tensor("gv%d" % g, [8 * NT, 256], BF16) for g in range(4)]
        mk = p.inp("maskt", [128, 8 * 128], BF16)
        p.mask_v = p.dma("sync", p.mask_s[:, :], mk, "ldc")
    if L == 0:
        p.cos_s = p.sb("cos_s", [128, NT], BF16)
        p.sin_s = p.sb("sin_s", [128, NT], BF16)
        p.perm_s = p.sb("perm_s", [128, 128], BF16)
        p.dma("sync", p.cos_s[:, :], p.inp("cos_t", [128, NT], BF16), "ldc")
        p.dma("sync", p.sin_s[:, :], p.inp("sin_t", [128, NT], BF16), "ldc")
        v = p.dma("sync", p.perm_s[:, :], p.inp("perm", [128, 128], BF16), "ldc")
        return ("ldc", v)
    p.m2k_s = p.sb("m2k_s", [128, 128], BF16)
    p.m2v_s = p.sb("m2v_s", [128, 128], BF16)
    with ExitStack() as st:
        wkc = st.enter_context(nc.sbuf_tensor(tag + "wkc", [32, 64], F32))
        wke = st.enter_context(nc.sbuf_tensor(tag + "wke", [32, 64], F32))
        wvc = st.enter_context(nc.sbuf_tensor(tag + "wvc", [32, 64], F32))
        wve = st.enter_context(nc.sbuf_tensor(tag + "wve", [32, 64], F32))
        psk = st.enter_context(nc.psum_tensor(tag + "psk", [64, 64], F32))
        psv = st.enter_context(nc.psum_tensor(tag + "psv", [64, 64], F32))
        p.dma("sync", wkc[:, :], p.inp("wkcT", [32, 64], F32), "ldw")
        p.dma("sync", wke[:, :], p.inp("wke", [32, 64], F32), "ldw")
        p.dma("sync", wvc[:, :], p.inp("wvcT", [32, 64], F32), "ldw")
        v0 = p.dma("sync", wve[:, :], p.inp("wve", [32, 64], F32), "ldw")
        mz = p.op("gpsimd", "memset", p.m2k_s[:, :], 0.0)
        mz = p.op("gpsimd", "memset", p.m2v_s[:, :], 0.0)
        p.wait("tensor", "ldw", v0)
        p.op("tensor", "matmul", psk[:, :], wkc[:, :], wke[:, :],
             start=True, stop=True, sem=sm)
        p.op("tensor", "matmul", psv[:, :], wvc[:, :], wve[:, :],
             start=True, stop=True, sem=sm)
        mmv = p.cnt[sm]
        p.wait("scalar", sm, mmv)
        p.wait("scalar", "gch", mz)
        p.op("scalar", "copy", p.m2k_s[0:64, 0:64], psk[:, :], sem=ss)
        p.op("scalar", "copy", p.m2k_s[64:128, 64:128], psk[:, :], sem=ss)
        p.op("scalar", "copy", p.m2v_s[0:64, 0:64], psv[:, :], sem=ss)
        p.op("scalar", "copy", p.m2v_s[64:128, 64:128], psv[:, :], sem=ss)
    return (ss, p.cnt[ss])


def emit_attn(p, tag, L, u_dep, prep_dep):
    nc = p.nc
    sv, ss, sm = "vch", "sch", "mm"
    scc = tag + "c"
    W = {}
    for nm in ("Wq", "Wk", "Wv", "Wp"):
        W[nm] = p.inp(tag + nm, [1024, 1024], F32)
    bq = p.inp(tag + "bq", [128, 8], F32)
    bk = p.inp(tag + "bk", [128, 8], F32)
    bp = p.inp(tag + "bp", [128, 8], F32)
    bvr = p.inp(tag + "bvr", [1, 1024], F32)

    for eng in ("tensor", "vector", "scalar", "gpsimd", "sync"):
        p.wait(eng, u_dep[0], u_dep[1])
        p.wait(eng, prep_dep[0], prep_dep[1])
        p.wait(eng, "ldc", p.mask_v)

    with ExitStack() as st:
        sb = lambda n, sh, dt: st.enter_context(nc.sbuf_tensor(tag + "_" + n, sh, dt))
        q_s = sb("q", [128, NCH * NT], BF16)
        bq_s = sb("bq", [128, 8], F32)
        bk_s = sb("bk", [128, 8], F32)
        bp_s = sb("bp", [128, 8], F32)
        bvr_s = sb("bvr", [1, 1024], BF16)
        vstg = sb("vstg", [128, 512], BF16)

        p.dma("sync", bq_s[:, :], bq, "ldw")
        p.dma("sync", bk_s[:, :], bk, "ldw")
        p.dma("sync", bp_s[:, :], bp, "ldw")
        bv_v = p.dma("gpsimd", bvr_s[:, :], bvr, tag + "bv")

        with ExitStack() as st2:
            psqk = [st2.enter_context(nc.psum_tensor(tag + "pq%d" % i,
                                                     [128, 512], F32))
                    for i in range(2)]
            psw = st2.enter_context(nc.psum_tensor(tag + "psw", [128, 512], F32))
            wsl = [st2.enter_context(nc.sbuf_tensor(tag + "_wsl%d" % i,
                                                    [128, 8 * 128], BF16))
                   for i in range(2)]
            kraw = st2.enter_context(nc.sbuf_tensor(tag + "_kraw", [128, 512], BF16))
            ksw = st2.enter_context(nc.sbuf_tensor(tag + "_ksw", [128, 512], BF16))
            t1 = st2.enter_context(nc.sbuf_tensor(tag + "_t1", [128, 512], BF16))
            wq_jobs = [("Wq", oc) for oc in range(NCH)] + \
                      [("Wk", oc) for oc in range(NCH)]
            wsem = [tag + "wA", tag + "wB"]
            wrdy = {}
            for j in (0, 1):
                nm, oc = wq_jobs[j]
                src = W[nm][:, oc * 128:(oc + 1) * 128].rearrange(
                    "(c p) n -> p c n", p=128)
                wrdy[j] = p.dma("gpsimd", wsl[j % 2][:, :].rearrange(
                    "p (c n) -> p c n", n=128), src, wsem[j % 2])
            slot_rd = [None, None]
            for j, (nm, oc) in enumerate(wq_jobs):
                p.wait("tensor", wsem[j % 2], wrdy[j])
                wb = wsl[j % 2]
                for tc in range(4):
                    slot = tc % 2
                    if slot_rd[slot] is not None:
                        p.wait("tensor", ss, slot_rd[slot])
                    for cc in range(NCH):
                        p.op("tensor", "matmul", psqk[slot][:, :],
                             wb[:, cc * 128:(cc + 1) * 128],
                             p.u_s[:, cc * NT + tc * 512: cc * NT + (tc + 1) * 512],
                             start=(cc == 0), stop=(cc == NCH - 1),
                             sem=(sm if cc == NCH - 1 else None))
                    mmv = p.cnt[sm]
                    p.wait("scalar", sm, mmv)
                    isq = (nm == "Wq")
                    bias = (bq_s if isq else bk_s)[:, oc:oc + 1]
                    if L == 0:
                        p.op("scalar", "activation", kraw[:, :],
                             psqk[slot][:, :], AF.Identity, bias=bias, sem=ss)
                        kv = p.cnt[ss]
                        slot_rd[slot] = kv
                        p.wait("tensor", ss, kv)
                        p.op("tensor", "matmul", psw[:, :], p.perm_s[:, :],
                             kraw[:, :], start=True, stop=True, sem=sm)
                        swv = p.cnt[sm]
                        p.wait("scalar", sm, swv)
                        p.op("scalar", "copy", ksw[:, :], psw[:, :], sem=ss)
                        swc = p.cnt[ss]
                        cs = slice(tc * 512, (tc + 1) * 512)
                        p.wait("vector", ss, kv)
                        p.op("vector", "tensor_tensor", t1[:, :], kraw[:, :],
                             p.cos_s[:, cs], op=ALU.mult)
                        p.wait("vector", ss, swc)
                        p.op("vector", "tensor_tensor", ksw[:, :], ksw[:, :],
                             p.sin_s[:, cs], op=ALU.mult)
                        dst = (q_s[:, oc * NT + tc * 512: oc * NT + (tc + 1) * 512]
                               if isq else kraw[:, :])
                        rv = p.op("vector", "tensor_tensor", dst, t1[:, :],
                                  ksw[:, :], op=ALU.add)
                        p.wait("scalar", sv, rv)
                        if not isq:
                            hg, half = oc // 2, oc % 2
                            p.wait("sync", sv, rv)
                            p.dma("sync",
                                  p.kc_d[hg].ap()[half * 128:(half + 1) * 128,
                                                  tc * 512:(tc + 1) * 512],
                                  kraw[:, :], tag + "kv%d" % hg)
                            p.wait("scalar", tag + "kv%d" % hg,
                                   p.cnt[tag + "kv%d" % hg])
                    else:
                        if isq:
                            p.op("scalar", "activation",
                                 q_s[:, oc * NT + tc * 512: oc * NT + (tc + 1) * 512],
                                 psqk[slot][:, :], AF.Identity, bias=bias,
                                 sem=ss)
                            slot_rd[slot] = p.cnt[ss]
                        else:
                            p.op("scalar", "activation", kraw[:, :],
                                 psqk[slot][:, :], AF.Identity, bias=bias,
                                 sem=ss)
                            kv = p.cnt[ss]
                            slot_rd[slot] = kv
                            p.wait("tensor", ss, kv)
                            p.op("tensor", "matmul", psw[:, :], p.m2k_s[:, :],
                                 kraw[:, :], start=True, stop=True, sem=sm)
                            swv = p.cnt[sm]
                            p.wait("scalar", sm, swv)
                            p.op("scalar", "copy", ksw[:, :], psw[:, :],
                                 sem=ss)
                            kxv = p.cnt[ss]
                            hg, half = oc // 2, oc % 2
                            p.wait("sync", ss, kxv)
                            p.dma("sync",
                                  p.kc_d[hg].ap()[half * 128:(half + 1) * 128,
                                                  tc * 512:(tc + 1) * 512],
                                  ksw[:, :], tag + "kv%d" % hg)
                            p.wait("scalar", tag + "kv%d" % hg,
                                   p.cnt[tag + "kv%d" % hg])
                if j + 2 < len(wq_jobs):
                    nm2, oc2 = wq_jobs[j + 2]
                    p.wait("gpsimd", sm, p.cnt[sm])
                    src = W[nm2][:, oc2 * 128:(oc2 + 1) * 128].rearrange(
                        "(c p) n -> p c n", p=128)
                    wrdy[j + 2] = p.dma("gpsimd", wsl[j % 2][:, :].rearrange(
                        "p (c n) -> p c n", n=128), src, wsem[j % 2])

        with ExitStack() as st2:
            psv = [st2.enter_context(nc.psum_tensor(tag + "pv%d" % i,
                                                    [128, 512], F32))
                   for i in range(2)]
            wvh = [st2.enter_context(nc.sbuf_tensor(tag + "_wvh%d" % i,
                                                    [128, 8 * 512], BF16))
                   for i in range(2)]
            vsem = [tag + "vA", tag + "vB"]
            vrdy = {}
            p.wait("gpsimd", ss, p.cnt[ss])
            p.wait("gpsimd", sv, p.cnt[sv])
            p.wait("gpsimd", sm, p.cnt[sm])
            for g in range(4):
                if (tag + "kv%d" % g) in p.cnt:
                    p.wait("gpsimd", tag + "kv%d" % g, p.cnt[tag + "kv%d" % g])
            for j in (0, 1):
                src = W["Wv"][:, j * 512:(j + 1) * 512].rearrange(
                    "(c p) n -> p c n", p=128)
                vrdy[j] = p.dma("gpsimd", wvh[j][:, :].rearrange(
                    "p (c n) -> p c n", n=512), src, vsem[j])
            p.wait("tensor", tag + "bv", bv_v)
            p.wait("tensor", ss, p.cnt[ss])
            p.wait("tensor", sv, p.cnt[sv])
            for g in range(4):
                if (tag + "kv%d" % g) in p.cnt:
                    p.wait("tensor", tag + "kv%d" % g, p.cnt[tag + "kv%d" % g])
            slot_rd = [None, None]
            for oh in range(2):
                p.wait("tensor", vsem[oh], vrdy[oh])
                for m in range(16):
                    slot = m % 2
                    if slot_rd[slot] is not None:
                        p.wait("tensor", ss, slot_rd[slot])
                    for cc in range(NCH):
                        p.op("tensor", "matmul", psv[slot][:, :],
                             p.u_s[:, cc * NT + m * 128: cc * NT + (m + 1) * 128],
                             wvh[oh][:, cc * 512:(cc + 1) * 512],
                             start=(cc == 0), stop=False)
                    p.op("tensor", "matmul", psv[slot][:, :],
                         p.ones_s[0:1, :],
                         bvr_s[0:1, oh * 512:(oh + 1) * 512],
                         start=False, stop=True, sem=sm)
                    mmv = p.cnt[sm]
                    p.wait("scalar", sm, mmv)
                    p.op("scalar", "copy", vstg[:, :], psv[slot][:, :], sem=ss)
                    vv = p.cnt[ss]
                    slot_rd[slot] = vv
                    p.wait("sync", ss, vv)
                    for hh in range(2):
                        hg = oh * 2 + hh
                        p.dma("sync",
                              p.vc_d[hg].ap()[m * 128:(m + 1) * 128, :],
                              vstg[:, hh * 256:(hh + 1) * 256],
                              tag + "kv%d" % hg)
                    for hh in range(2):
                        hgx = oh * 2 + hh
                        p.wait("scalar", tag + "kv%d" % hgx,
                               p.cnt[tag + "kv%d" % hgx])

        for hg in range(4):
            p.wait("gpsimd", tag + "kv%d" % hg, p.cnt[tag + "kv%d" % hg])
            p.op("gpsimd", "collective_compute",
                 "AllGather", ALU.bypass,
                 replica_groups=[list(range(8))],
                 ins=[p.kc_d[hg].ap().opt()], outs=[p.gk_d[hg].ap().opt()],
                 sem=scc)
            p.op("gpsimd", "collective_compute",
                 "AllGather", ALU.bypass,
                 replica_groups=[list(range(8))],
                 ins=[p.vc_d[hg].ap().opt()], outs=[p.gv_d[hg].ap().opt()],
                 sem=scc)
        cc_base = p.cnt[scc] - 8

        khg = [sb("khg0", [128, 2 * T], BF16)]
        vhg = [sb("vhg0", [128, 4 * 32 * 65], BF16)]
        ohgb = sb("ohgb", [128, 2 * 512], BF16)
        ohgb_raw = ohgb
        sig_s = sb("sig", [1, 2048], BF16)

        pbuf = [sb("pb%d" % i, [128, 1024], BF16) for i in range(2)]
        wp_s = sb("wp", [128, 2 * 1024], BF16)
        p.wait("gpsimd", ss, p.cnt[ss])
        p.wait("gpsimd", sm, p.cnt[sm])
        mz1 = p.op("gpsimd", "memset", vhg[0][:, :], 1.0)
        p.wait("sync", "gch", mz1)

        with ExitStack() as st2:
            psS = [st2.enter_context(nc.psum_tensor(tag + "pS%d" % i,
                                                    [128, 1024], F32))
                   for i in range(2)]
            psO = [st2.enter_context(nc.psum_tensor(tag + "pO%d" % i,
                                                    [65, 512], F32))
                   for i in range(2)]
            psP = [st2.enter_context(nc.psum_tensor(tag + "pP%d" % i,
                                                    [128, 512], F32))
                   for i in range(2)]
            stg = [tag + "sA", tag + "sB"]
            wpsem = [tag + "pA2", tag + "pB2"]

            def stage(i, hg, b):
                gk = p.gk_d[hg].ap()
                gv = p.gv_d[hg].ap()
                for r in range(8):
                    for hp in range(2):
                        src = gk[256 * r + 128 * hp: 256 * r + 128 * (hp + 1),
                                 b * 512:(b + 1) * 512]
                        src = src.rearrange("p (t n) -> p t n", n=128)
                        dst = khg[0][:, hp * T: (hp + 1) * T].rearrange(
                            "p (t n) -> p t n", n=128)[:, r::8, :]
                        p.dma("sync", dst, src, stg[0])
                for r in range(8):
                    srcv = gv[2048 * r + 512 * b: 2048 * r + 512 * (b + 1), :]
                    srcv = srcv.rearrange("(t p) (h d) -> p h t d", p=128, d=64)
                    dstv = vhg[0][:, :].rearrange(
                        "p (h k e) -> p h k e", h=4, e=65)[:, :, r::8, 0:64]
                    p.dma("sync", dstv, srcv, stg[1])
                return (p.cnt[stg[0]], p.cnt[stg[1]])

            def wp_load(hg):
                src = W["Wp"][2 * hg * 128:(2 * hg + 2) * 128, :].rearrange(
                    "(c p) n -> p c n", p=128)
                return p.dma("gpsimd", wp_s[:, :].rearrange(
                    "p (c n) -> p c n", n=1024), src, wpsem[0])

            hgb = [(hg, b) for hg in range(4) for b in range(B)]
            stg_rdy = {}
            wp_rdy = {}
            psp_free = 0
            reg_rd = [0, 0]
            reg_av = [0, 0]
            po_free = [0, 0]
            psS_guard = 0
            wp_rdy[0] = wp_load(0)
            last_proj_dve = None
            pe_prev = 0
            for i, (hg, b) in enumerate(hgb):
                if i == 0:
                    p.wait("sync", scc, cc_base + 2)
                    stg_rdy[0] = stage(0, hg, b)
                p.wait("tensor", stg[0], stg_rdy[i][0])
                if b == 0:
                    p.wait("tensor", wpsem[0], wp_rdy[hg])
                kb, vb = khg[0], vhg[0]
                wpb = wp_s
                def tile_regions():
                    regs, cur, off = [], [], 0
                    for kt in range(32):
                        N = 512 - 128 * (kt // 8)
                        if off % 512 + N > 512 and off % 512 != 0:
                            off = (off // 512 + 1) * 512
                        if off + N > 1024:
                            regs.append(cur)
                            cur, off = [], 0
                        cur.append((kt, off, N))
                        off += N
                    regs.append(cur)
                    return regs
                TREG = tile_regions()
                R = [(hq, ri) for hq in range(4) for ri in range(len(TREG))]
                score_mm, exp_rd, msk_rd, rcps = {}, {}, {}, {}

                def emit_scores(j):
                    hq, ri = R[j]
                    reg = j % 2
                    hb = (hq % 2) * 64
                    hp = hq // 2
                    qbase = (hg * 2 + hp) * NT + b * 512
                    if reg_rd[reg]:
                        p.wait("tensor", ss, reg_rd[reg])
                    if reg == 0 and psS_guard:
                        p.wait("tensor", sv, psS_guard)
                    for (kt, off, N) in TREG[ri]:
                        q0 = 128 * (kt // 8)
                        p.op("tensor", "matmul",
                             psS[reg][:, off:off + N],
                             kb[hb:hb + 64,
                                hp * T + kt * 128: hp * T + (kt + 1) * 128],
                             q_s[hb:hb + 64, qbase + q0: qbase + 512],
                             start=(off % 512 == 0),
                             stop=((kt, off, N) == TREG[ri][-1]),
                             sem=(sm if (kt, off, N) == TREG[ri][-1] else None))
                    score_mm[j] = p.cnt[sm]

                def emit_exp(j):
                    hq, ri = R[j]
                    reg = j % 2
                    p.wait("scalar", sm, score_mm[j])
                    if reg_av[reg]:
                        p.wait("scalar", sm, reg_av[reg])
                    p.op("scalar", "activation", pbuf[reg][:, :],
                         psS[reg][:, :], AF.Exp,
                         bias=p.cst_s[:, 0:1], scale=SCALE, sem=ss)
                    reg_rd[reg] = p.cnt[ss]
                    exp_rd[j] = reg_rd[reg]
                    p.wait("vector", ss, exp_rd[j])
                    tiles = TREG[ri]
                    nt = len(tiles)
                    step = tiles[1][1] - tiles[0][1] if nt > 1 else 128
                    c0 = tiles[0][0] % 8
                    pv = pbuf[reg][:, 0:nt * step].rearrange(
                        "p (t n) -> p t n", n=step)[:, :, 0:128]
                    mkv = p.mask_s[:, c0 * 128:(c0 + nt) * 128].rearrange(
                        "p (t n) -> p t n", n=128)
                    msk_rd[j] = p.op("vector", "tensor_tensor", pv, pv, mkv,
                                     op=ALU.mult)

                def emit_av(j):
                    hq, ri = R[j]
                    reg = j % 2
                    po = psO[hq % 2]
                    if j == 0:
                        p.wait("tensor", stg[1], stg_rdy[i][1])
                    p.wait("tensor", ss, exp_rd[j])
                    p.wait("tensor", sv, msk_rd[j])
                    if ri == 0 and po_free[hq % 2]:
                        p.wait("tensor", ss, po_free[hq % 2])
                    for (kt, off, N) in TREG[ri]:
                        q0 = 128 * (kt // 8)
                        p.op("tensor", "matmul", po[:, q0:512],
                             vb[:, (hq * 32 + kt) * 65:
                                (hq * 32 + kt) * 65 + 65],
                             pbuf[reg][:, off:off + N],
                             start=(kt == 0),
                             stop=(ri == len(TREG) - 1 and
                                   (kt, off, N) == TREG[ri][-1]),
                             sem=(sm if (kt, off, N) == TREG[ri][-1] else None))
                    reg_av[reg] = p.cnt[sm]
                    if ri == len(TREG) - 1:
                        hb = (hq % 2) * 64
                        hp = hq // 2
                        p.wait("scalar", sm, reg_av[reg])
                        p.op("scalar", "copy",
                             ohgb_raw[hb:hb + 64, hp * 512:(hp + 1) * 512],
                             po[0:64, 0:512])
                        p.op("scalar", "copy",
                             sig_s[0:1, hq * 512:(hq + 1) * 512],
                             po[64:65, 0:512], sem=ss)
                        po_free[hq % 2] = p.cnt[ss]
                        p.wait("vector", ss, po_free[hq % 2])
                        rcps[hq] = p.op("vector", "reciprocal",
                                        sig_s[0:1, hq * 512:(hq + 1) * 512],
                                        sig_s[0:1, hq * 512:(hq + 1) * 512],
                                        lp=True)

                emit_scores(0)
                emit_exp(0)
                if len(R) > 1:
                    emit_scores(1)
                    emit_exp(1)
                for j in range(len(R)):
                    emit_av(j)
                    if j + 2 < len(R):
                        emit_scores(j + 2)
                        emit_exp(j + 2)
                av_end = p.cnt[sm]

                if i + 1 < len(hgb):
                    hg2, b2 = hgb[i + 1]
                    p.wait("sync", scc, cc_base + 2 * (hg2 + 1))
                    p.wait("sync", sm, av_end)
                    stg_rdy[i + 1] = stage(i + 1, hg2, b2)

                sig_done = p.cnt[ss]
                p.wait("vector", ss, sig_done)
                p.wait("tensor", ss, max(reg_rd))
                if psS_guard:
                    p.wait("tensor", sv, psS_guard)
                p.wait("tensor", sv, p.cnt[sv])
                nv = 0
                for hq in range(4):
                    hb = (hq % 2) * 64
                    hp = hq // 2
                    p.wait("tensor", sv, rcps[hq])
                    if nv:
                        p.wait("tensor", sv, nv)
                    p.op("tensor", "matmul",
                         psP[0][hb:hb + 64, 0:512],
                         p.ones_s[0:1, 0:64],
                         sig_s[0:1, hq * 512:(hq + 1) * 512],
                         start=True, stop=True, sem=sm)
                    bcv = p.cnt[sm]
                    p.wait("vector", sm, bcv)
                    nv = p.op("vector", "tensor_tensor",
                              ohgb[hb:hb + 64, hp * 512:(hp + 1) * 512],
                              ohgb_raw[hb:hb + 64, hp * 512:(hp + 1) * 512],
                              psP[0][hb:hb + 64, 0:512], op=ALU.mult)
                psS_guard = 0
                if L == 1:
                    p.wait("tensor", sv, nv)
                    for half in range(2):
                        p.op("tensor", "matmul",
                             psS[1][:, half * 512:(half + 1) * 512],
                             p.m2v_s[:, :], ohgb[:, half * 512:(half + 1) * 512],
                             start=True, stop=True,
                             sem=(sm if half == 1 else None))
                    mvv = p.cnt[sm]
                    p.wait("scalar", sm, mvv)
                    p.op("scalar", "copy", ohgb[:, :], psS[1][:, :], sem=ss)
                reg_rd = [p.cnt[ss], p.cnt[ss]]
                p.wait("tensor", sv, p.cnt[sv])
                p.wait("tensor", ss, p.cnt[ss])
                proj_dve = {}
                for oc in range(NCH):
                    pp = psP[oc % 2]
                    for hp in range(2):
                        p.op("tensor", "matmul", pp[:, :],
                             wpb[:, hp * 1024 + oc * 128: hp * 1024 + (oc + 1) * 128],
                             ohgb[:, hp * 512:(hp + 1) * 512],
                             start=(hp == 0), stop=(hp == 1),
                             sem=(sm if hp == 1 else None))
                    mmv = p.cnt[sm]
                    xsl = p.x_s[:, oc * NT + b * 512: oc * NT + (b + 1) * 512]
                    if hg == 0:
                        p.wait("scalar", sm, mmv)
                        p.op("scalar", "activation", vstg[:, :], pp[:, :],
                             AF.Identity, bias=bp_s[:, oc:oc + 1], sem=ss)
                        prv = p.cnt[ss]
                        p.wait("vector", ss, prv)
                        last_proj_dve = p.op("vector", "tensor_tensor", xsl,
                                             xsl, vstg[:, :], op=ALU.add)
                        p.wait("scalar", sv, last_proj_dve)
                    else:
                        p.wait("vector", sm, mmv)
                        last_proj_dve = p.op("vector", "tensor_tensor", xsl,
                                             xsl, pp[:, :], op=ALU.add)
                    p.wait("tensor", sv, last_proj_dve)
                psp_free = p.cnt[sv]
                if b == B - 1 and hg + 1 < 4:
                    p.wait("gpsimd", sm, p.cnt[sm])
                    wp_rdy[hg + 1] = wp_load(hg + 1)
    return (sv, p.cnt[sv])


def bias_cols(b):
    return np.ascontiguousarray(np.asarray(b, np.float32).reshape(8, 128).T)


def attn_host_inputs(L, core, inputs):
    tag = "a%d" % L
    d = {}
    if L == 0:
        d[tag + "Wq"] = np.ascontiguousarray(inputs["L0_attn_Wq"])
        d[tag + "Wk"] = np.ascontiguousarray(inputs["L0_attn_Wk"])
        d[tag + "Wv"] = np.ascontiguousarray(inputs["L0_attn_Wv"])
        d[tag + "Wp"] = np.ascontiguousarray(inputs["L0_attn_Wp"])
        d[tag + "bq"] = bias_cols(inputs["L0_attn_bq"])
        d[tag + "bk"] = bias_cols(inputs["L0_attn_bk"])
        d[tag + "bp"] = bias_cols(inputs["L0_attn_bp"])
        d[tag + "bvr"] = np.asarray(inputs["L0_attn_bv"], np.float32)[None, :]
        pk = ((8 * np.arange(4)[:, None] + core) * 128 +
              np.arange(128)[None, :]).reshape(-1).astype(np.float32)
        pos = np.tile(pk, 4)
        dd = np.arange(128) % 64
        inv = (10000.0 ** (-2.0 * (dd // 2) / 64.0)).astype(np.float32)
        ang = pos[None, :] * inv[:, None]
        sign = np.where(dd % 2 == 0, -1.0, 1.0).astype(np.float32)
        d["cos_t"] = np.cos(ang).astype(BF)
        d["sin_t"] = (sign[:, None] * np.sin(ang)).astype(BF)
        pm = np.zeros((128, 128), np.float32)
        pm[np.arange(128), np.arange(128) ^ 1] = 1.0
        d["perm"] = pm.astype(BF)
    else:
        w = np.asarray(inputs["L1_attn_c_attn_w"], np.float32)
        bb = np.asarray(inputs["L1_attn_c_attn_b"], np.float32)
        d[tag + "Wq"] = np.ascontiguousarray(w[:, 0:1024])
        d[tag + "Wk"] = np.ascontiguousarray(w[:, 1024:2048])
        d[tag + "Wv"] = np.ascontiguousarray(w[:, 2048:3072])
        d[tag + "Wp"] = np.ascontiguousarray(inputs["L1_attn_c_proj_w"])
        d[tag + "bq"] = bias_cols(bb[0:1024])
        d[tag + "bk"] = bias_cols(bb[1024:2048])
        d[tag + "bp"] = bias_cols(inputs["L1_attn_c_proj_b"])
        d[tag + "bvr"] = bb[2048:3072][None, :].copy()
        d["wkcT"] = np.ascontiguousarray(inputs["L1_attn_wk_c"].T)
        d["wke"] = np.ascontiguousarray(inputs["L1_attn_wk_e"])
        d["wvcT"] = np.ascontiguousarray(inputs["L1_attn_wv_c"].T)
        d["wve"] = np.ascontiguousarray(inputs["L1_attn_wv_e"])
    ki = np.arange(128)[:, None]
    qi = np.arange(128)[None, :]
    mk = np.ones((128, 8 * 128), np.float32)
    for cc in range(8):
        if cc == core:
            mk[:, cc * 128:(cc + 1) * 128] = np.where(ki <= qi, 1.0, 0.0)
        elif cc > core:
            mk[:, cc * 128:(cc + 1) * 128] = 0.0
    d["maskt"] = mk.astype(BF)
    return d


def build_attn_launch(L):
    p = Prog()
    emit_prelude(p)
    xd = emit_load_x(p)
    pd = emit_attn_prep(p, L)
    ud = emit_ln(p, "ln", [xd, pd])
    done = emit_attn(p, "a%d" % L, L, ud, pd)
    emit_store_x(p, [done])
    return p.finalize()
